# version 07
# baseline (speedup 1.0000x reference)
import numpy as np
import ml_dtypes
from contextlib import ExitStack

import concourse.bass as bass
import concourse.mybir as mybir
from concourse.alu_op_type import AluOpType
from concourse.bass_utils import run_bass_kernel_spmd

F32 = mybir.dt.float32
BF16 = mybir.dt.bfloat16
AF = mybir.ActivationFunctionType
OP = AluOpType
bf16 = ml_dtypes.bfloat16

S, D, H, DH = 2048, 1024, 16, 64
QG = 2
PVW = 128 * QG
NG = 16 // QG
NSEQ = 4
NT = S // 128
ND = D // 128
FH = 512
NE = 8
EPS = 1e-5
ENGINES = ("tensor", "vector", "scalar", "gpsimd", "sync")


class Tok:
    __slots__ = ("eng", "op")
    def __init__(self, eng, op):
        self.eng, self.op = eng, op


class Buf:
    __slots__ = ("name", "writer", "readers")
    def __init__(self, name):
        self.name = name
        self.writer = None
        self.readers = []


class Op:
    __slots__ = ("eng", "fn", "waits", "k", "consumed", "val", "slot")
    def __init__(self, eng, fn, waits, k):
        self.eng, self.fn, self.waits, self.k = eng, fn, waits, k
        self.consumed = False
        self.val = None
        self.slot = None


class Bld:
    def __init__(self, nc):
        self.nc = nc
        self.ops = {e: [] for e in ENGINES}

    def op(self, eng, fn, reads=(), writes=(), after=(), dma=False):
        waits = []
        for b_ in reads:
            if b_.writer is not None:
                waits.append(b_.writer)
        for b_ in writes:
            if b_.writer is not None:
                waits.append(b_.writer)
            waits.extend(b_.readers)
        waits.extend(t for t in after if t is not None)
        o = Op(eng, fn, waits, 16 if dma else 1)
        self.ops[eng].append(o)
        t = Tok(eng, o)
        for b_ in reads:
            b_.readers.append(t)
        for b_ in writes:
            b_.writer = t
            b_.readers = []
        return t

    NDMA = 8

    def emit(self, final_toks):
        self.op("gpsimd", lambda e: e.nop(), after=list(final_toks))
        for e in ENGINES:
            for o in self.ops[e]:
                for t in o.waits:
                    t.op.consumed = True
        for t in final_toks:
            t.op.consumed = True
        R = self.NDMA
        for e in ENGINES:
            run = 0
            dma_i = 0
            for o in self.ops[e]:
                if o.k == 16:
                    o.consumed = True
                    o.slot = dma_i % R
                    o.val = 16 * (dma_i // R + 1)
                    dma_i += 1
                elif o.consumed:
                    run += 1
                    o.val = run
                    o.slot = None
        with ExitStack() as es:
            sems = {e: es.enter_context(self.nc.semaphore("sem_" + e)) for e in ENGINES}
            dsems = {e: [es.enter_context(self.nc.semaphore(f"dma_{e}_{i}")) for i in range(R)]
                     for e in ("sync", "gpsimd", "scalar", "vector", "tensor")}
            blk = es.enter_context(self.nc.Block())

            def mkbody(engname):
                def _(eng):
                    maxw = {}
                    for o in self.ops[engname]:
                        best = {}
                        for t in o.waits:
                            v = t.op.val
                            assert v is not None, "wait on unconsumed op"
                            key = (t.eng, t.op.slot)
                            if v > best.get(key, 0):
                                best[key] = v
                        if o.k == 16 and o.val > 16:
                            key = (engname, o.slot)
                            if best.get(key, 0) < o.val - 16:
                                best[key] = o.val - 16
                        for key in sorted(best, key=lambda k: (k[0], -1 if k[1] is None else k[1])):
                            v = best[key]
                            if v > maxw.get(key, 0):
                                semh = sems[key[0]] if key[1] is None else dsems[key[0]][key[1]]
                                eng.wait_ge(semh, v)
                                maxw[key] = v
                        instr = o.fn(eng)
                        if o.consumed:
                            if instr is None:
                                instr = eng.nop()
                            if o.k == 16:
                                instr.then_inc(dsems[engname][o.slot], 16)
                            else:
                                instr.then_inc(sems[engname], 1)
                return _

            for e in ENGINES:
                getattr(blk, e)(mkbody(e))


def build_program(n_moe, fix_runs, alpha_runs, upto="ALL", debug=()):
    nc = bass.Bass("TRN2", target_bir_lowering=False)
    NM = max(n_moe // 128, 1)
    AP = bass.AP

    dr = {}
    def din(name, shape, dt=F32):
        dr[name] = nc.dram_tensor(name, shape, dt, kind="ExternalInput")
    din("x", [NSEQ, S, D]); din("xmoe", [NM * 128, D])
    din("ident_bf", [128, 128], BF16); din("ident_f32", [128, 128], F32)
    din("bdiag", [128, 2], BF16)
    din("ln_w", [D]); din("ln_b", [D])
    din("Wq", [D, D]); din("Wkv", [D, 2 * D]); din("Wo", [D, D])
    din("q_gamma", [H * DH]); din("k_gamma", [H * DH])
    din("sw1T", [D, FH]); din("sw2T", [D, FH]); din("sw3T", [FH, D])
    din("sb1", [FH]); din("sb2", [FH]); din("sb3", [D])
    din("s_ln_g", [FH]); din("s_ln_b", [FH])
    din("rw", [D, 9]); din("rb", [9]); din("exp_bias", [NE])
    din("ew1T", [NE, D, FH]); din("ew2T", [NE, D, FH]); din("ew3T", [NE, FH, D])
    din("eb1", [NE, FH]); din("eb2", [NE, FH]); din("eb3", [NE, D])
    din("sel_ln_g", [FH]); din("sel_ln_b", [FH])
    din("mvalid", [NM * 128])

    out_t = nc.dram_tensor("out", [NSEQ, S, D], F32, kind="ExternalOutput")
    v_scr = nc.dram_tensor("v_scr", [NT, 128, D], BF16)
    s_scr = nc.dram_tensor("s_scr", [NT, 128, D], BF16)
    fix_scr = nc.dram_tensor("fix_scr", [NM * 128, D], BF16)
    alpha_scr = nc.dram_tensor("alpha_scr", [NM, 128], F32)
    bounce = nc.dram_tensor("bounce", [2600, S], BF16)
    bouncef = nc.dram_tensor("bouncef", [1700, 512], BF16)
    bouncez = nc.dram_tensor("bouncez", [300, 512], F32)
    zfctr = [0]
    bctr = [0]
    zctr = [0]

    dbg = {}
    out_toks = []

    es = ExitStack()
    with es:
        def sbuf(name, shape, dt):
            t = es.enter_context(nc.sbuf_tensor("sb_" + name, shape, dt))
            return t, Buf(name)

        b = Bld(nc)

        ident, B_id = sbuf("ident", [128, 128], BF16)
        identf, B_idf = sbuf("identf", [128, 128], F32)
        bdiag, B_bd = sbuf("bdiag_sb", [128, 2], BF16)
        ones2k, B_ones = sbuf("ones65", [65, 512], BF16)
        nmrow, B_nm = sbuf("nmrow", [65, 2048], BF16)
        gq8, B_gq = sbuf("gq8", [128, ND], F32)
        gk8, B_gk = sbuf("gk8", [128, ND], F32)
        lnw8, B_lnw = sbuf("lnw8", [128, ND], F32)
        lnwi, B_lnwi = sbuf("lnwi", [128, ND], F32)
        lnb8, B_lnb = sbuf("lnb8", [128, ND], BF16)
        sw1T, B_sw1 = sbuf("sw1T", [128, ND, FH], BF16)
        sw2T, B_sw2 = sbuf("sw2T", [128, ND, FH], BF16)
        sw3gT, B_sw3 = sbuf("sw3gT", [128, 4, D], BF16)
        rw_sb, B_rw = sbuf("rw_sb", [128, ND, 9], BF16)
        rows_a, _Ba = sbuf("rows_a", [65, D], BF16)
        rows_b, _Bb = sbuf("rows_b", [65, D], BF16)
        rows_c, _Bc = sbuf("rows_c", [65, FH], BF16)
        qbrow = rows_a[0:1, :]; B_qb = Buf("qbrow")
        kbrow = rows_a[32:33, :]; B_kb = Buf("kbrow")
        vbrow = rows_a[64:65, :]; B_vb = Buf("vbrow")
        cbrow = rows_b[0:1, :]; B_cb = Buf("cbrow")
        sb3row = rows_b[32:33, :]; B_sb3r = Buf("sb3row")
        eb3row = rows_b[64:65, :]; B_eb3 = Buf("eb3row")
        eb1row = rows_c[0:1, :]; B_eb1 = Buf("eb1row")
        eb2row = rows_c[32:33, :]; B_eb2 = Buf("eb2row")
        rbrow = rows_c[64:65, 0:9]; B_rb = Buf("rbrow")
        sb1bc, B_sb1 = sbuf("sb1bc", [128, FH], BF16)
        sb2bc, B_sb2 = sbuf("sb2bc", [128, FH], BF16)
        selgbc, B_selg = sbuf("selgbc", [128, FH], BF16)
        selbbc, B_selb = sbuf("selbbc", [128, FH], BF16)
        expbbc, B_expb = sbuf("expbbc", [128, NE], F32)
        slg4, B_slg = sbuf("slg4", [128, 4], F32)
        slb4, B_slb = sbuf("slb4", [128, 4], BF16)
        pconst, B_pc = sbuf("pconst", [128, 4], F32)
        stats, B_stats = sbuf("stats", [128, NT, 4], F32)
        warena, B_wa = sbuf("warena", [128, ND, D], BF16)
        junk_v, B_jv = sbuf("junk_v", [128, 2048], BF16)
        junk_a, B_ja = junk_v, B_jv
        tstat, B_ts = sbuf("tstat", [128, 8], F32)

        ps_a = es.enter_context(nc.psum_tensor("ps_a", [128, 2048], F32))
        ps_b = es.enter_context(nc.psum_tensor("ps_b", [128, 2048], F32))
        ps_a_bf = ps_a.bitcast(BF16)
        ps_b_bf = ps_b.bitcast(BF16)
        PA = [Buf(f"pa{i}") for i in range(4)]
        PB = [Buf(f"pb{i}") for i in range(4)]
        PA_ALL = PA
        PB_ALL = PB

        xhT, B_xhT = sbuf("xhT", [128, ND, S], BF16)
        ktl, B_kt = sbuf("ktl", [128, ND, S], BF16)
        aoT, B_ao = sbuf("aoT", [128, ND, S], BF16)
        x_t, B_x = zip(*[sbuf(f"x_t{i}", [128, D], F32) for i in range(1)])
        x_t = (x_t[0], x_t[0]); B_x = (B_x[0], B_x[0])
        xc_t, B_xc = zip(*[sbuf(f"xc_t{i}", [128, D], BF16) for i in range(2)])
        v_t, B_v = zip(*[sbuf(f"v_t{i}", [128, D], BF16) for i in range(2)])
        ss_sb, B_ss = junk_v[0:2, 0:2048], B_jv
        rr_sb, B_rr = junk_v[0:2, 0:2048], B_jv
        qt_sb, B_qt = sbuf("qt_sb", [128, 2048], BF16)
        p_A, B_pA = sbuf("p_A", [128, 2048], BF16)
        p_B, B_pB = sbuf("p_B", [128, 2048], BF16)
        ptA, B_ptA = sbuf("ptA", [128, 16, PVW], BF16)
        rrbc = bass.AP(ptA, 0, [[4096, 128], [1, 2048]])
        B_rrbc = B_ptA
        ptAs = [bass.AP(ptA, s * 1024, [[4096, 128], [1, 1024]]) for s in range(2)]
        ptBs = [bass.AP(ptA, 2048 + s * 1024, [[4096, 128], [1, 1024]]) for s in range(2)]
        B_pt2 = [Buf(f"pt2_{s}") for s in range(4)]
        B_nmq = [Buf(f"nmq{q}") for q in range(4)]
        ztl, B_ztl = sbuf("ztl", [128, 8], F32)
        zt_sb, B_zt = sbuf("zt_sb", [4, 256], BF16)
        zbc, B_zbc = sbuf("zbc", [64, 1024], BF16)
        bstage, B_bst = sbuf("bstage", [64, 512], BF16)
        zrow, B_zr = sbuf("zrow", [65, 512], F32)
        negmQA, B_nqa = sbuf("negmQA", [128, 16], F32)
        negmQB, B_nqb = sbuf("negmQB", [128, 16], F32)
        vsl, B_vsl = zip(*[sbuf(f"vsl{i}", [128, 16, 130], BF16) for i in range(1)])
        stile, B_st = v_t, B_v
        x1f, B_x1 = qt_sb[:, 0:512], B_qt
        x2f, B_x2 = qt_sb[:, 512:1024], B_qt
        x1s, B_x1s = qt_sb[:, 1024:1536], B_qt
        g0_sb, B_g0 = qt_sb[:, 1536:2048], B_qt
        gh_sb, B_gh = p_B[:, 0:512], B_pB
        shTt, B_sh = p_B[:, 512:1024], B_pB
        ffn_f, B_ffn = p_A[:, 0:1024], B_pA
        g_t, B_g = p_A[:, 1024:2048], B_pA
        junk_f32 = junk_v.bitcast(F32)
        o_t, B_o = junk_f32[:, 0:1024], B_jv
        junk_f32_holder = (junk_f32, B_jv)
        fix_t, B_fx = zip(*[sbuf(f"fix_t{i}", [128, D], BF16) for i in range(1)])
        alpha_sb, B_al = sbuf("alpha_sb", [128, NT], F32)
        wexpall, B_we = sbuf("wexpall", [128, 16, NE], F32)
        valid_sb, B_valid = sbuf("valid_sb", [128, 16], F32)

        B_vscr = Buf("v_scr"); B_sscr = Buf("s_scr"); B_fixs = Buf("fix_scr")
        B_als = Buf("alpha_scr")

        def full(t, n):
            return t[tuple(slice(None) for _ in range(n))]

        def dump(name, src_ap, shape, dt, reads):
            if name not in debug:
                return
            t = nc.dram_tensor(name, shape, dt, kind="ExternalOutput")
            dbg[name] = t
            out_toks.append(b.op("sync", lambda e: e.dma_start(full(t, len(shape)), src_ap),
                                 reads=reads, dma=True))

        def dump_ps(name, ps_ap, width, reads):
            if name not in debug:
                return
            t = nc.dram_tensor(name, [128, width], F32, kind="ExternalOutput")
            dbg[name] = t
            tmp = junk_f32_holder[0][:, 0:width]
            Bt = junk_f32_holder[1]
            b.op("vector", lambda e: e.tensor_copy(tmp, ps_ap), reads=reads, writes=[Bt])
            out_toks.append(b.op("sync", lambda e: e.dma_start(t[:, :], tmp),
                                 reads=[Bt], dma=True))

        dbgst = p_B.bitcast(F32)

        def dump_ps2(name, ps_ap, width, reads):
            if name not in debug:
                return
            t = nc.dram_tensor(name, [128, width], F32, kind="ExternalOutput")
            dbg[name] = t
            b.op("scalar", lambda e: e.activation(dbgst[:, 0:width], ps_ap, AF.Identity),
                 reads=reads, writes=[B_pB])
            out_toks.append(b.op("sync", lambda e: e.dma_start(t[:, :], dbgst[:, 0:width]),
                                 reads=[B_pB], dma=True))

        def bounce_bcast(src_row_ap, dst_ap, width, reads, writes):
            slot = bctr[0]; bctr[0] += 1
            Bs = Buf(f"b{slot}")
            b.op("sync", lambda e: e.dma_start(bounce[slot, 0:width], src_row_ap),
                 reads=reads, writes=[Bs], dma=True)
            np_ = dst_ap.shape[0]
            b.op("sync", lambda e: e.dma_start(dst_ap, AP(bounce, slot * S, [[0, np_], [1, width]])),
                 reads=[Bs], writes=writes, dma=True)

        def ld(dst_ap, src_ap, writes, reads=(), slow=False):
            return b.op("gpsimd", lambda e: e.dma_start(dst_ap, src_ap,
                        allow_slow_non_contiguous=slow), reads=reads,
                        writes=writes, dma=True)

        ld(ident[:, :], dr["ident_bf"][:, :], [B_id])
        ld(identf[:, :], dr["ident_f32"][:, :], [B_idf])
        ld(bdiag[:, :], dr["bdiag"][:, :], [B_bd])
        ld(gq8[:, :], AP(dr["q_gamma"], 0, [[1, 128], [128, ND]]), [B_gq], slow=True)
        ld(gk8[:, :], AP(dr["k_gamma"], 0, [[1, 128], [128, ND]]), [B_gk], slow=True)
        ld(lnw8[:, :], AP(dr["ln_w"], 0, [[1, 128], [128, ND]]), [B_lnw], slow=True)
        ld(lnb8[:, :], AP(dr["ln_b"], 0, [[1, 128], [128, ND]]), [B_lnb], slow=True)
        ld(sb1bc[:, :], AP(dr["sb1"], 0, [[0, 128], [1, FH]]), [B_sb1])
        ld(sb2bc[:, :], AP(dr["sb2"], 0, [[0, 128], [1, FH]]), [B_sb2])
        ld(selgbc[:, :], AP(dr["sel_ln_g"], 0, [[0, 128], [1, FH]]), [B_selg])
        ld(selbbc[:, :], AP(dr["sel_ln_b"], 0, [[0, 128], [1, FH]]), [B_selb])
        ld(expbbc[:, :], AP(dr["exp_bias"], 0, [[0, 128], [1, NE]]), [B_expb])
        ld(slg4[:, :], AP(dr["s_ln_g"], 0, [[1, 128], [128, 4]]), [B_slg], slow=True)
        ld(slb4[:, :], AP(dr["s_ln_b"], 0, [[1, 128], [128, 4]]), [B_slb], slow=True)
        ld(sb3row, AP(dr["sb3"], 0, [[D, 1], [1, D]]), [B_sb3r])
        ld(rbrow, AP(dr["rb"], 0, [[9, 1], [1, 9]]), [B_rb])
        ld(valid_sb[:, 0:NM], AP(dr["mvalid"], 0, [[1, 128], [128, NM]]), [B_valid], slow=True)
        for dc in range(ND):
            ld(sw1T[:, dc, :], dr["sw1T"][dc * 128:(dc + 1) * 128, :], [B_sw1])
            ld(sw2T[:, dc, :], dr["sw2T"][dc * 128:(dc + 1) * 128, :], [B_sw2])
            ld(rw_sb[:, dc, :], dr["rw"][dc * 128:(dc + 1) * 128, :], [B_rw])
        for hc in range(4):
            ld(sw3gT[:, hc, :], dr["sw3T"][hc * 128:(hc + 1) * 128, :], [B_sw3])
        b.op("gpsimd", lambda e: e.memset(ones2k[:, :], 1.0), writes=[B_ones])
        def msc(e):
            e.memset(pconst[:, 0:1], -0.5)
            e.memset(pconst[:, 1:2], -1.0)
            e.memset(pconst[:, 2:3], 0.5)
            return e.memset(pconst[:, 3:4], 1e-8)
        b.op("gpsimd", msc, writes=[B_pc])
        b.op("gpsimd", lambda e: e.tensor_scalar(gq8[:, :], gq8[:, :], 8.0, None, OP.mult),
             reads=[B_gq], writes=[B_gq])
        b.op("gpsimd", lambda e: e.tensor_scalar(gk8[:, :], gk8[:, :], 8.0, None, OP.mult),
             reads=[B_gk], writes=[B_gk])
        def recip_all(e):
            for dc in range(ND):
                ii = e.reciprocal(lnwi[:, dc:dc + 1], lnw8[:, dc:dc + 1])
            return ii
        b.op("vector", recip_all, reads=[B_lnw], writes=[B_lnwi])
        def fold12(e):
            for dc in range(ND):
                e.tensor_scalar(sw1T[:, dc, :], sw1T[:, dc, :], lnwi[:, dc:dc + 1], None, OP.mult)
                ii = e.tensor_scalar(sw2T[:, dc, :], sw2T[:, dc, :], lnwi[:, dc:dc + 1], None, OP.mult)
            return ii
        b.op("vector", fold12, reads=[B_sw1, B_sw2, B_lnwi], writes=[B_sw1, B_sw2])
        def fold3(e):
            for hc in range(4):
                ii = e.tensor_scalar(sw3gT[:, hc, :], sw3gT[:, hc, :], slg4[:, hc:hc + 1], None, OP.mult)
            return ii
        b.op("vector", fold3, reads=[B_sw3, B_slg], writes=[B_sw3])

        def stream_w(src, coloff, rowlen):
            for dc in range(ND):
                b.op("gpsimd",
                     lambda e, dc=dc: e.dma_start(warena[:, dc, :],
                         AP(src, dc * 128 * rowlen + coloff, [[rowlen, 128], [1, D]])),
                     writes=[B_wa], dma=True)

        for srcname, coloff, dstrow, dstbuf in [("Wq", 0, qbrow, B_qb),
                                                ("Wkv", 0, kbrow, B_kb),
                                                ("Wkv", D, vbrow, B_vb)]:
            rowlen = 2 * D if srcname == "Wkv" else D
            stream_w(dr[srcname], coloff, rowlen)
            def bmm(e):
                for half in range(2):
                    for dc in range(ND):
                        ii = e.matmul(ps_a[0:1, half * 512:(half + 1) * 512], lnb8[:, dc:dc + 1],
                                      warena[:, dc, half * 512:(half + 1) * 512],
                                      start=(dc == 0), stop=(dc == ND - 1))
                return ii
            b.op("tensor", bmm, reads=[B_lnb, B_wa], writes=[PA[0], PA[1]])
            b.op("vector", lambda e, dstrow=dstrow: e.tensor_copy(dstrow[0:1, :], ps_a[0:1, 0:1024]),
                 reads=[PA[0], PA[1]], writes=[dstbuf])
        def cbmm(e):
            for half in range(2):
                for hc in range(4):
                    e.matmul(ps_a[0:1, half * 512:(half + 1) * 512], slb4[:, hc:hc + 1],
                             sw3gT[:, hc, half * 512:(half + 1) * 512], start=(hc == 0), stop=False)
                ii = e.matmul(ps_a[0:1, half * 512:(half + 1) * 512], ones2k[32:33, 0:1],
                              rows_b[32:33, half * 512:(half + 1) * 512], start=False, stop=True)
            return ii
        b.op("tensor", cbmm, reads=[B_slb, B_sw3, B_ones, B_sb3r], writes=[PA[0], PA[1]])
        b.op("vector", lambda e: e.tensor_copy(cbrow, ps_a[0:1, 0:1024]),
             reads=[PA[0], PA[1]], writes=[B_cb])

        if "d_rows" in debug:
            tr = nc.dram_tensor("d_rows", [4, D], BF16, kind="ExternalOutput")
            dbg["d_rows"] = tr
            for ri, (rap, rbuf) in enumerate([(qbrow, B_qb), (kbrow, B_kb),
                                              (vbrow, B_vb), (cbrow, B_cb)]):
                out_toks.append(b.op("sync", lambda e, ri=ri, rap=rap: e.dma_start(
                     tr[ri:ri + 1, :], rap), reads=[rbuf], dma=True))

        def tile_stats(x_ap, Bx, tt, mode):
            b.op("vector", lambda e: e.tensor_scalar(junk_v[:, 0:1024], x_ap, 0.0, None,
                 OP.add, op1=OP.add, accum_out=tstat[:, 0:1]), reads=[Bx], writes=[B_jv, B_ts])
            b.op("scalar", lambda e: e.activation(junk_a[:, 0:1024], x_ap, AF.Square,
                 accum_out=tstat[:, 1:2]), reads=[Bx], writes=[B_ja, B_ts])
            def post(e):
                n = 1024.0
                e.tensor_scalar(tstat[:, 2:3], tstat[:, 0:1], 1.0 / n, None, OP.mult)
                e.tensor_scalar(tstat[:, 3:4], tstat[:, 1:2], 1.0 / n, None, OP.mult)
                e.tensor_tensor(tstat[:, 4:5], tstat[:, 2:3], tstat[:, 2:3], OP.mult)
                e.tensor_tensor(tstat[:, 4:5], tstat[:, 3:4], tstat[:, 4:5], OP.subtract)
                e.tensor_scalar(tstat[:, 5:6], tstat[:, 4:5], EPS, None, OP.add)
                e.tensor_tensor(tstat[:, 6:7], tstat[:, 5:6], pconst[:, 0:1], OP.pow)
                if mode == "moe":
                    e.tensor_copy(stats[:, tt, 0:1], tstat[:, 6:7])
                    e.tensor_tensor(tstat[:, 7:8], tstat[:, 2:3], tstat[:, 6:7], OP.mult)
                    return e.tensor_scalar(stats[:, tt, 1:2], tstat[:, 7:8], -1.0, None, OP.mult)
                e.tensor_tensor(tstat[:, 7:8], tstat[:, 5:6], pconst[:, 1:2], OP.pow)
                e.tensor_tensor(tstat[:, 7:8], tstat[:, 4:5], tstat[:, 7:8], OP.mult)
                e.tensor_scalar(tstat[:, 7:8], tstat[:, 7:8], EPS, None, OP.add)
                e.tensor_tensor(stats[:, tt, 2:3], tstat[:, 7:8], pconst[:, 2:3], OP.pow)
                e.tensor_tensor(tstat[:, 7:8], tstat[:, 7:8], pconst[:, 0:1], OP.pow)
                e.tensor_tensor(tstat[:, 6:7], tstat[:, 6:7], tstat[:, 7:8], OP.mult)
                e.tensor_copy(stats[:, tt, 0:1], tstat[:, 6:7])
                e.tensor_tensor(tstat[:, 7:8], tstat[:, 2:3], tstat[:, 6:7], OP.mult)
                return e.tensor_scalar(stats[:, tt, 1:2], tstat[:, 7:8], -1.0, None, OP.mult)
            b.op("gpsimd", post, reads=[B_ts, B_pc], writes=[B_stats, B_ts])

        def rms_apply(dst_ap, dst_buf, g_ap, Bg, dbgpre=None):
            if dbgpre:
                dump_ps(dbgpre + "_raw", ps_a[:, 0:1024], 1024, PA_ALL)
            b.op("scalar", lambda e: e.activation(junk_a[:, 0:2048], ps_a[:, :], AF.Square),
                 reads=PA_ALL, writes=[B_ja])
            def ssmm(e):
                for tsl in range(4):
                    ii = e.matmul(ps_b[0:2, tsl * 512:(tsl + 1) * 512], bdiag[:, :],
                                  junk_a[:, tsl * 512:(tsl + 1) * 512], start=True, stop=True)
                return ii
            b.op("tensor", ssmm, reads=[B_bd, B_ja], writes=PB_ALL)
            b.op("vector", lambda e: e.tensor_copy(ss_sb, ps_b[0:2, 0:2048]),
                 reads=PB_ALL, writes=[B_ss])
            if dbgpre:
                dump(dbgpre + "_ss", ss_sb, [2, 2048], BF16, [B_ss])
            b.op("scalar", lambda e: e.activation(rr_sb, ss_sb, AF.Ln,
                 bias=pconst[0:2, 3:4], scale=1.0 / 64), reads=[B_ss, B_pc], writes=[B_rr])
            b.op("scalar", lambda e: e.activation(rr_sb, rr_sb, AF.Exp, scale=-0.5),
                 reads=[B_rr], writes=[B_rr])
            if dbgpre:
                dump(dbgpre + "_rr", junk_v[0:2, 0:2048], [2, 2048], BF16, [B_rr])
            bounce_bcast(junk_v[0:1, 0:2048], rrbc[0:64, :], S, [B_rr], [B_rrbc])
            bounce_bcast(junk_v[1:2, 0:2048], rrbc[64:128, :], S, [B_rr], [B_rrbc])
            if dbgpre:
                dump(dbgpre + "_rrbc", rrbc[:, :], [128, 2048], BF16, [B_rrbc])
            b.op("vector", lambda e: e.scalar_tensor_tensor(dst_ap, ps_a[:, :], g_ap, rrbc[:, :],
                 OP.mult, OP.mult), reads=PA_ALL + [Bg, B_rrbc], writes=[dst_buf])

        def proj_chunk(ic, brow_ap, Bbrow, obase):
            def mm(e, ic=ic):
                for dc in range(ND):
                    for tsl in range(4):
                        e.matmul(ps_a[:, tsl * 512:(tsl + 1) * 512],
                                 warena[:, dc, ic * 128:(ic + 1) * 128],
                                 xhT[:, dc, tsl * 512:(tsl + 1) * 512],
                                 start=(dc == 0), stop=False)
                for tsl in range(4):
                    ii = e.matmul(ps_a[:, tsl * 512:(tsl + 1) * 512], brow_ap,
                                  ones2k[obase:obase + 1, 0:512], start=False, stop=True)
                return ii
            b.op("tensor", mm, reads=[B_wa, B_xhT, Bbrow, B_ones], writes=PA_ALL)

        def phase_A(seq):
            stream_w(dr["Wkv"], D, 2 * D)
            for tt in range(NT):
                pr = tt % 2
                b.op("sync", lambda e, tt=tt, pr=pr: e.dma_start(x_t[pr][:, :],
                     dr["x"][seq, tt * 128:(tt + 1) * 128, :]), writes=[B_x[pr]], dma=True)
                tile_stats(x_t[pr][:, :], B_x[pr], tt, "seq")
                b.op("scalar", lambda e, tt=tt, pr=pr: e.activation(xc_t[pr][:, :], x_t[pr][:, :],
                     AF.Identity, bias=stats[:, tt, 1:2], scale=stats[:, tt, 0:1]),
                     reads=[B_x[pr], B_stats], writes=[B_xc[pr]])
                def tr(e, pr=pr):
                    for dc in range(ND):
                        ii = e.transpose(ps_b_bf[:, 2048 + dc * 128:2048 + (dc + 1) * 128],
                                         xc_t[pr][:, dc * 128:(dc + 1) * 128], ident[:, :])
                    return ii
                b.op("tensor", tr, reads=[B_xc[pr], B_id], writes=[PB[2]])
                def evac_xhT(e, tt=tt):
                    for dc in range(ND):
                        ii = e.tensor_scalar(xhT[:, dc, tt * 128:(tt + 1) * 128],
                                             ps_b_bf[:, 2048 + dc * 128:2048 + (dc + 1) * 128],
                                             lnw8[:, dc:dc + 1], None, OP.mult)
                    return ii
                b.op("vector", evac_xhT, reads=[PB[2], B_lnw], writes=[B_xhT])
                def vmm(e, tt=tt):
                    for dc in range(ND):
                        e.matmul(ps_b[:, 0:512], xhT[:, dc, tt * 128:(tt + 1) * 128],
                                 warena[:, dc, 0:512], start=(dc == 0), stop=False)
                        e.matmul(ps_b[:, 512:1024], xhT[:, dc, tt * 128:(tt + 1) * 128],
                                 warena[:, dc, 512:1024], start=(dc == 0), stop=False)
                    e.matmul(ps_b[:, 0:512], ones2k[64:65, 0:128], rows_a[64:65, 0:512],
                             start=False, stop=True)
                    return e.matmul(ps_b[:, 512:1024], ones2k[64:65, 0:128], rows_a[64:65, 512:1024],
                                    start=False, stop=True)
                b.op("tensor", vmm, reads=[B_xhT, B_wa, B_ones, B_vb], writes=[PB[0], PB[1]])
                b.op("vector", lambda e, pr=pr: e.tensor_copy(v_t[pr][:, :], ps_b[:, 0:1024]),
                     reads=[PB[0], PB[1]], writes=[B_v[pr]])
                b.op("sync", lambda e, tt=tt, pr=pr: e.dma_start(v_scr[tt, :, :], v_t[pr][:, :]),
                     reads=[B_v[pr]], writes=[B_vscr], dma=True)
            if seq == 0:
                dump("d_stats", stats[:, :, :], [128, NT, 4], F32, [B_stats])
                dump("d_xhT", xhT[:, :, :], [128, ND, S], BF16, [B_xhT])
                dump("d_v", v_scr[0, :, :], [128, D], BF16, [B_vscr])
            stream_w(dr["Wkv"], 0, 2 * D)
            for ic in range(ND):
                proj_chunk(ic, rows_a[32:33, ic * 128:(ic + 1) * 128], B_kb, 32)
                rms_apply(ktl[:, ic, :], B_kt, gk8[:, ic:ic + 1], B_gk,
                          dbgpre=("d_k0" if (seq == 0 and ic == 0) else None))
            if seq == 0:
                dump("d_kt", ktl[:, :, :], [128, ND, S], BF16, [B_kt])

        def phase_B(seq):
            stream_w(dr["Wq"], 0, D)
            for hp in range(ND):
                hA, hB = 2 * hp, 2 * hp + 1
                b.op("sync", lambda e, hp=hp: e.dma_start(
                     vsl[0][:, :, 0:64],
                     AP(v_scr, hp * 128, [[D, 128], [128 * D, 16], [1, 64]])),
                     reads=[B_vscr], writes=[B_vsl[0]], dma=True)
                b.op("sync", lambda e, hp=hp: e.dma_start(
                     vsl[0][:, :, 65:129],
                     AP(v_scr, hp * 128 + 64, [[D, 128], [128 * D, 16], [1, 64]])),
                     reads=[B_vscr], writes=[B_vsl[0]], dma=True)
                def vones(e):
                    e.memset(vsl[0][:, :, 64:65], 1.0)
                    return e.memset(vsl[0][:, :, 129:130], 1.0)
                b.op("vector", vones, reads=[B_vsl[0]], writes=[B_vsl[0]])
                proj_chunk(hp, rows_a[0:1, hp * 128:(hp + 1) * 128], B_qb, 0)
                rms_apply(qt_sb[:, :], B_qt, gq8[:, hp:hp + 1], B_gq)
                if seq == 0 and hp == 0:
                    dump("d_qt", qt_sb[:, :], [128, 2048], BF16, [B_qt])
                if seq == 0 and hp == 0:
                    dump("d_ktB", ktl[:, :, :], [128, 8, 2048], BF16, [B_kt])

                def p1q(e, i, c, side, hp):
                    if side == 0:
                        return e.matmul(ps_a[:, 0:512], qt_sb[0:64, i * 128:(i + 1) * 128],
                                        ktl[0:64, hp, c * 512:(c + 1) * 512],
                                        start=True, stop=True, tile_position=(0, 0))
                    return e.matmul(ps_a[:, 512:1024], qt_sb[64:128, i * 128:(i + 1) * 128],
                                    ktl[64:128, hp, c * 512:(c + 1) * 512],
                                    start=True, stop=True, tile_position=(64, 0))

                def emit_p1q(i, c, hp):
                    b.op("tensor", lambda e, i=i, c=c, hp=hp: p1q(e, i, c, 0, hp),
                         reads=[B_qt, B_kt], writes=[PA[0]])
                    b.op("tensor", lambda e, i=i, c=c, hp=hp: p1q(e, i, c, 1, hp),
                         reads=[B_qt, B_kt], writes=[PA[1]])
                    seedA = None if c == 0 else negmQA[:, i:i + 1]
                    seedB = None if c == 0 else negmQB[:, i:i + 1]
                    rA = [PA[0]] + ([B_nqa] if c else [])
                    rB = [PA[1]] + ([B_nqb] if c else [])
                    b.op("vector", lambda e, i=i, seedA=seedA: e.tensor_scalar(
                         junk_v[:, 0:512], ps_a[:, 0:512], -1.0, seedA,
                         OP.mult, op1=OP.min, accum_out=negmQA[:, i:i + 1]),
                         reads=rA, writes=[B_jv, B_nqa])
                    b.op("vector", lambda e, i=i, seedB=seedB: e.tensor_scalar(
                         junk_v[:, 0:512], ps_a[:, 512:1024], -1.0, seedB,
                         OP.mult, op1=OP.min, accum_out=negmQB[:, i:i + 1]),
                         reads=rB, writes=[B_jv, B_nqb])

                def emit_pair(Q, pj, hp):
                    sl = pj % 2
                    def p2A(e, Q=Q, pj=pj, hp=hp):
                        for kk in range(2):
                            kt = 2 * pj + kk
                            e.matmul(ps_a[:, 1024 + kk * 512:1024 + (kk + 1) * 512],
                                     ktl[0:64, hp, kt * 128:(kt + 1) * 128],
                                     qt_sb[0:64, Q * 512:(Q + 1) * 512],
                                     start=True, stop=False, tile_position=(0, 0))
                            ii = e.matmul(ps_a[:, 1024 + kk * 512:1024 + (kk + 1) * 512],
                                          ones2k[0:1, 0:128],
                                          nmrow[0:1, Q * 512:(Q + 1) * 512],
                                          start=False, stop=True, tile_position=(0, 0))
                        return ii
                    b.op("tensor", p2A, reads=[B_kt, B_qt, B_ones, B_nmq[Q]],
                         writes=[PA[2], PA[3]])
                    def p2B(e, Q=Q, pj=pj, hp=hp):
                        for kk in range(2):
                            kt = 2 * pj + kk
                            e.matmul(ps_b[:, 1024 + kk * 512:1024 + (kk + 1) * 512],
                                     ktl[64:128, hp, kt * 128:(kt + 1) * 128],
                                     qt_sb[64:128, Q * 512:(Q + 1) * 512],
                                     start=True, stop=False, tile_position=(64, 0))
                            ii = e.matmul(ps_b[:, 1024 + kk * 512:1024 + (kk + 1) * 512],
                                          ones2k[64:65, 0:128],
                                          nmrow[64:65, Q * 512:(Q + 1) * 512],
                                          start=False, stop=True, tile_position=(64, 0))
                        return ii
                    b.op("tensor", p2B, reads=[B_kt, B_qt, B_ones, B_nmq[Q]],
                         writes=[PB[2], PB[3]])
                    b.op("scalar", lambda e, sl=sl: e.activation(ptAs[sl], ps_a[:, 1024:2048],
                         AF.Exp), reads=[PA[2], PA[3]], writes=[B_pt2[sl], B_ptA])
                    b.op("scalar", lambda e, sl=sl: e.activation(ptBs[sl], ps_b[:, 1024:2048],
                         AF.Exp), reads=[PB[2], PB[3]], writes=[B_pt2[2 + sl], B_ptA])
                    def pvA(e, pj=pj, sl=sl):
                        for kk in range(2):
                            ii = e.matmul(ps_b[0:65, 0:512],
                                          vsl[0][:, 2 * pj + kk, 0:65],
                                          ptAs[sl][:, kk * 512:(kk + 1) * 512],
                                          start=(pj == 0 and kk == 0),
                                          stop=(pj == 7 and kk == 1))
                        return ii
                    b.op("tensor", pvA, reads=[B_vsl[0], B_pt2[sl]], writes=[PB[0]])
                    def pvB(e, pj=pj, sl=sl):
                        for kk in range(2):
                            ii = e.matmul(ps_b[0:65, 512:1024],
                                          vsl[0][:, 2 * pj + kk, 65:130],
                                          ptBs[sl][:, kk * 512:(kk + 1) * 512],
                                          start=(pj == 0 and kk == 0),
                                          stop=(pj == 7 and kk == 1))
                        return ii
                    b.op("tensor", pvB, reads=[B_vsl[0], B_pt2[2 + sl]], writes=[PB[1]])

                for step in range(NT + 8):
                    inwin = 8 <= step < 24
                    s2 = step - 8
                    if step < NT:
                        i = step
                        emit_p1q(i, 0, hp)
                        emit_p1q(i, 1, hp)
                        if inwin:
                            emit_pair(s2 // 4, 2 * (s2 % 4), hp)
                        emit_p1q(i, 2, hp)
                        if inwin:
                            emit_pair(s2 // 4, 2 * (s2 % 4) + 1, hp)
                        emit_p1q(i, 3, hp)
                        if i % 4 == 3:
                            Q = i // 4
                            def nmt(e, Q=Q):
                                e.transpose(ps_b[0:4, 1024:1152], negmQA[:, 4 * Q:4 * Q + 4], identf[:, :])
                                return e.transpose(ps_b[0:4, 1152:1280], negmQB[:, 4 * Q:4 * Q + 4], identf[:, :])
                            b.op("tensor", nmt, reads=[B_nqa, B_nqb, B_idf], writes=[PB[2]])
                            def nme(e):
                                e.tensor_copy(zt_sb[0:4, 0:128], ps_b[0:4, 1024:1152])
                                return e.tensor_copy(zt_sb[0:4, 128:256], ps_b[0:4, 1152:1280])
                            b.op("vector", nme, reads=[PB[2]], writes=[B_zt])
                            slot = zctr[0]; zctr[0] += 2
                            BsA, BsB = Buf(f"nm{slot}"), Buf(f"nm{slot + 1}")
                            b.op("sync", lambda e, slot=slot: e.dma_start(
                                 AP(bouncef, slot * 512, [[128, 4], [1, 128]]), zt_sb[0:4, 0:128]),
                                 reads=[B_zt], writes=[BsA], dma=True)
                            b.op("sync", lambda e, slot=slot: e.dma_start(
                                 AP(bouncef, (slot + 1) * 512, [[128, 4], [1, 128]]), zt_sb[0:4, 128:256]),
                                 reads=[B_zt], writes=[BsB], dma=True)
                            b.op("sync", lambda e, slot=slot, Q=Q: e.dma_start(
                                 nmrow[0:1, Q * 512:(Q + 1) * 512],
                                 bouncef[slot:slot + 1, 0:512]),
                                 reads=[BsA], writes=[B_nmq[Q]], dma=True)
                            b.op("sync", lambda e, slot=slot, Q=Q: e.dma_start(
                                 nmrow[64:65, Q * 512:(Q + 1) * 512],
                                 bouncef[slot + 1:slot + 2, 0:512]),
                                 reads=[BsB], writes=[B_nmq[Q]], dma=True)
                    else:
                        emit_pair(s2 // 4, 2 * (s2 % 4), hp)
                        emit_pair(s2 // 4, 2 * (s2 % 4) + 1, hp)
                    if inwin and s2 % 4 == 3:
                        Q = s2 // 4
                        slot = zfctr[0]; zfctr[0] += 2
                        Bz1, Bz2 = Buf(f"zz{slot}"), Buf(f"zz{slot + 1}")
                        b.op("scalar", lambda e: e.activation(zrow[64:65, 0:512],
                             ps_b[64:65, 0:512], AF.Ln), reads=[PB[0]], writes=[B_zr])
                        b.op("scalar", lambda e: e.activation(zrow[64:65, 0:512],
                             zrow[64:65, 0:512], AF.Exp, scale=-1.0), reads=[B_zr], writes=[B_zr])
                        b.op("sync", lambda e, slot=slot: e.dma_start(
                             AP(bouncez, slot * 512, [[1, 1], [1, 512]]), zrow[64:65, 0:512]),
                             reads=[B_zr], writes=[Bz1], dma=True)
                        b.op("scalar", lambda e: e.activation(zrow[64:65, 0:512],
                             ps_b[64:65, 512:1024], AF.Ln), reads=[PB[1], Bz1], writes=[B_zr])
                        b.op("scalar", lambda e: e.activation(zrow[64:65, 0:512],
                             zrow[64:65, 0:512], AF.Exp, scale=-1.0), reads=[B_zr], writes=[B_zr])
                        b.op("sync", lambda e, slot=slot: e.dma_start(
                             AP(bouncez, (slot + 1) * 512, [[1, 1], [1, 512]]), zrow[64:65, 0:512]),
                             reads=[B_zr], writes=[Bz2], dma=True)
                        b.op("gpsimd", lambda e, slot=slot: e.dma_start(zbc[0:64, 0:512],
                             AP(bouncez, slot * 512, [[0, 64], [1, 512]])),
                             reads=[Bz1], writes=[B_zbc], dma=True)
                        b.op("gpsimd", lambda e, slot=slot: e.dma_start(zbc[0:64, 512:1024],
                             AP(bouncez, (slot + 1) * 512, [[0, 64], [1, 512]])),
                             reads=[Bz2], writes=[B_zbc], dma=True)
                        b.op("vector", lambda e, Q=Q, hp=hp: e.scalar_tensor_tensor(
                             aoT[0:64, hp, Q * 512:(Q + 1) * 512], ps_b[0:64, 0:512], 1.0,
                             zbc[0:64, 0:512], OP.mult, OP.mult), reads=[PB[0], B_zbc], writes=[B_ao])
                        b.op("vector", lambda e: e.scalar_tensor_tensor(
                             bstage[:, :], ps_b[0:64, 512:1024], 1.0,
                             zbc[0:64, 512:1024], OP.mult, OP.mult), reads=[PB[1], B_zbc], writes=[B_bst])
                        b.op("sync", lambda e, Q=Q, hp=hp: e.dma_start(
                             aoT[64:128, hp, Q * 512:(Q + 1) * 512], bstage[:, :]),
                             reads=[B_bst], writes=[B_ao], dma=True)
                if seq == 0 and hp == 0 and "d_negmq2" in debug:
                    t2 = nc.dram_tensor("d_negmq2", [128, 16], F32, kind="ExternalOutput")
                    dbg["d_negmq2"] = t2
                    b.op("scalar", lambda e: e.activation(dbgst[:, 0:16], negmQA[:, :], AF.Identity),
                         reads=[B_nqa], writes=[B_pB])
                    out_toks.append(b.op("sync", lambda e: e.dma_start(t2[:, :], dbgst[:, 0:16]),
                                         reads=[B_pB], dma=True))
            if seq == 0:
                dump("d_aoT", aoT[:, :, :], [128, ND, S], BF16, [B_ao])
            stream_w(dr["Wo"], 0, D)
            for tt in range(NT):
                pr = tt % 2
                def womm(e, tt=tt):
                    for ic in range(ND):
                        for half in range(2):
                            ii = e.matmul(ps_a[:, half * 512:(half + 1) * 512],
                                          aoT[:, ic, tt * 128:(tt + 1) * 128],
                                          warena[:, ic, half * 512:(half + 1) * 512],
                                          start=(ic == 0), stop=(ic == ND - 1))
                    return ii
                b.op("tensor", womm, reads=[B_ao, B_wa], writes=[PA[0], PA[1]])
                b.op("vector", lambda e, pr=pr: e.tensor_copy(stile[pr][:, :], ps_a[:, 0:1024]),
                     reads=[PA[0], PA[1]], writes=[B_st[pr]])
                b.op("sync", lambda e, tt=tt, pr=pr: e.dma_start(s_scr[tt, :, :], stile[pr][:, :]),
                     reads=[B_st[pr]], writes=[B_sscr], dma=True)
            if seq == 0:
                dump("d_s", stile[(NT - 1) % 2][:, :], [128, D], BF16, [B_st[(NT - 1) % 2]])

        def phase_C(seq):
            b.op("vector", lambda e: e.memset(alpha_sb[:, :], 1.0), writes=[B_al])
            for (p0, pstep, cnt, col, srow) in alpha_runs.get(seq, []):
                b.op("sync", lambda e, p0=p0, pstep=pstep, cnt=cnt, col=col, srow=srow:
                     e.dma_start(AP(alpha_sb, p0 * NT + col, [[pstep * NT, cnt], [1, 1]]),
                                 AP(alpha_scr, srow, [[1, cnt], [1, 1]])),
                     reads=[B_als, B_al], writes=[B_al], dma=True)
            for tt in range(NT):
                pr = tt % 2
                b.op("sync", lambda e, tt=tt, pr=pr: e.dma_start(x_t[pr][:, :],
                     dr["x"][seq, tt * 128:(tt + 1) * 128, :]), writes=[B_x[pr]], dma=True)
                b.op("sync", lambda e, tt=tt, pr=pr: e.dma_start(stile[pr][:, :], s_scr[tt, :, :]),
                     reads=[B_sscr], writes=[B_st[pr]], dma=True)
                b.op("vector", lambda e: e.memset(fix_t[0][:, :], 0.0), writes=[B_fx[0]])
                for (p0, pstep, cnt, srow) in fix_runs.get((seq, tt), []):
                    b.op("sync", lambda e, p0=p0, pstep=pstep, cnt=cnt, srow=srow:
                         e.dma_start(AP(fix_t[0], p0 * D, [[pstep * D, cnt], [1, D]]),
                                     fix_scr[srow:srow + cnt, :]),
                         reads=[B_fixs, B_fx[0]], writes=[B_fx[0]], dma=True)
                def x12(e, tt=tt):
                    for dc in range(ND):
                        e.matmul(ps_a[:, 0:512], xhT[:, dc, tt * 128:(tt + 1) * 128],
                                 sw1T[:, dc, :], start=(dc == 0), stop=(dc == ND - 1))
                        ii = e.matmul(ps_a[:, 512:1024], xhT[:, dc, tt * 128:(tt + 1) * 128],
                                      sw2T[:, dc, :], start=(dc == 0), stop=(dc == ND - 1))
                    return ii
                b.op("tensor", x12, reads=[B_xhT, B_sw1, B_sw2], writes=[PA[0], PA[1]])
                b.op("scalar", lambda e, tt=tt: e.activation(x1f, ps_a[:, 0:512],
                     AF.Identity, scale=stats[:, tt, 2:3]), reads=[PA[0], B_stats], writes=[B_x1])
                b.op("scalar", lambda e, tt=tt: e.activation(x2f, ps_a[:, 512:1024],
                     AF.Identity, scale=stats[:, tt, 2:3]), reads=[PA[1], B_stats], writes=[B_x2])
                b.op("vector", lambda e: e.tensor_tensor(x1f, x1f, sb1bc[:, :], OP.add),
                     reads=[B_x1, B_sb1], writes=[B_x1])
                b.op("vector", lambda e: e.tensor_tensor(x2f, x2f, sb2bc[:, :], OP.add),
                     reads=[B_x2, B_sb2], writes=[B_x2])
                b.op("scalar", lambda e: e.activation(x1s, x1f, AF.Silu),
                     reads=[B_x1], writes=[B_x1s])
                b.op("vector", lambda e: e.tensor_tensor(g0_sb, x1s, x2f, OP.mult),
                     reads=[B_x1s, B_x2], writes=[B_g0])
                if seq == 0 and tt == 0:
                    dump("d_x1", x1f, [128, FH], BF16, [B_x1])
                    dump("d_g0", g0_sb, [128, FH], BF16, [B_g0])
                ln512(g0_sb, B_g0, gh_sb, B_gh)
                def trg(e):
                    for hc in range(4):
                        ii = e.transpose(ps_a_bf[:, 2048 + hc * 128:2048 + (hc + 1) * 128],
                                         p_B[:, hc * 128:(hc + 1) * 128], ident[:, :])
                    return ii
                b.op("tensor", trg, reads=[B_gh, B_id], writes=[PA[2]])
                b.op("vector", lambda e: e.tensor_copy(shTt, ps_a_bf[:, 2048:2560]),
                     reads=[PA[2]], writes=[B_sh])
                def shmm(e):
                    for hc in range(4):
                        e.matmul(ps_b[:, 0:512], p_B[:, 512 + hc * 128:512 + (hc + 1) * 128],
                                 sw3gT[:, hc, 0:512], start=(hc == 0), stop=False)
                        e.matmul(ps_b[:, 512:1024], p_B[:, 512 + hc * 128:512 + (hc + 1) * 128],
                                 sw3gT[:, hc, 512:1024], start=(hc == 0), stop=False)
                    e.matmul(ps_b[:, 0:512], ones2k[0:1, 0:128], rows_b[0:1, 0:512],
                             start=False, stop=True)
                    return e.matmul(ps_b[:, 512:1024], ones2k[0:1, 0:128], rows_b[0:1, 512:1024],
                                    start=False, stop=True)
                b.op("tensor", shmm, reads=[B_sh, B_sw3, B_ones, B_cb], writes=[PB[0], PB[1]])
                if seq == 0 and tt == 0:
                    dump_ps("d_shared", ps_b[:, 0:1024], 1024, [PB[0], PB[1]])
                b.op("vector", lambda e, tt=tt: e.scalar_tensor_tensor(ffn_f,
                     ps_b[:, 0:1024], alpha_sb[:, tt:tt + 1], fix_t[0][:, :], OP.mult, OP.add),
                     reads=[PB[0], PB[1], B_al, B_fx[0]], writes=[B_ffn])
                b.op("scalar", lambda e: e.activation(g_t, ffn_f, AF.Silu),
                     reads=[B_ffn], writes=[B_g])
                b.op("gpsimd", lambda e, pr=pr: e.tensor_tensor(o_t, stile[pr][:, :],
                     g_t, OP.mult), reads=[B_st[pr], B_g], writes=[B_o])
                b.op("gpsimd", lambda e, pr=pr: e.tensor_tensor(o_t, o_t,
                     x_t[pr][:, :], OP.add), reads=[B_o, B_x[pr]], writes=[B_o])
                out_toks.append(b.op("sync", lambda e, tt=tt: e.dma_start(
                     out_t[seq, tt * 128:(tt + 1) * 128, :], o_t), reads=[B_o], dma=True))

        def ln512(src, Bsrc, dst, Bdst):
            b.op("vector", lambda e: e.tensor_scalar(junk_v[:, 0:512], src[:, :], 0.0, None,
                 OP.add, op1=OP.add, accum_out=tstat[:, 0:1]), reads=[Bsrc], writes=[B_jv, B_ts])
            b.op("scalar", lambda e: e.activation(junk_a[:, 0:512], src[:, :], AF.Square,
                 accum_out=tstat[:, 1:2]), reads=[Bsrc], writes=[B_ja, B_ts])
            def post(e):
                n = 512.0
                e.tensor_scalar(tstat[:, 2:3], tstat[:, 0:1], 1.0 / n, None, OP.mult)
                e.tensor_scalar(tstat[:, 3:4], tstat[:, 1:2], 1.0 / n, None, OP.mult)
                e.tensor_tensor(tstat[:, 4:5], tstat[:, 2:3], tstat[:, 2:3], OP.mult)
                e.tensor_tensor(tstat[:, 4:5], tstat[:, 3:4], tstat[:, 4:5], OP.subtract)
                e.tensor_scalar(tstat[:, 5:6], tstat[:, 4:5], EPS, None, OP.add)
                e.tensor_tensor(tstat[:, 6:7], tstat[:, 5:6], pconst[:, 0:1], OP.pow)
                e.tensor_tensor(tstat[:, 7:8], tstat[:, 2:3], tstat[:, 6:7], OP.mult)
                return e.tensor_scalar(tstat[:, 7:8], tstat[:, 7:8], -1.0, None, OP.mult)
            b.op("gpsimd", post, reads=[B_ts, B_pc], writes=[B_ts])
            b.op("scalar", lambda e: e.activation(dst[:, :], src[:, :], AF.Identity,
                 bias=tstat[:, 7:8], scale=tstat[:, 6:7]), reads=[Bsrc, B_ts], writes=[Bdst])

        def phase_M():
            xmT, B_xmT = xhT, B_xhT
            acc, B_acc = aoT, B_ao
            sig9, B_s9 = sbuf("sig9", [128, 9], F32)
            t8, B_t8 = sbuf("t8", [128, NE], F32)
            tmax, B_tm = sbuf("tmax", [128, 1], F32)
            oh8, B_oh = sbuf("oh8", [128, NE], F32)
            pref, B_pf = sbuf("pref", [128, 1], F32)
            nots, B_no = sbuf("nots", [128, 1], F32)
            ts1, B_t1 = sbuf("ts1", [128, 1], F32)
            swv, B_swv = sbuf("swv", [128, 1], F32)
            m1, B_m1 = p_B[:, 1024:1536], B_pB
            zz2, B_zz = p_B[:, 1536:2048], B_pB

            for mt in range(NM):
                pr = mt % 2
                b.op("sync", lambda e, mt=mt, pr=pr: e.dma_start(x_t[pr][:, :],
                     dr["xmoe"][mt * 128:(mt + 1) * 128, :]), writes=[B_x[pr]], dma=True)
                tile_stats(x_t[pr][:, :], B_x[pr], mt, "moe")
                b.op("scalar", lambda e, mt=mt, pr=pr: e.activation(xc_t[pr][:, :], x_t[pr][:, :],
                     AF.Identity, bias=stats[:, mt, 1:2], scale=stats[:, mt, 0:1]),
                     reads=[B_x[pr], B_stats], writes=[B_xc[pr]])
                def trm(e, pr=pr):
                    for dc in range(ND):
                        ii = e.transpose(ps_b_bf[:, 2048 + dc * 128:2048 + (dc + 1) * 128],
                                         xc_t[pr][:, dc * 128:(dc + 1) * 128], ident[:, :])
                    return ii
                b.op("tensor", trm, reads=[B_xc[pr], B_id], writes=[PB[2]])
                def evm(e, mt=mt):
                    for dc in range(ND):
                        ii = e.tensor_copy(xmT[:, dc, mt * 128:(mt + 1) * 128],
                                           ps_b_bf[:, 2048 + dc * 128:2048 + (dc + 1) * 128])
                    return ii
                b.op("vector", evm, reads=[PB[2]], writes=[B_xmT])
                def rmm(e, mt=mt):
                    for dc in range(ND):
                        e.matmul(ps_a[:, 0:9], xmT[:, dc, mt * 128:(mt + 1) * 128],
                                 rw_sb[:, dc, :], start=(dc == 0), stop=False)
                    return e.matmul(ps_a[:, 0:9], ones2k[64:65, 0:128], rbrow,
                                    start=False, stop=True)
                b.op("tensor", rmm, reads=[B_xmT, B_rw, B_ones, B_rb], writes=[PA[0]])
                b.op("scalar", lambda e: e.activation(sig9[:, :], ps_a[:, 0:9], AF.Sigmoid),
                     reads=[PA[0]], writes=[B_s9])
                b.op("vector", lambda e: e.tensor_tensor(t8[:, :], sig9[:, 1:9], expbbc[:, :],
                     OP.add), reads=[B_s9, B_expb], writes=[B_t8])
                if mt == 0:
                    dump("d_t8pre", t8[:, :], [128, NE], F32, [B_t8])
                    dump("d_expb", expbbc[:, :], [128, NE], F32, [B_expb])
                b.op("vector", lambda e: e.tensor_tensor(ztl[:, 0:4], t8[:, 0:4],
                     t8[:, 4:8], OP.max), reads=[B_t8], writes=[B_ztl])
                b.op("vector", lambda e: e.tensor_tensor(ztl[:, 0:2], ztl[:, 0:2],
                     ztl[:, 2:4], OP.max), reads=[B_ztl], writes=[B_ztl])
                b.op("vector", lambda e: e.tensor_tensor(tmax[:, :], ztl[:, 0:1],
                     ztl[:, 1:2], OP.max), reads=[B_ztl], writes=[B_tm])
                b.op("vector", lambda e: e.tensor_scalar(oh8[:, :], t8[:, :], tmax[:, :], None,
                     OP.is_ge), reads=[B_t8, B_tm], writes=[B_oh])
                b.op("vector", lambda e: e.tensor_copy(pref[:, :], oh8[:, 0:1]),
                     reads=[B_oh], writes=[B_pf])
                for ee in range(1, NE):
                    b.op("vector", lambda e: e.tensor_scalar(nots[:, :], pref[:, :], -1.0, 1.0,
                         OP.mult, op1=OP.add), reads=[B_pf], writes=[B_no])
                    b.op("vector", lambda e, ee=ee: e.tensor_tensor(oh8[:, ee:ee + 1],
                         oh8[:, ee:ee + 1], nots[:, :], OP.mult), reads=[B_oh, B_no], writes=[B_oh])
                    b.op("vector", lambda e, ee=ee: e.tensor_tensor(pref[:, :], pref[:, :],
                         oh8[:, ee:ee + 1], OP.max), reads=[B_pf, B_oh], writes=[B_pf])
                b.op("vector", lambda e: e.tensor_tensor(t8[:, :], oh8[:, :], sig9[:, 1:9],
                     OP.mult), reads=[B_oh, B_s9], writes=[B_t8])
                b.op("vector", lambda e: e.tensor_tensor(ztl[:, 0:4], t8[:, 0:4],
                     t8[:, 4:8], OP.add), reads=[B_t8], writes=[B_ztl])
                b.op("vector", lambda e: e.tensor_tensor(ztl[:, 0:2], ztl[:, 0:2],
                     ztl[:, 2:4], OP.add), reads=[B_ztl], writes=[B_ztl])
                b.op("vector", lambda e: e.tensor_tensor(ts1[:, :], ztl[:, 0:1],
                     ztl[:, 1:2], OP.add), reads=[B_ztl], writes=[B_t1])
                b.op("vector", lambda e: e.tensor_tensor(nots[:, :], sig9[:, 0:1], ts1[:, :],
                     OP.subtract), reads=[B_s9, B_t1], writes=[B_no])
                b.op("scalar", lambda e: e.activation(swv[:, :], nots[:, :], AF.Sigmoid),
                     reads=[B_no], writes=[B_swv])
                b.op("vector", lambda e: e.tensor_scalar(nots[:, :], nots[:, :], -1.0, None,
                     OP.mult), reads=[B_no], writes=[B_no])
                b.op("scalar", lambda e: e.activation(ts1[:, :], nots[:, :], AF.Sigmoid),
                     reads=[B_no], writes=[B_t1])
                b.op("vector", lambda e, mt=mt: e.tensor_scalar(wexpall[:, mt, :], oh8[:, :],
                     ts1[:, :], None, OP.mult), reads=[B_oh, B_t1], writes=[B_we])
                b.op("vector", lambda e, mt=mt: e.tensor_scalar(wexpall[:, mt, :],
                     wexpall[:, mt, :], valid_sb[:, mt:mt + 1], None, OP.mult),
                     reads=[B_we, B_valid], writes=[B_we])
                b.op("vector", lambda e: e.tensor_scalar(swv[:, :], swv[:, :], -1.0, None, OP.add),
                     reads=[B_swv], writes=[B_swv])
                b.op("vector", lambda e, mt=mt: e.tensor_scalar(swv[:, :], swv[:, :],
                     valid_sb[:, mt:mt + 1], 1.0, OP.mult, op1=OP.add),
                     reads=[B_swv, B_valid], writes=[B_swv])
                b.op("sync", lambda e, mt=mt: e.dma_start(alpha_scr[mt, :], swv[:, 0:1]),
                     reads=[B_swv], writes=[B_als], dma=True)
                if mt == 0:
                    dump("d_sig9", sig9[:, :], [128, 9], F32, [B_s9])
                    dump("d_swv", swv[:, :], [128, 1], F32, [B_swv])
                    dump("d_t8", t8[:, :], [128, NE], F32, [B_t8])
                    dump("d_tmax", tmax[:, :], [128, 1], F32, [B_tm])
                    dump("d_oh8", oh8[:, :], [128, NE], F32, [B_oh])
                    dump("d_ts1", ts1[:, :], [128, 1], F32, [B_t1])
            dump("d_wexp", wexpall[:, 0, :], [128, NE], F32, [B_we])
            dump("d_xmT", xmT[:, :, :], [128, ND, S], BF16, [B_xmT])

            for e_i in range(NE):
                for dc in range(ND):
                    b.op("gpsimd", lambda e, e_i=e_i, dc=dc: e.dma_start(warena[:, dc, 0:FH],
                         dr["ew1T"][e_i, dc * 128:(dc + 1) * 128, :]), writes=[B_wa], dma=True)
                    b.op("gpsimd", lambda e, e_i=e_i, dc=dc: e.dma_start(warena[:, dc, FH:2 * FH],
                         dr["ew2T"][e_i, dc * 128:(dc + 1) * 128, :]), writes=[B_wa], dma=True)
                b.op("gpsimd", lambda e, e_i=e_i: e.dma_start(eb1row,
                     dr["eb1"][e_i:e_i + 1, :]), writes=[B_eb1], dma=True)
                b.op("gpsimd", lambda e, e_i=e_i: e.dma_start(eb2row,
                     dr["eb2"][e_i:e_i + 1, :]), writes=[B_eb2], dma=True)
                b.op("gpsimd", lambda e, e_i=e_i: e.dma_start(eb3row,
                     dr["eb3"][e_i:e_i + 1, :]), writes=[B_eb3], dma=True)
                for hc in range(4):
                    b.op("gpsimd", lambda e, e_i=e_i, hc=hc: e.dma_start(
                         ptA[:, 2 * hc:2 * hc + 2, :].rearrange("p a q -> p (a q)"),
                         dr["ew3T"][e_i, hc * 128:(hc + 1) * 128, 0:512]), writes=[B_ptA], dma=True)
                    b.op("gpsimd", lambda e, e_i=e_i, hc=hc: e.dma_start(
                         ptA[:, 8 + 2 * hc:8 + 2 * hc + 2, :].rearrange("p a q -> p (a q)"),
                         dr["ew3T"][e_i, hc * 128:(hc + 1) * 128, 512:1024]), writes=[B_ptA], dma=True)
                for mt in range(NM):
                    def h12(e, mt=mt):
                        for dc in range(ND):
                            e.matmul(ps_a[:, 0:512], xmT[:, dc, mt * 128:(mt + 1) * 128],
                                     warena[:, dc, 0:FH], start=(dc == 0), stop=False)
                            e.matmul(ps_a[:, 512:1024], xmT[:, dc, mt * 128:(mt + 1) * 128],
                                     warena[:, dc, FH:2 * FH], start=(dc == 0), stop=False)
                        e.matmul(ps_a[:, 0:512], ones2k[0:1, 0:128], eb1row,
                                 start=False, stop=True)
                        return e.matmul(ps_a[:, 512:1024], ones2k[32:33, 0:128], eb2row,
                                        start=False, stop=True)
                    b.op("tensor", h12, reads=[B_xmT, B_wa, B_ones, B_eb1, B_eb2],
                         writes=[PA[0], PA[1]])
                    b.op("scalar", lambda e: e.activation(m1, ps_a[:, 0:512], AF.Silu),
                         reads=[PA[0]], writes=[B_m1])
                    b.op("vector", lambda e: e.tensor_tensor(g0_sb, m1,
                         ps_a[:, 512:1024], OP.mult), reads=[B_m1, PA[1]], writes=[B_g0])
                    ln512(g0_sb, B_g0, zz2, B_zz)
                    b.op("vector", lambda e: e.scalar_tensor_tensor(zz2, zz2, 1.0,
                         selgbc[:, :], OP.mult, OP.mult), reads=[B_zz, B_selg], writes=[B_zz])
                    b.op("vector", lambda e: e.tensor_tensor(zz2, zz2, selbbc[:, :],
                         OP.add), reads=[B_zz, B_selb], writes=[B_zz])
                    def trz(e):
                        for hc in range(4):
                            ii = e.transpose(ps_a_bf[:, 2048 + hc * 128:2048 + (hc + 1) * 128],
                                             p_B[:, 1536 + hc * 128:1536 + (hc + 1) * 128], ident[:, :])
                        return ii
                    b.op("tensor", trz, reads=[B_zz, B_id], writes=[PA[2]])
                    b.op("vector", lambda e: e.tensor_copy(shTt, ps_a_bf[:, 2048:2560]),
                         reads=[PA[2]], writes=[B_sh])
                    def omm(e):
                        for hc in range(4):
                            e.matmul(ps_b[:, 0:512], p_B[:, 512 + hc * 128:512 + (hc + 1) * 128],
                                     ptA[:, 2 * hc:2 * hc + 2, :].rearrange("p a q -> p (a q)"),
                                     start=(hc == 0), stop=False)
                            e.matmul(ps_b[:, 512:1024], p_B[:, 512 + hc * 128:512 + (hc + 1) * 128],
                                     ptA[:, 8 + 2 * hc:8 + 2 * hc + 2, :].rearrange("p a q -> p (a q)"),
                                     start=(hc == 0), stop=False)
                        e.matmul(ps_b[:, 0:512], ones2k[64:65, 0:128], rows_b[64:65, 0:512],
                                 start=False, stop=True)
                        return e.matmul(ps_b[:, 512:1024], ones2k[64:65, 0:128], rows_b[64:65, 512:1024],
                                        start=False, stop=True)
                    b.op("tensor", omm, reads=[B_sh, B_ptA, B_ones, B_eb3], writes=[PB[0], PB[1]])
                    a_ap = acc[:, mt // 2, (mt % 2) * 1024:(mt % 2 + 1) * 1024]
                    if e_i == 0:
                        b.op("vector", lambda e, a_ap=a_ap, mt=mt: e.tensor_scalar(a_ap,
                             ps_b[:, 0:1024], wexpall[:, mt, 0:1], None, OP.mult),
                             reads=[PB[0], PB[1], B_we], writes=[B_acc])
                    else:
                        b.op("vector", lambda e, a_ap=a_ap, mt=mt, e_i=e_i:
                             e.scalar_tensor_tensor(a_ap, ps_b[:, 0:1024],
                             wexpall[:, mt, e_i:e_i + 1], a_ap, OP.mult, OP.add),
                             reads=[PB[0], PB[1], B_we, B_acc], writes=[B_acc])
                    if e_i == NE - 1:
                        b.op("sync", lambda e, mt=mt, a_ap=a_ap: e.dma_start(
                             fix_scr[mt * 128:(mt + 1) * 128, :], a_ap),
                             reads=[B_acc], writes=[B_fixs], dma=True)
                        if mt == 0:
                            dump("d_acc", fix_scr[0:128, :], [128, D], BF16, [B_fixs])

        if upto in ("M", "ALL"):
            phase_M()
        if upto != "M":
            for seq in range(NSEQ):
                phase_A(seq)
                if upto in ("AB", "ABC", "ALL"):
                    phase_B(seq)
                if upto in ("ABC", "ALL"):
                    phase_C(seq)

        b.emit(out_toks)

    return nc, dbg


def prepare_inputs(inp, n_seq_batches, union, valid, n_moe):
    x = np.ascontiguousarray(inp["L0_attn_x"][n_seq_batches])
    xm = np.zeros((n_moe, D), np.float32)
    for u, (sl, t) in enumerate(union):
        xm[u] = x[sl, t]
    m = {
        "x": x, "xmoe": xm,
        "ident_bf": np.eye(128, dtype=bf16),
        "ident_f32": np.eye(128, dtype=np.float32),
        "bdiag": np.concatenate([np.repeat([[1, 0]], 64, 0), np.repeat([[0, 1]], 64, 0)]).astype(bf16),
        "ln_w": inp["L0_attn_ln_w"], "ln_b": inp["L0_attn_ln_b"],
        "Wq": inp["L0_attn_Wq"], "Wkv": inp["L0_attn_Wkv"], "Wo": inp["L0_attn_Wo"],
        "q_gamma": np.ascontiguousarray(inp["L0_attn_q_gamma"]).reshape(-1),
        "k_gamma": np.ascontiguousarray(inp["L0_attn_k_gamma"]).reshape(-1),
        "sw1T": np.ascontiguousarray(inp["L0_ffn_sw1"].T),
        "sw2T": np.ascontiguousarray(inp["L0_ffn_sw2"].T),
        "sw3T": np.ascontiguousarray(inp["L0_ffn_sw3"].T),
        "sb1": inp["L0_ffn_sb1"], "sb2": inp["L0_ffn_sb2"], "sb3": inp["L0_ffn_sb3"],
        "s_ln_g": inp["L0_ffn_s_ln_g"], "s_ln_b": inp["L0_ffn_s_ln_b"],
        "rw": np.ascontiguousarray(
            np.concatenate([inp["L0_ffn_sg_w"], inp["L0_ffn_eg_w"]], 0).T),
        "rb": np.concatenate([inp["L0_ffn_sg_b"], inp["L0_ffn_eg_b"]]),
        "exp_bias": inp["L0_ffn_exp_bias"],
        "ew1T": np.ascontiguousarray(inp["L0_ffn_ew1"].transpose(0, 2, 1)),
        "ew2T": np.ascontiguousarray(inp["L0_ffn_ew2"].transpose(0, 2, 1)),
        "ew3T": np.ascontiguousarray(inp["L0_ffn_ew3"].transpose(0, 2, 1)),
        "eb1": inp["L0_ffn_eb1"], "eb2": inp["L0_ffn_eb2"], "eb3": inp["L0_ffn_eb3"],
        "sel_ln_g": inp["L0_ffn_sel_ln_g"], "sel_ln_b": inp["L0_ffn_sel_ln_b"],
        "mvalid": valid,
    }
    return {k: np.ascontiguousarray(v) for k, v in m.items()}


def plan_cores(inp):
    ib = np.asarray(inp["L0_ffn_index_b"])
    it = np.asarray(inp["L0_ffn_index_t"])
    M = len(ib)
    key = ib.astype(np.int64) * S + it
    last = {}
    for i in range(M):
        last[int(key[i])] = i
    keep = sorted(last.values())
    ib2, it2 = ib[keep], it[keep]
    counts = np.bincount(ib2, minlength=32)
    order = np.argsort(-counts, kind="stable")
    cores = [[] for _ in range(8)]
    load = [0] * 8
    for b_ in order:
        cands = [c for c in range(8) if len(cores[c]) < 4]
        ci = min(cands, key=lambda c: load[c])
        cores[ci].append(int(b_))
        load[ci] += int(counts[b_])
    targets = [set() for _ in range(8)]
    owner = {}
    for ci in range(8):
        bmap = {b_: i for i, b_ in enumerate(cores[ci])}
        for j in range(len(ib2)):
            bb = int(ib2[j])
            if bb in bmap:
                targets[ci].add((bmap[bb], int(it2[j])))
    union = sorted(set().union(*targets)) if any(targets) else []
    n_moe = max(128, ((len(union) + 127) // 128) * 128)
    fix_runs = {}
    alpha_runs = {}
    i = 0
    while i < len(union):
        sl, t = union[i]
        tt = t // 128
        p = t % 128
        j = i + 1
        step = None
        while j < len(union):
            sl2, t2 = union[j]
            if sl2 != sl or t2 // 128 != tt:
                break
            st = (t2 % 128) - (union[j - 1][1] % 128)
            if step is None:
                if st <= 0:
                    break
                step = st
            elif st != step:
                break
            j += 1
        cnt = j - i
        if cnt == 1:
            step = 1
        fix_runs.setdefault((sl, tt), []).append((p, step, cnt, i))
        alpha_runs.setdefault(sl, []).append((p, step, cnt, tt, i))
        i = j
    core_plans = []
    for ci in range(8):
        batches = cores[ci]
        valid = np.zeros(n_moe, np.float32)
        rows = []
        for u, (sl, t) in enumerate(union):
            if (sl, t) in targets[ci]:
                valid[u] = 1.0
                rows.append((u, sl, t))
        core_plans.append(dict(batches=batches, union=union, valid=valid))
    return core_plans, union, n_moe, fix_runs, alpha_runs


def kernel_host(inp, upto="ALL", debug=(), core_ids=None):
    core_plans, union, n_moe, fix_runs, alpha_runs = plan_cores(inp)
    nc, dbg = build_program(n_moe, fix_runs, alpha_runs, upto=upto, debug=debug)
    in_maps = []
    ids = core_ids if core_ids is not None else list(range(8))
    for ci in ids:
        pl = core_plans[ci]
        in_maps.append(prepare_inputs(inp, pl["batches"], union, pl["valid"], n_moe))
    res = run_bass_kernel_spmd(nc, in_maps, core_ids=ids)
    return res, core_plans, dbg


def kernel(**inputs):
    res, core_plans, _ = kernel_host(inputs, upto="ALL", debug=(), core_ids=list(range(8)))
    out = np.empty((32, S, D), np.float32)
    for ci in range(8):
        got = np.asarray(res.results[ci]["out"], np.float32)
        for sl, b_ in enumerate(core_plans[ci]["batches"]):
            out[b_] = got[sl]
    return out


# version 08
# speedup vs baseline: 1.1562x; 1.1562x over previous
import numpy as np
import ml_dtypes
from contextlib import ExitStack

import concourse.bass as bass
import concourse.mybir as mybir
from concourse.alu_op_type import AluOpType
from concourse.bass_utils import run_bass_kernel_spmd

F32 = mybir.dt.float32
BF16 = mybir.dt.bfloat16
AF = mybir.ActivationFunctionType
OP = AluOpType
bf16 = ml_dtypes.bfloat16

S, D, H, DH = 2048, 1024, 16, 64
QG = 2
PVW = 128 * QG
NG = 16 // QG
NSEQ = 4
NT = S // 128
ND = D // 128
FH = 512
NE = 8
EPS = 1e-5
ENGINES = ("tensor", "vector", "scalar", "gpsimd", "sync")


class Tok:
    __slots__ = ("eng", "op")
    def __init__(self, eng, op):
        self.eng, self.op = eng, op


class Buf:
    __slots__ = ("name", "writer", "readers")
    def __init__(self, name):
        self.name = name
        self.writer = None
        self.readers = []


class Op:
    __slots__ = ("eng", "fn", "waits", "k", "consumed", "val", "slot")
    def __init__(self, eng, fn, waits, k):
        self.eng, self.fn, self.waits, self.k = eng, fn, waits, k
        self.consumed = False
        self.val = None
        self.slot = None


class Bld:
    def __init__(self, nc):
        self.nc = nc
        self.ops = {e: [] for e in ENGINES}

    def op(self, eng, fn, reads=(), writes=(), after=(), dma=False):
        waits = []
        for b_ in reads:
            if b_.writer is not None:
                waits.append(b_.writer)
        for b_ in writes:
            if b_.writer is not None:
                waits.append(b_.writer)
            waits.extend(b_.readers)
        waits.extend(t for t in after if t is not None)
        o = Op(eng, fn, waits, 16 if dma else 1)
        self.ops[eng].append(o)
        t = Tok(eng, o)
        for b_ in reads:
            b_.readers.append(t)
        for b_ in writes:
            b_.writer = t
            b_.readers = []
        return t

    NDMA = 8

    def emit(self, final_toks):
        self.op("gpsimd", lambda e: e.nop(), after=list(final_toks))
        for e in ENGINES:
            for o in self.ops[e]:
                for t in o.waits:
                    t.op.consumed = True
        for t in final_toks:
            t.op.consumed = True
        R = self.NDMA
        for e in ENGINES:
            run = 0
            dma_i = 0
            for o in self.ops[e]:
                if o.k == 16:
                    o.consumed = True
                    o.slot = dma_i % R
                    o.val = 16 * (dma_i // R + 1)
                    dma_i += 1
                elif o.consumed:
                    run += 1
                    o.val = run
                    o.slot = None
        with ExitStack() as es:
            sems = {e: es.enter_context(self.nc.semaphore("sem_" + e)) for e in ENGINES}
            dsems = {e: [es.enter_context(self.nc.semaphore(f"dma_{e}_{i}")) for i in range(R)]
                     for e in ("sync", "gpsimd", "scalar", "vector", "tensor")}
            blk = es.enter_context(self.nc.Block())

            def mkbody(engname):
                def _(eng):
                    maxw = {}
                    for o in self.ops[engname]:
                        best = {}
                        for t in o.waits:
                            v = t.op.val
                            assert v is not None, "wait on unconsumed op"
                            key = (t.eng, t.op.slot)
                            if v > best.get(key, 0):
                                best[key] = v
                        if o.k == 16 and o.val > 16:
                            key = (engname, o.slot)
                            if best.get(key, 0) < o.val - 16:
                                best[key] = o.val - 16
                        for key in sorted(best, key=lambda k: (k[0], -1 if k[1] is None else k[1])):
                            v = best[key]
                            if v > maxw.get(key, 0):
                                semh = sems[key[0]] if key[1] is None else dsems[key[0]][key[1]]
                                eng.wait_ge(semh, v)
                                maxw[key] = v
                        instr = o.fn(eng)
                        if o.consumed:
                            if instr is None:
                                instr = eng.nop()
                            if o.k == 16:
                                instr.then_inc(dsems[engname][o.slot], 16)
                            else:
                                instr.then_inc(sems[engname], 1)
                return _

            for e in ENGINES:
                getattr(blk, e)(mkbody(e))


def build_program(n_moe, fix_runs, alpha_runs, upto="ALL", debug=()):
    nc = bass.Bass("TRN2", target_bir_lowering=False)
    NM = max(n_moe // 128, 1)
    AP = bass.AP

    dr = {}
    def din(name, shape, dt=F32):
        dr[name] = nc.dram_tensor(name, shape, dt, kind="ExternalInput")
    din("x", [NSEQ, S, D]); din("xmoe", [NM * 128, D])
    din("ident_bf", [128, 128], BF16); din("ident_f32", [128, 128], F32)
    din("bdiag", [128, 2], BF16)
    din("ln_w", [D]); din("ln_b", [D])
    din("Wq", [D, D]); din("Wkv", [D, 2 * D]); din("Wo", [D, D])
    din("q_gamma", [H * DH]); din("k_gamma", [H * DH])
    din("sw1T", [D, FH]); din("sw2T", [D, FH]); din("sw3T", [FH, D])
    din("sb1", [FH]); din("sb2", [FH]); din("sb3", [D])
    din("s_ln_g", [FH]); din("s_ln_b", [FH])
    din("rw", [D, 9]); din("rb", [9]); din("exp_bias", [NE])
    din("ew1T", [NE, D, FH]); din("ew2T", [NE, D, FH]); din("ew3T", [NE, FH, D])
    din("eb1", [NE, FH]); din("eb2", [NE, FH]); din("eb3", [NE, D])
    din("sel_ln_g", [FH]); din("sel_ln_b", [FH])
    din("mvalid", [NM * 128])

    out_t = nc.dram_tensor("out", [NSEQ, S, D], F32, kind="ExternalOutput")
    v_scr = nc.dram_tensor("v_scr", [NT, 128, D], BF16)
    s_scr = nc.dram_tensor("s_scr", [NT, 128, D], BF16)
    fix_scr = nc.dram_tensor("fix_scr", [NM * 128, D], BF16)
    alpha_scr = nc.dram_tensor("alpha_scr", [NM, 128], F32)
    bounce = nc.dram_tensor("bounce", [2600, S], BF16)
    bouncef = nc.dram_tensor("bouncef", [600, 512], BF16)
    bctr = [0]
    zctr = [0]

    dbg = {}
    out_toks = []

    es = ExitStack()
    with es:
        def sbuf(name, shape, dt):
            t = es.enter_context(nc.sbuf_tensor("sb_" + name, shape, dt))
            return t, Buf(name)

        b = Bld(nc)

        ident, B_id = sbuf("ident", [128, 128], BF16)
        identf, B_idf = sbuf("identf", [128, 128], F32)
        bdiag, B_bd = sbuf("bdiag_sb", [128, 2], BF16)
        ones2k, B_ones = sbuf("ones65", [65, 512], BF16)
        gq8, B_gq = sbuf("gq8", [128, ND], F32)
        gk8, B_gk = sbuf("gk8", [128, ND], F32)
        lnw8, B_lnw = sbuf("lnw8", [128, ND], F32)
        lnwi, B_lnwi = sbuf("lnwi", [128, ND], F32)
        lnb8, B_lnb = sbuf("lnb8", [128, ND], BF16)
        sw1T, B_sw1 = sbuf("sw1T", [128, ND, FH], BF16)
        sw2T, B_sw2 = sbuf("sw2T", [128, ND, FH], BF16)
        sw3gT, B_sw3 = sbuf("sw3gT", [128, 4, D], BF16)
        rw_sb, B_rw = sbuf("rw_sb", [128, ND, 9], BF16)
        rows_a, _Ba = sbuf("rows_a", [65, D], BF16)
        rows_b, _Bb = sbuf("rows_b", [65, D], BF16)
        rows_c, _Bc = sbuf("rows_c", [65, FH], BF16)
        qbrow = rows_a[0:1, :]; B_qb = Buf("qbrow")
        kbrow = rows_a[32:33, :]; B_kb = Buf("kbrow")
        vbrow = rows_a[64:65, :]; B_vb = Buf("vbrow")
        cbrow = rows_b[0:1, :]; B_cb = Buf("cbrow")
        sb3row = rows_b[32:33, :]; B_sb3r = Buf("sb3row")
        eb3row = rows_b[64:65, :]; B_eb3 = Buf("eb3row")
        eb1row = rows_c[0:1, :]; B_eb1 = Buf("eb1row")
        eb2row = rows_c[32:33, :]; B_eb2 = Buf("eb2row")
        rbrow = rows_c[64:65, 0:9]; B_rb = Buf("rbrow")
        sb1bc, B_sb1 = sbuf("sb1bc", [128, FH], BF16)
        sb2bc, B_sb2 = sbuf("sb2bc", [128, FH], BF16)
        selgbc, B_selg = sbuf("selgbc", [128, FH], BF16)
        selbbc, B_selb = sbuf("selbbc", [128, FH], BF16)
        expbbc, B_expb = sbuf("expbbc", [128, NE], F32)
        slg4, B_slg = sbuf("slg4", [128, 4], F32)
        slb4, B_slb = sbuf("slb4", [128, 4], BF16)
        pconst, B_pc = sbuf("pconst", [128, 4], F32)
        stats, B_stats = sbuf("stats", [128, NT, 4], F32)
        warena, B_wa = sbuf("warena", [128, ND, D], BF16)
        junk_v, B_jv = sbuf("junk_v", [128, 2048], BF16)
        junk_a, B_ja = junk_v, B_jv
        tstat, B_ts = sbuf("tstat", [128, 8], F32)

        ps_a = es.enter_context(nc.psum_tensor("ps_a", [128, 2048], F32))
        ps_b = es.enter_context(nc.psum_tensor("ps_b", [128, 2048], F32))
        ps_a_bf = ps_a.bitcast(BF16)
        ps_b_bf = ps_b.bitcast(BF16)
        PA = [Buf(f"pa{i}") for i in range(4)]
        PB = [Buf(f"pb{i}") for i in range(4)]
        PA_ALL = PA
        PB_ALL = PB

        xhT, B_xhT = sbuf("xhT", [128, ND, S], BF16)
        ktl, B_kt = sbuf("ktl", [128, ND, S], BF16)
        aoT, B_ao = sbuf("aoT", [128, ND, S], BF16)
        x_t, B_x = zip(*[sbuf(f"x_t{i}", [128, D], F32) for i in range(2)])
        xc_t, B_xc = zip(*[sbuf(f"xc_t{i}", [128, D], BF16) for i in range(2)])
        v_t, B_v = zip(*[sbuf(f"v_t{i}", [128, D], BF16) for i in range(2)])
        ss_sb, B_ss = junk_v[0:2, 0:2048], B_jv
        rr_sb, B_rr = junk_v[0:2, 0:2048], B_jv
        rrbc, B_rrbc = sbuf("rrbc", [128, 2048], BF16)
        qt_sb, B_qt = sbuf("qt_sb", [128, 2048], BF16)
        p_A, B_pA = sbuf("p_A", [128, 2048], BF16)
        p_B, B_pB = sbuf("p_B", [128, 2048], BF16)
        ptA, B_ptA = sbuf("ptA", [128, 16, PVW], BF16)
        ptB, B_ptB = sbuf("ptB", [128, 16, PVW], BF16)
        negmA, B_nA = sbuf("negmA", [128, 1], F32)
        negmB, B_nB = sbuf("negmB", [128, 1], F32)
        zZ, B_z = sbuf("zZ", [128, 32], F32)
        zZh, B_zh = sbuf("zZh", [128, 4], F32)
        ztl, B_ztl = sbuf("ztl", [128, 8], F32)
        zinv2, B_zi = sbuf("zinv2", [128, 8], F32)
        zt_sb, B_zt = sbuf("zt_sb", [2, 256], BF16)
        zbc, B_zbc = sbuf("zbc", [128, PVW], BF16)
        vsl, B_vsl = zip(*[sbuf(f"vsl{i}", [128, 16, 128], BF16) for i in range(1)])
        stile, B_st = v_t, B_v
        x1f, B_x1 = qt_sb[:, 0:512], B_qt
        x2f, B_x2 = qt_sb[:, 512:1024], B_qt
        x1s, B_x1s = qt_sb[:, 1024:1536], B_qt
        g0_sb, B_g0 = qt_sb[:, 1536:2048], B_qt
        gh_sb, B_gh = p_B[:, 0:512], B_pB
        shTt, B_sh = p_B[:, 512:1024], B_pB
        ffn_f, B_ffn = p_A[:, 0:1024], B_pA
        g_t, B_g = p_A[:, 1024:2048], B_pA
        junk_f32 = junk_v.bitcast(F32)
        o_t, B_o = junk_f32[:, 0:1024], B_jv
        junk_f32_holder = (junk_f32, B_jv)
        fix_t, B_fx = zip(*[sbuf(f"fix_t{i}", [128, D], BF16) for i in range(1)])
        alpha_sb, B_al = sbuf("alpha_sb", [128, NT], F32)
        wexpall, B_we = sbuf("wexpall", [128, 16, NE], F32)
        valid_sb, B_valid = sbuf("valid_sb", [128, 16], F32)

        B_vscr = Buf("v_scr"); B_sscr = Buf("s_scr"); B_fixs = Buf("fix_scr")
        B_als = Buf("alpha_scr")

        def full(t, n):
            return t[tuple(slice(None) for _ in range(n))]

        def dump(name, src_ap, shape, dt, reads):
            if name not in debug:
                return
            t = nc.dram_tensor(name, shape, dt, kind="ExternalOutput")
            dbg[name] = t
            out_toks.append(b.op("sync", lambda e: e.dma_start(full(t, len(shape)), src_ap),
                                 reads=reads, dma=True))

        def dump_ps(name, ps_ap, width, reads):
            if name not in debug:
                return
            t = nc.dram_tensor(name, [128, width], F32, kind="ExternalOutput")
            dbg[name] = t
            tmp = junk_f32_holder[0][:, 0:width]
            Bt = junk_f32_holder[1]
            b.op("vector", lambda e: e.tensor_copy(tmp, ps_ap), reads=reads, writes=[Bt])
            out_toks.append(b.op("sync", lambda e: e.dma_start(t[:, :], tmp),
                                 reads=[Bt], dma=True))

        def bounce_bcast(src_row_ap, dst_ap, width, reads, writes):
            slot = bctr[0]; bctr[0] += 1
            Bs = Buf(f"b{slot}")
            b.op("sync", lambda e: e.dma_start(bounce[slot, 0:width], src_row_ap),
                 reads=reads, writes=[Bs], dma=True)
            np_ = dst_ap.shape[0]
            b.op("sync", lambda e: e.dma_start(dst_ap, AP(bounce, slot * S, [[0, np_], [1, width]])),
                 reads=[Bs], writes=writes, dma=True)

        def ld(dst_ap, src_ap, writes, reads=(), slow=False):
            return b.op("gpsimd", lambda e: e.dma_start(dst_ap, src_ap,
                        allow_slow_non_contiguous=slow), reads=reads,
                        writes=writes, dma=True)

        ld(ident[:, :], dr["ident_bf"][:, :], [B_id])
        ld(identf[:, :], dr["ident_f32"][:, :], [B_idf])
        ld(bdiag[:, :], dr["bdiag"][:, :], [B_bd])
        ld(gq8[:, :], AP(dr["q_gamma"], 0, [[1, 128], [128, ND]]), [B_gq], slow=True)
        ld(gk8[:, :], AP(dr["k_gamma"], 0, [[1, 128], [128, ND]]), [B_gk], slow=True)
        ld(lnw8[:, :], AP(dr["ln_w"], 0, [[1, 128], [128, ND]]), [B_lnw], slow=True)
        ld(lnb8[:, :], AP(dr["ln_b"], 0, [[1, 128], [128, ND]]), [B_lnb], slow=True)
        ld(sb1bc[:, :], AP(dr["sb1"], 0, [[0, 128], [1, FH]]), [B_sb1])
        ld(sb2bc[:, :], AP(dr["sb2"], 0, [[0, 128], [1, FH]]), [B_sb2])
        ld(selgbc[:, :], AP(dr["sel_ln_g"], 0, [[0, 128], [1, FH]]), [B_selg])
        ld(selbbc[:, :], AP(dr["sel_ln_b"], 0, [[0, 128], [1, FH]]), [B_selb])
        ld(expbbc[:, :], AP(dr["exp_bias"], 0, [[0, 128], [1, NE]]), [B_expb])
        ld(slg4[:, :], AP(dr["s_ln_g"], 0, [[1, 128], [128, 4]]), [B_slg], slow=True)
        ld(slb4[:, :], AP(dr["s_ln_b"], 0, [[1, 128], [128, 4]]), [B_slb], slow=True)
        ld(sb3row, AP(dr["sb3"], 0, [[D, 1], [1, D]]), [B_sb3r])
        ld(rbrow, AP(dr["rb"], 0, [[9, 1], [1, 9]]), [B_rb])
        ld(valid_sb[:, 0:NM], AP(dr["mvalid"], 0, [[1, 128], [128, NM]]), [B_valid], slow=True)
        for dc in range(ND):
            ld(sw1T[:, dc, :], dr["sw1T"][dc * 128:(dc + 1) * 128, :], [B_sw1])
            ld(sw2T[:, dc, :], dr["sw2T"][dc * 128:(dc + 1) * 128, :], [B_sw2])
            ld(rw_sb[:, dc, :], dr["rw"][dc * 128:(dc + 1) * 128, :], [B_rw])
        for hc in range(4):
            ld(sw3gT[:, hc, :], dr["sw3T"][hc * 128:(hc + 1) * 128, :], [B_sw3])
        b.op("gpsimd", lambda e: e.memset(ones2k[:, :], 1.0), writes=[B_ones])
        def msc(e):
            e.memset(pconst[:, 0:1], -0.5)
            e.memset(pconst[:, 1:2], -1.0)
            e.memset(pconst[:, 2:3], 0.5)
            return e.memset(pconst[:, 3:4], 1e-8)
        b.op("gpsimd", msc, writes=[B_pc])
        b.op("gpsimd", lambda e: e.tensor_scalar(gq8[:, :], gq8[:, :], 8.0, None, OP.mult),
             reads=[B_gq], writes=[B_gq])
        b.op("gpsimd", lambda e: e.tensor_scalar(gk8[:, :], gk8[:, :], 8.0, None, OP.mult),
             reads=[B_gk], writes=[B_gk])
        def recip_all(e):
            for dc in range(ND):
                ii = e.reciprocal(lnwi[:, dc:dc + 1], lnw8[:, dc:dc + 1])
            return ii
        b.op("vector", recip_all, reads=[B_lnw], writes=[B_lnwi])
        def fold12(e):
            for dc in range(ND):
                e.tensor_scalar(sw1T[:, dc, :], sw1T[:, dc, :], lnwi[:, dc:dc + 1], None, OP.mult)
                ii = e.tensor_scalar(sw2T[:, dc, :], sw2T[:, dc, :], lnwi[:, dc:dc + 1], None, OP.mult)
            return ii
        b.op("vector", fold12, reads=[B_sw1, B_sw2, B_lnwi], writes=[B_sw1, B_sw2])
        def fold3(e):
            for hc in range(4):
                ii = e.tensor_scalar(sw3gT[:, hc, :], sw3gT[:, hc, :], slg4[:, hc:hc + 1], None, OP.mult)
            return ii
        b.op("vector", fold3, reads=[B_sw3, B_slg], writes=[B_sw3])

        def stream_w(src, coloff, rowlen):
            for dc in range(ND):
                b.op("gpsimd",
                     lambda e, dc=dc: e.dma_start(warena[:, dc, :],
                         AP(src, dc * 128 * rowlen + coloff, [[rowlen, 128], [1, D]])),
                     writes=[B_wa], dma=True)

        for srcname, coloff, dstrow, dstbuf in [("Wq", 0, qbrow, B_qb),
                                                ("Wkv", 0, kbrow, B_kb),
                                                ("Wkv", D, vbrow, B_vb)]:
            rowlen = 2 * D if srcname == "Wkv" else D
            stream_w(dr[srcname], coloff, rowlen)
            def bmm(e):
                for half in range(2):
                    for dc in range(ND):
                        ii = e.matmul(ps_a[0:1, half * 512:(half + 1) * 512], lnb8[:, dc:dc + 1],
                                      warena[:, dc, half * 512:(half + 1) * 512],
                                      start=(dc == 0), stop=(dc == ND - 1))
                return ii
            b.op("tensor", bmm, reads=[B_lnb, B_wa], writes=[PA[0], PA[1]])
            b.op("vector", lambda e, dstrow=dstrow: e.tensor_copy(dstrow[0:1, :], ps_a[0:1, 0:1024]),
                 reads=[PA[0], PA[1]], writes=[dstbuf])
        def cbmm(e):
            for half in range(2):
                for hc in range(4):
                    e.matmul(ps_a[0:1, half * 512:(half + 1) * 512], slb4[:, hc:hc + 1],
                             sw3gT[:, hc, half * 512:(half + 1) * 512], start=(hc == 0), stop=False)
                ii = e.matmul(ps_a[0:1, half * 512:(half + 1) * 512], ones2k[32:33, 0:1],
                              rows_b[32:33, half * 512:(half + 1) * 512], start=False, stop=True)
            return ii
        b.op("tensor", cbmm, reads=[B_slb, B_sw3, B_ones, B_sb3r], writes=[PA[0], PA[1]])
        b.op("vector", lambda e: e.tensor_copy(cbrow, ps_a[0:1, 0:1024]),
             reads=[PA[0], PA[1]], writes=[B_cb])

        if "d_rows" in debug:
            tr = nc.dram_tensor("d_rows", [4, D], BF16, kind="ExternalOutput")
            dbg["d_rows"] = tr
            for ri, (rap, rbuf) in enumerate([(qbrow, B_qb), (kbrow, B_kb),
                                              (vbrow, B_vb), (cbrow, B_cb)]):
                out_toks.append(b.op("sync", lambda e, ri=ri, rap=rap: e.dma_start(
                     tr[ri:ri + 1, :], rap), reads=[rbuf], dma=True))

        def tile_stats(x_ap, Bx, tt, mode):
            b.op("vector", lambda e: e.tensor_scalar(junk_v[:, 0:1024], x_ap, 0.0, None,
                 OP.add, op1=OP.add, accum_out=tstat[:, 0:1]), reads=[Bx], writes=[B_jv, B_ts])
            b.op("scalar", lambda e: e.activation(junk_a[:, 0:1024], x_ap, AF.Square,
                 accum_out=tstat[:, 1:2]), reads=[Bx], writes=[B_ja, B_ts])
            def post(e):
                n = 1024.0
                e.tensor_scalar(tstat[:, 2:3], tstat[:, 0:1], 1.0 / n, None, OP.mult)
                e.tensor_scalar(tstat[:, 3:4], tstat[:, 1:2], 1.0 / n, None, OP.mult)
                e.tensor_tensor(tstat[:, 4:5], tstat[:, 2:3], tstat[:, 2:3], OP.mult)
                e.tensor_tensor(tstat[:, 4:5], tstat[:, 3:4], tstat[:, 4:5], OP.subtract)
                e.tensor_scalar(tstat[:, 5:6], tstat[:, 4:5], EPS, None, OP.add)
                e.tensor_tensor(tstat[:, 6:7], tstat[:, 5:6], pconst[:, 0:1], OP.pow)
                if mode == "moe":
                    e.tensor_copy(stats[:, tt, 0:1], tstat[:, 6:7])
                    e.tensor_tensor(tstat[:, 7:8], tstat[:, 2:3], tstat[:, 6:7], OP.mult)
                    return e.tensor_scalar(stats[:, tt, 1:2], tstat[:, 7:8], -1.0, None, OP.mult)
                e.tensor_tensor(tstat[:, 7:8], tstat[:, 5:6], pconst[:, 1:2], OP.pow)
                e.tensor_tensor(tstat[:, 7:8], tstat[:, 4:5], tstat[:, 7:8], OP.mult)
                e.tensor_scalar(tstat[:, 7:8], tstat[:, 7:8], EPS, None, OP.add)
                e.tensor_tensor(stats[:, tt, 2:3], tstat[:, 7:8], pconst[:, 2:3], OP.pow)
                e.tensor_tensor(tstat[:, 7:8], tstat[:, 7:8], pconst[:, 0:1], OP.pow)
                e.tensor_tensor(tstat[:, 6:7], tstat[:, 6:7], tstat[:, 7:8], OP.mult)
                e.tensor_copy(stats[:, tt, 0:1], tstat[:, 6:7])
                e.tensor_tensor(tstat[:, 7:8], tstat[:, 2:3], tstat[:, 6:7], OP.mult)
                return e.tensor_scalar(stats[:, tt, 1:2], tstat[:, 7:8], -1.0, None, OP.mult)
            b.op("gpsimd", post, reads=[B_ts, B_pc], writes=[B_stats, B_ts])

        def rms_apply(dst_ap, dst_buf, g_ap, Bg, dbgpre=None):
            if dbgpre:
                dump_ps(dbgpre + "_raw", ps_a[:, 0:1024], 1024, PA_ALL)
            b.op("scalar", lambda e: e.activation(junk_a[:, 0:2048], ps_a[:, :], AF.Square),
                 reads=PA_ALL, writes=[B_ja])
            def ssmm(e):
                for tsl in range(4):
                    ii = e.matmul(ps_b[0:2, tsl * 512:(tsl + 1) * 512], bdiag[:, :],
                                  junk_a[:, tsl * 512:(tsl + 1) * 512], start=True, stop=True)
                return ii
            b.op("tensor", ssmm, reads=[B_bd, B_ja], writes=PB_ALL)
            b.op("vector", lambda e: e.tensor_copy(ss_sb, ps_b[0:2, 0:2048]),
                 reads=PB_ALL, writes=[B_ss])
            if dbgpre:
                dump(dbgpre + "_ss", ss_sb, [2, 2048], BF16, [B_ss])
            b.op("scalar", lambda e: e.activation(rr_sb, ss_sb, AF.Ln,
                 bias=pconst[0:2, 3:4], scale=1.0 / 64), reads=[B_ss, B_pc], writes=[B_rr])
            b.op("scalar", lambda e: e.activation(rr_sb, rr_sb, AF.Exp, scale=-0.5),
                 reads=[B_rr], writes=[B_rr])
            if dbgpre:
                dump(dbgpre + "_rr", junk_v[0:2, 0:2048], [2, 2048], BF16, [B_rr])
            bounce_bcast(junk_v[0:1, 0:2048], rrbc[0:64, :], S, [B_rr], [B_rrbc])
            bounce_bcast(junk_v[1:2, 0:2048], rrbc[64:128, :], S, [B_rr], [B_rrbc])
            if dbgpre:
                dump(dbgpre + "_rrbc", rrbc[:, :], [128, 2048], BF16, [B_rrbc])
            b.op("vector", lambda e: e.scalar_tensor_tensor(dst_ap, ps_a[:, :], g_ap, rrbc[:, :],
                 OP.mult, OP.mult), reads=PA_ALL + [Bg, B_rrbc], writes=[dst_buf])

        def proj_chunk(ic, brow_ap, Bbrow, obase):
            def mm(e, ic=ic):
                for dc in range(ND):
                    for tsl in range(4):
                        e.matmul(ps_a[:, tsl * 512:(tsl + 1) * 512],
                                 warena[:, dc, ic * 128:(ic + 1) * 128],
                                 xhT[:, dc, tsl * 512:(tsl + 1) * 512],
                                 start=(dc == 0), stop=False)
                for tsl in range(4):
                    ii = e.matmul(ps_a[:, tsl * 512:(tsl + 1) * 512], brow_ap,
                                  ones2k[obase:obase + 1, 0:512], start=False, stop=True)
                return ii
            b.op("tensor", mm, reads=[B_wa, B_xhT, Bbrow, B_ones], writes=PA_ALL)

        def phase_A(seq):
            stream_w(dr["Wkv"], D, 2 * D)
            for tt in range(NT):
                pr = tt % 2
                b.op("sync", lambda e, tt=tt, pr=pr: e.dma_start(x_t[pr][:, :],
                     dr["x"][seq, tt * 128:(tt + 1) * 128, :]), writes=[B_x[pr]], dma=True)
                tile_stats(x_t[pr][:, :], B_x[pr], tt, "seq")
                b.op("scalar", lambda e, tt=tt, pr=pr: e.activation(xc_t[pr][:, :], x_t[pr][:, :],
                     AF.Identity, bias=stats[:, tt, 1:2], scale=stats[:, tt, 0:1]),
                     reads=[B_x[pr], B_stats], writes=[B_xc[pr]])
                def tr(e, pr=pr):
                    for dc in range(ND):
                        ii = e.transpose(ps_b_bf[:, 2048 + dc * 128:2048 + (dc + 1) * 128],
                                         xc_t[pr][:, dc * 128:(dc + 1) * 128], ident[:, :])
                    return ii
                b.op("tensor", tr, reads=[B_xc[pr], B_id], writes=[PB[2]])
                def evac_xhT(e, tt=tt):
                    for dc in range(ND):
                        ii = e.tensor_scalar(xhT[:, dc, tt * 128:(tt + 1) * 128],
                                             ps_b_bf[:, 2048 + dc * 128:2048 + (dc + 1) * 128],
                                             lnw8[:, dc:dc + 1], None, OP.mult)
                    return ii
                b.op("vector", evac_xhT, reads=[PB[2], B_lnw], writes=[B_xhT])
                def vmm(e, tt=tt):
                    for dc in range(ND):
                        e.matmul(ps_b[:, 0:512], xhT[:, dc, tt * 128:(tt + 1) * 128],
                                 warena[:, dc, 0:512], start=(dc == 0), stop=False)
                        e.matmul(ps_b[:, 512:1024], xhT[:, dc, tt * 128:(tt + 1) * 128],
                                 warena[:, dc, 512:1024], start=(dc == 0), stop=False)
                    e.matmul(ps_b[:, 0:512], ones2k[64:65, 0:128], rows_a[64:65, 0:512],
                             start=False, stop=True)
                    return e.matmul(ps_b[:, 512:1024], ones2k[64:65, 0:128], rows_a[64:65, 512:1024],
                                    start=False, stop=True)
                b.op("tensor", vmm, reads=[B_xhT, B_wa, B_ones, B_vb], writes=[PB[0], PB[1]])
                b.op("vector", lambda e, pr=pr: e.tensor_copy(v_t[pr][:, :], ps_b[:, 0:1024]),
                     reads=[PB[0], PB[1]], writes=[B_v[pr]])
                b.op("sync", lambda e, tt=tt, pr=pr: e.dma_start(v_scr[tt, :, :], v_t[pr][:, :]),
                     reads=[B_v[pr]], writes=[B_vscr], dma=True)
            if seq == 0:
                dump("d_stats", stats[:, :, :], [128, NT, 4], F32, [B_stats])
                dump("d_xhT", xhT[:, :, :], [128, ND, S], BF16, [B_xhT])
                dump("d_v", v_scr[0, :, :], [128, D], BF16, [B_vscr])
            stream_w(dr["Wkv"], 0, 2 * D)
            for ic in range(ND):
                proj_chunk(ic, rows_a[32:33, ic * 128:(ic + 1) * 128], B_kb, 32)
                rms_apply(ktl[:, ic, :], B_kt, gk8[:, ic:ic + 1], B_gk,
                          dbgpre=("d_k0" if (seq == 0 and ic == 0) else None))
            if seq == 0:
                dump("d_kt", ktl[:, :, :], [128, ND, S], BF16, [B_kt])

        def phase_B(seq):
            stream_w(dr["Wq"], 0, D)
            for hp in range(ND):
                pr = 0
                b.op("sync", lambda e, hp=hp, pr=pr: e.dma_start(vsl[pr][:, :, :],
                     AP(v_scr, hp * 128, [[D, 128], [128 * D, 16], [1, 128]])),
                     reads=[B_vscr], writes=[B_vsl[pr]], dma=True)
                proj_chunk(hp, rows_a[0:1, hp * 128:(hp + 1) * 128], B_qb, 0)
                rms_apply(qt_sb[:, :], B_qt, gq8[:, hp:hp + 1], B_gq)
                if seq == 0 and hp == 0:
                    dump("d_qt", qt_sb[:, :], [128, 2048], BF16, [B_qt])
                B_pAh = [Buf(f"pAh{h}") for h in range(2)]
                B_pBh = [Buf(f"pBh{h}") for h in range(2)]
                for qt in range(NT):
                    qq, g = qt % QG, qt // QG
                    for hh in range(2):
                        def scA(e, hp=hp, qt=qt, hh=hh):
                            for tsl in (2 * hh, 2 * hh + 1):
                                ii = e.matmul(ps_a[:, tsl * 512:(tsl + 1) * 512],
                                              qt_sb[0:64, qt * 128:(qt + 1) * 128],
                                              ktl[0:64, hp, tsl * 512:(tsl + 1) * 512],
                                              start=True, stop=True, tile_position=(0, 0))
                            return ii
                        b.op("tensor", scA, reads=[B_qt, B_kt],
                             writes=[PA[2 * hh], PA[2 * hh + 1]])
                        def scB(e, hp=hp, qt=qt, hh=hh):
                            for tsl in (2 * hh, 2 * hh + 1):
                                ii = e.matmul(ps_b[:, tsl * 512:(tsl + 1) * 512],
                                              qt_sb[64:128, qt * 128:(qt + 1) * 128],
                                              ktl[64:128, hp, tsl * 512:(tsl + 1) * 512],
                                              start=True, stop=True, tile_position=(64, 0))
                            return ii
                        b.op("tensor", scB, reads=[B_qt, B_kt],
                             writes=[PB[2 * hh], PB[2 * hh + 1]])
                    if seq == 0 and hp == 0 and qt == 0:
                        dump_ps("d_scB", ps_b[:, 0:1024], 1024, PB_ALL)
                    for hh in range(2):
                        seedA = None if hh == 0 else negmA[:, :]
                        seedB = None if hh == 0 else negmB[:, :]
                        rA = [PA[2 * hh], PA[2 * hh + 1]] + ([B_nA] if hh else [])
                        rB = [PB[2 * hh], PB[2 * hh + 1]] + ([B_nB] if hh else [])
                        b.op("vector", lambda e, hh=hh, seedA=seedA: e.tensor_scalar(
                             junk_v[:, 0:1024], ps_a[:, hh * 1024:(hh + 1) * 1024], -1.0, seedA,
                             OP.mult, op1=OP.min, accum_out=negmA[:, :]),
                             reads=rA, writes=[B_jv, B_nA])
                        b.op("vector", lambda e, hh=hh, seedB=seedB: e.tensor_scalar(
                             junk_v[:, 0:1024], ps_b[:, hh * 1024:(hh + 1) * 1024], -1.0, seedB,
                             OP.mult, op1=OP.min, accum_out=negmB[:, :]),
                             reads=rB, writes=[B_jv, B_nB])
                    for hh in range(2):
                        b.op("scalar", lambda e, hh=hh: e.activation(
                             p_A[:, hh * 1024:(hh + 1) * 1024],
                             ps_a[:, hh * 1024:(hh + 1) * 1024], AF.Exp,
                             bias=negmA[:, :], accum_out=zZh[:, hh:hh + 1]),
                             reads=[PA[2 * hh], PA[2 * hh + 1], B_nA],
                             writes=[B_pAh[hh], B_pA, B_zh])
                        b.op("scalar", lambda e, hh=hh: e.activation(
                             p_B[:, hh * 1024:(hh + 1) * 1024],
                             ps_b[:, hh * 1024:(hh + 1) * 1024], AF.Exp,
                             bias=negmB[:, :], accum_out=zZh[:, 2 + hh:3 + hh]),
                             reads=[PB[2 * hh], PB[2 * hh + 1], B_nB],
                             writes=[B_pBh[hh], B_pB, B_zh])
                    b.op("vector", lambda e, qt=qt: e.tensor_tensor(zZ[:, qt:qt + 1],
                         zZh[:, 0:1], zZh[:, 1:2], OP.add), reads=[B_zh], writes=[B_z])
                    b.op("vector", lambda e, qt=qt: e.tensor_tensor(zZ[:, 16 + qt:17 + qt],
                         zZh[:, 2:3], zZh[:, 3:4], OP.add), reads=[B_zh], writes=[B_z])
                    if seq == 0 and hp == 0 and qt == 0:
                        dump("d_p", p_A[:, :], [128, 2048], BF16, [B_pA])
                        dump("d_pB", p_B[:, :], [128, 2048], BF16, [B_pB])
                        dump("d_negm", negmA[:, :], [128, 1], F32, [B_nA])
                        dump("d_negmB", negmB[:, :], [128, 1], F32, [B_nB])
                    for hh in range(2):
                        def trA(e, hh=hh):
                            for kt in range(8 * hh, 8 * hh + 8):
                                ii = e.transpose(ps_a_bf[:, kt * 128:(kt + 1) * 128],
                                                 p_A[:, kt * 128:(kt + 1) * 128], ident[:, :])
                            return ii
                        b.op("tensor", trA, reads=[B_pAh[hh], B_id], writes=[PA[hh]])
                        b.op("vector", lambda e, qq=qq, hh=hh: e.tensor_copy(
                             ptA[:, 8 * hh:8 * hh + 8, qq * 128:(qq + 1) * 128],
                             ps_a_bf[:, hh * 1024:(hh + 1) * 1024].rearrange(
                                 "p (k q) -> p k q", q=128)),
                             reads=[PA[hh]], writes=[B_ptA])
                    for hh in range(2):
                        def trB(e, hh=hh):
                            for kt in range(8 * hh, 8 * hh + 8):
                                ii = e.transpose(ps_a_bf[:, 2048 + kt * 128:2048 + (kt + 1) * 128],
                                                 p_B[:, kt * 128:(kt + 1) * 128], ident[:, :])
                            return ii
                        b.op("tensor", trB, reads=[B_pBh[hh], B_id], writes=[PA[2 + hh]])
                        b.op("vector", lambda e, qq=qq, hh=hh: e.tensor_copy(
                             ptB[:, 8 * hh:8 * hh + 8, qq * 128:(qq + 1) * 128],
                             ps_a_bf[:, 2048 + hh * 1024:2048 + (hh + 1) * 1024].rearrange(
                                 "p (k q) -> p k q", q=128)),
                             reads=[PA[2 + hh]], writes=[B_ptB])
                    if qq == QG - 1:
                        b.op("scalar", lambda e, g=g: e.activation(ztl[:, 0:QG],
                             zZ[:, g * QG:(g + 1) * QG], AF.Ln), reads=[B_z], writes=[B_ztl])
                        b.op("scalar", lambda e, g=g: e.activation(ztl[:, 4:4 + QG],
                             zZ[:, 16 + g * QG:16 + (g + 1) * QG], AF.Ln), reads=[B_z], writes=[B_ztl])
                        b.op("scalar", lambda e: e.activation(zinv2[:, :], ztl[:, :], AF.Exp,
                             scale=-1.0), reads=[B_ztl], writes=[B_zi])
                        def ztr(e):
                            e.transpose(ps_b[0:QG, 1536:1664], zinv2[:, 0:QG], identf[:, :])
                            return e.transpose(ps_b[0:QG, 1664:1792], zinv2[:, 4:4 + QG], identf[:, :])
                        b.op("tensor", ztr, reads=[B_zi, B_idf], writes=[PB[3]])
                        def zev(e):
                            e.tensor_copy(zt_sb[0:QG, 0:128], ps_b[0:QG, 1536:1664])
                            return e.tensor_copy(zt_sb[0:QG, 128:256], ps_b[0:QG, 1664:1792])
                        b.op("vector", zev, reads=[PB[3]], writes=[B_zt])
                        slot = zctr[0]; zctr[0] += 2
                        BsA, BsB = Buf(f"z{slot}"), Buf(f"z{slot+1}")
                        b.op("sync", lambda e, slot=slot: e.dma_start(
                             AP(bouncef, slot * 512, [[128, QG], [1, 128]]), zt_sb[0:QG, 0:128]),
                             reads=[B_zt], writes=[BsA], dma=True)
                        b.op("sync", lambda e, slot=slot: e.dma_start(
                             AP(bouncef, (slot + 1) * 512, [[128, QG], [1, 128]]), zt_sb[0:QG, 128:256]),
                             reads=[B_zt], writes=[BsB], dma=True)
                        b.op("sync", lambda e, slot=slot: e.dma_start(zbc[0:64, :],
                             AP(bouncef, slot * 512, [[0, 64], [1, PVW]])), reads=[BsA],
                             writes=[B_zbc], dma=True)
                        b.op("sync", lambda e, slot=slot: e.dma_start(zbc[64:128, :],
                             AP(bouncef, (slot + 1) * 512, [[0, 64], [1, PVW]])), reads=[BsB],
                             writes=[B_zbc], dma=True)
                        def pv(e, pr=pr):
                            for kt in range(16):
                                e.matmul(ps_b[0:64, 0:PVW], vsl[pr][:, kt, 0:64],
                                         ptA[:, kt, :], start=(kt == 0), stop=(kt == 15),
                                         tile_position=(0, 0))
                                ii = e.matmul(ps_b[64:128, 0:PVW], vsl[pr][:, kt, 64:128],
                                              ptB[:, kt, :], start=(kt == 0), stop=(kt == 15),
                                              tile_position=(0, 64))
                            return ii
                        b.op("tensor", pv, reads=[B_vsl[pr], B_ptA, B_ptB], writes=[PB[0]])
                        if seq == 0 and hp == 0 and g == 0:
                            dump("d_pt", ptA[:, :, :], [128, 16, PVW], BF16, [B_ptA])
                            dump_ps("d_pv", ps_b[:, 0:PVW], PVW, [PB[0]])
                        b.op("vector", lambda e, hp=hp, g=g: e.scalar_tensor_tensor(
                             aoT[:, hp, g * PVW:(g + 1) * PVW], ps_b[:, 0:PVW], 1.0, zbc[:, :],
                             OP.mult, OP.mult), reads=[PB[0], B_zbc], writes=[B_ao])
                if seq == 0 and hp == 0:
                    dump("d_z", zZ[:, :], [128, 32], F32, [B_z])
            if seq == 0:
                dump("d_aoT", aoT[:, :, :], [128, ND, S], BF16, [B_ao])
            stream_w(dr["Wo"], 0, D)
            for tt in range(NT):
                pr = tt % 2
                def womm(e, tt=tt):
                    for ic in range(ND):
                        for half in range(2):
                            ii = e.matmul(ps_a[:, half * 512:(half + 1) * 512],
                                          aoT[:, ic, tt * 128:(tt + 1) * 128],
                                          warena[:, ic, half * 512:(half + 1) * 512],
                                          start=(ic == 0), stop=(ic == ND - 1))
                    return ii
                b.op("tensor", womm, reads=[B_ao, B_wa], writes=[PA[0], PA[1]])
                b.op("vector", lambda e, pr=pr: e.tensor_copy(stile[pr][:, :], ps_a[:, 0:1024]),
                     reads=[PA[0], PA[1]], writes=[B_st[pr]])
                b.op("sync", lambda e, tt=tt, pr=pr: e.dma_start(s_scr[tt, :, :], stile[pr][:, :]),
                     reads=[B_st[pr]], writes=[B_sscr], dma=True)
            if seq == 0:
                dump("d_s", stile[(NT - 1) % 2][:, :], [128, D], BF16, [B_st[(NT - 1) % 2]])

        def phase_C(seq):
            b.op("vector", lambda e: e.memset(alpha_sb[:, :], 1.0), writes=[B_al])
            for (p0, pstep, cnt, col, srow) in alpha_runs.get(seq, []):
                b.op("sync", lambda e, p0=p0, pstep=pstep, cnt=cnt, col=col, srow=srow:
                     e.dma_start(AP(alpha_sb, p0 * NT + col, [[pstep * NT, cnt], [1, 1]]),
                                 AP(alpha_scr, srow, [[1, cnt], [1, 1]])),
                     reads=[B_als, B_al], writes=[B_al], dma=True)
            for tt in range(NT):
                pr = tt % 2
                b.op("sync", lambda e, tt=tt, pr=pr: e.dma_start(x_t[pr][:, :],
                     dr["x"][seq, tt * 128:(tt + 1) * 128, :]), writes=[B_x[pr]], dma=True)
                b.op("sync", lambda e, tt=tt, pr=pr: e.dma_start(stile[pr][:, :], s_scr[tt, :, :]),
                     reads=[B_sscr], writes=[B_st[pr]], dma=True)
                b.op("vector", lambda e: e.memset(fix_t[0][:, :], 0.0), writes=[B_fx[0]])
                for (p0, pstep, cnt, srow) in fix_runs.get((seq, tt), []):
                    b.op("sync", lambda e, p0=p0, pstep=pstep, cnt=cnt, srow=srow:
                         e.dma_start(AP(fix_t[0], p0 * D, [[pstep * D, cnt], [1, D]]),
                                     fix_scr[srow:srow + cnt, :]),
                         reads=[B_fixs, B_fx[0]], writes=[B_fx[0]], dma=True)
                def x12(e, tt=tt):
                    for dc in range(ND):
                        e.matmul(ps_a[:, 0:512], xhT[:, dc, tt * 128:(tt + 1) * 128],
                                 sw1T[:, dc, :], start=(dc == 0), stop=(dc == ND - 1))
                        ii = e.matmul(ps_a[:, 512:1024], xhT[:, dc, tt * 128:(tt + 1) * 128],
                                      sw2T[:, dc, :], start=(dc == 0), stop=(dc == ND - 1))
                    return ii
                b.op("tensor", x12, reads=[B_xhT, B_sw1, B_sw2], writes=[PA[0], PA[1]])
                b.op("scalar", lambda e, tt=tt: e.activation(x1f, ps_a[:, 0:512],
                     AF.Identity, scale=stats[:, tt, 2:3]), reads=[PA[0], B_stats], writes=[B_x1])
                b.op("scalar", lambda e, tt=tt: e.activation(x2f, ps_a[:, 512:1024],
                     AF.Identity, scale=stats[:, tt, 2:3]), reads=[PA[1], B_stats], writes=[B_x2])
                b.op("vector", lambda e: e.tensor_tensor(x1f, x1f, sb1bc[:, :], OP.add),
                     reads=[B_x1, B_sb1], writes=[B_x1])
                b.op("vector", lambda e: e.tensor_tensor(x2f, x2f, sb2bc[:, :], OP.add),
                     reads=[B_x2, B_sb2], writes=[B_x2])
                b.op("scalar", lambda e: e.activation(x1s, x1f, AF.Silu),
                     reads=[B_x1], writes=[B_x1s])
                b.op("vector", lambda e: e.tensor_tensor(g0_sb, x1s, x2f, OP.mult),
                     reads=[B_x1s, B_x2], writes=[B_g0])
                if seq == 0 and tt == 0:
                    dump("d_x1", x1f, [128, FH], BF16, [B_x1])
                    dump("d_g0", g0_sb, [128, FH], BF16, [B_g0])
                ln512(g0_sb, B_g0, gh_sb, B_gh)
                def trg(e):
                    for hc in range(4):
                        ii = e.transpose(ps_a_bf[:, 2048 + hc * 128:2048 + (hc + 1) * 128],
                                         p_B[:, hc * 128:(hc + 1) * 128], ident[:, :])
                    return ii
                b.op("tensor", trg, reads=[B_gh, B_id], writes=[PA[2]])
                b.op("vector", lambda e: e.tensor_copy(shTt, ps_a_bf[:, 2048:2560]),
                     reads=[PA[2]], writes=[B_sh])
                def shmm(e):
                    for hc in range(4):
                        e.matmul(ps_b[:, 0:512], p_B[:, 512 + hc * 128:512 + (hc + 1) * 128],
                                 sw3gT[:, hc, 0:512], start=(hc == 0), stop=False)
                        e.matmul(ps_b[:, 512:1024], p_B[:, 512 + hc * 128:512 + (hc + 1) * 128],
                                 sw3gT[:, hc, 512:1024], start=(hc == 0), stop=False)
                    e.matmul(ps_b[:, 0:512], ones2k[0:1, 0:128], rows_b[0:1, 0:512],
                             start=False, stop=True)
                    return e.matmul(ps_b[:, 512:1024], ones2k[0:1, 0:128], rows_b[0:1, 512:1024],
                                    start=False, stop=True)
                b.op("tensor", shmm, reads=[B_sh, B_sw3, B_ones, B_cb], writes=[PB[0], PB[1]])
                if seq == 0 and tt == 0:
                    dump_ps("d_shared", ps_b[:, 0:1024], 1024, [PB[0], PB[1]])
                b.op("vector", lambda e, tt=tt: e.scalar_tensor_tensor(ffn_f,
                     ps_b[:, 0:1024], alpha_sb[:, tt:tt + 1], fix_t[0][:, :], OP.mult, OP.add),
                     reads=[PB[0], PB[1], B_al, B_fx[0]], writes=[B_ffn])
                b.op("scalar", lambda e: e.activation(g_t, ffn_f, AF.Silu),
                     reads=[B_ffn], writes=[B_g])
                b.op("gpsimd", lambda e, pr=pr: e.tensor_tensor(o_t, stile[pr][:, :],
                     g_t, OP.mult), reads=[B_st[pr], B_g], writes=[B_o])
                b.op("gpsimd", lambda e, pr=pr: e.tensor_tensor(o_t, o_t,
                     x_t[pr][:, :], OP.add), reads=[B_o, B_x[pr]], writes=[B_o])
                out_toks.append(b.op("sync", lambda e, tt=tt: e.dma_start(
                     out_t[seq, tt * 128:(tt + 1) * 128, :], o_t), reads=[B_o], dma=True))

        def ln512(src, Bsrc, dst, Bdst):
            b.op("vector", lambda e: e.tensor_scalar(junk_v[:, 0:512], src[:, :], 0.0, None,
                 OP.add, op1=OP.add, accum_out=tstat[:, 0:1]), reads=[Bsrc], writes=[B_jv, B_ts])
            b.op("scalar", lambda e: e.activation(junk_a[:, 0:512], src[:, :], AF.Square,
                 accum_out=tstat[:, 1:2]), reads=[Bsrc], writes=[B_ja, B_ts])
            def post(e):
                n = 512.0
                e.tensor_scalar(tstat[:, 2:3], tstat[:, 0:1], 1.0 / n, None, OP.mult)
                e.tensor_scalar(tstat[:, 3:4], tstat[:, 1:2], 1.0 / n, None, OP.mult)
                e.tensor_tensor(tstat[:, 4:5], tstat[:, 2:3], tstat[:, 2:3], OP.mult)
                e.tensor_tensor(tstat[:, 4:5], tstat[:, 3:4], tstat[:, 4:5], OP.subtract)
                e.tensor_scalar(tstat[:, 5:6], tstat[:, 4:5], EPS, None, OP.add)
                e.tensor_tensor(tstat[:, 6:7], tstat[:, 5:6], pconst[:, 0:1], OP.pow)
                e.tensor_tensor(tstat[:, 7:8], tstat[:, 2:3], tstat[:, 6:7], OP.mult)
                return e.tensor_scalar(tstat[:, 7:8], tstat[:, 7:8], -1.0, None, OP.mult)
            b.op("gpsimd", post, reads=[B_ts, B_pc], writes=[B_ts])
            b.op("scalar", lambda e: e.activation(dst[:, :], src[:, :], AF.Identity,
                 bias=tstat[:, 7:8], scale=tstat[:, 6:7]), reads=[Bsrc, B_ts], writes=[Bdst])

        def phase_M():
            xmT, B_xmT = xhT, B_xhT
            acc, B_acc = aoT, B_ao
            sig9, B_s9 = sbuf("sig9", [128, 9], F32)
            t8, B_t8 = sbuf("t8", [128, NE], F32)
            tmax, B_tm = sbuf("tmax", [128, 1], F32)
            oh8, B_oh = sbuf("oh8", [128, NE], F32)
            pref, B_pf = sbuf("pref", [128, 1], F32)
            nots, B_no = sbuf("nots", [128, 1], F32)
            ts1, B_t1 = sbuf("ts1", [128, 1], F32)
            swv, B_swv = sbuf("swv", [128, 1], F32)
            m1, B_m1 = p_B[:, 1024:1536], B_pB
            zz2, B_zz = p_B[:, 1536:2048], B_pB

            for mt in range(NM):
                pr = mt % 2
                b.op("sync", lambda e, mt=mt, pr=pr: e.dma_start(x_t[pr][:, :],
                     dr["xmoe"][mt * 128:(mt + 1) * 128, :]), writes=[B_x[pr]], dma=True)
                tile_stats(x_t[pr][:, :], B_x[pr], mt, "moe")
                b.op("scalar", lambda e, mt=mt, pr=pr: e.activation(xc_t[pr][:, :], x_t[pr][:, :],
                     AF.Identity, bias=stats[:, mt, 1:2], scale=stats[:, mt, 0:1]),
                     reads=[B_x[pr], B_stats], writes=[B_xc[pr]])
                def trm(e, pr=pr):
                    for dc in range(ND):
                        ii = e.transpose(ps_b_bf[:, 2048 + dc * 128:2048 + (dc + 1) * 128],
                                         xc_t[pr][:, dc * 128:(dc + 1) * 128], ident[:, :])
                    return ii
                b.op("tensor", trm, reads=[B_xc[pr], B_id], writes=[PB[2]])
                def evm(e, mt=mt):
                    for dc in range(ND):
                        ii = e.tensor_copy(xmT[:, dc, mt * 128:(mt + 1) * 128],
                                           ps_b_bf[:, 2048 + dc * 128:2048 + (dc + 1) * 128])
                    return ii
                b.op("vector", evm, reads=[PB[2]], writes=[B_xmT])
                def rmm(e, mt=mt):
                    for dc in range(ND):
                        e.matmul(ps_a[:, 0:9], xmT[:, dc, mt * 128:(mt + 1) * 128],
                                 rw_sb[:, dc, :], start=(dc == 0), stop=False)
                    return e.matmul(ps_a[:, 0:9], ones2k[64:65, 0:128], rbrow,
                                    start=False, stop=True)
                b.op("tensor", rmm, reads=[B_xmT, B_rw, B_ones, B_rb], writes=[PA[0]])
                b.op("scalar", lambda e: e.activation(sig9[:, :], ps_a[:, 0:9], AF.Sigmoid),
                     reads=[PA[0]], writes=[B_s9])
                b.op("vector", lambda e: e.tensor_tensor(t8[:, :], sig9[:, 1:9], expbbc[:, :],
                     OP.add), reads=[B_s9, B_expb], writes=[B_t8])
                if mt == 0:
                    dump("d_t8pre", t8[:, :], [128, NE], F32, [B_t8])
                    dump("d_expb", expbbc[:, :], [128, NE], F32, [B_expb])
                b.op("vector", lambda e: e.tensor_tensor(ztl[:, 0:4], t8[:, 0:4],
                     t8[:, 4:8], OP.max), reads=[B_t8], writes=[B_ztl])
                b.op("vector", lambda e: e.tensor_tensor(ztl[:, 0:2], ztl[:, 0:2],
                     ztl[:, 2:4], OP.max), reads=[B_ztl], writes=[B_ztl])
                b.op("vector", lambda e: e.tensor_tensor(tmax[:, :], ztl[:, 0:1],
                     ztl[:, 1:2], OP.max), reads=[B_ztl], writes=[B_tm])
                b.op("vector", lambda e: e.tensor_scalar(oh8[:, :], t8[:, :], tmax[:, :], None,
                     OP.is_ge), reads=[B_t8, B_tm], writes=[B_oh])
                b.op("vector", lambda e: e.tensor_copy(pref[:, :], oh8[:, 0:1]),
                     reads=[B_oh], writes=[B_pf])
                for ee in range(1, NE):
                    b.op("vector", lambda e: e.tensor_scalar(nots[:, :], pref[:, :], -1.0, 1.0,
                         OP.mult, op1=OP.add), reads=[B_pf], writes=[B_no])
                    b.op("vector", lambda e, ee=ee: e.tensor_tensor(oh8[:, ee:ee + 1],
                         oh8[:, ee:ee + 1], nots[:, :], OP.mult), reads=[B_oh, B_no], writes=[B_oh])
                    b.op("vector", lambda e, ee=ee: e.tensor_tensor(pref[:, :], pref[:, :],
                         oh8[:, ee:ee + 1], OP.max), reads=[B_pf, B_oh], writes=[B_pf])
                b.op("vector", lambda e: e.tensor_tensor(t8[:, :], oh8[:, :], sig9[:, 1:9],
                     OP.mult), reads=[B_oh, B_s9], writes=[B_t8])
                b.op("vector", lambda e: e.tensor_tensor(ztl[:, 0:4], t8[:, 0:4],
                     t8[:, 4:8], OP.add), reads=[B_t8], writes=[B_ztl])
                b.op("vector", lambda e: e.tensor_tensor(ztl[:, 0:2], ztl[:, 0:2],
                     ztl[:, 2:4], OP.add), reads=[B_ztl], writes=[B_ztl])
                b.op("vector", lambda e: e.tensor_tensor(ts1[:, :], ztl[:, 0:1],
                     ztl[:, 1:2], OP.add), reads=[B_ztl], writes=[B_t1])
                b.op("vector", lambda e: e.tensor_tensor(nots[:, :], sig9[:, 0:1], ts1[:, :],
                     OP.subtract), reads=[B_s9, B_t1], writes=[B_no])
                b.op("scalar", lambda e: e.activation(swv[:, :], nots[:, :], AF.Sigmoid),
                     reads=[B_no], writes=[B_swv])
                b.op("vector", lambda e: e.tensor_scalar(nots[:, :], nots[:, :], -1.0, None,
                     OP.mult), reads=[B_no], writes=[B_no])
                b.op("scalar", lambda e: e.activation(ts1[:, :], nots[:, :], AF.Sigmoid),
                     reads=[B_no], writes=[B_t1])
                b.op("vector", lambda e, mt=mt: e.tensor_scalar(wexpall[:, mt, :], oh8[:, :],
                     ts1[:, :], None, OP.mult), reads=[B_oh, B_t1], writes=[B_we])
                b.op("vector", lambda e, mt=mt: e.tensor_scalar(wexpall[:, mt, :],
                     wexpall[:, mt, :], valid_sb[:, mt:mt + 1], None, OP.mult),
                     reads=[B_we, B_valid], writes=[B_we])
                b.op("vector", lambda e: e.tensor_scalar(swv[:, :], swv[:, :], -1.0, None, OP.add),
                     reads=[B_swv], writes=[B_swv])
                b.op("vector", lambda e, mt=mt: e.tensor_scalar(swv[:, :], swv[:, :],
                     valid_sb[:, mt:mt + 1], 1.0, OP.mult, op1=OP.add),
                     reads=[B_swv, B_valid], writes=[B_swv])
                b.op("sync", lambda e, mt=mt: e.dma_start(alpha_scr[mt, :], swv[:, 0:1]),
                     reads=[B_swv], writes=[B_als], dma=True)
                if mt == 0:
                    dump("d_sig9", sig9[:, :], [128, 9], F32, [B_s9])
                    dump("d_swv", swv[:, :], [128, 1], F32, [B_swv])
                    dump("d_t8", t8[:, :], [128, NE], F32, [B_t8])
                    dump("d_tmax", tmax[:, :], [128, 1], F32, [B_tm])
                    dump("d_oh8", oh8[:, :], [128, NE], F32, [B_oh])
                    dump("d_ts1", ts1[:, :], [128, 1], F32, [B_t1])
            dump("d_wexp", wexpall[:, 0, :], [128, NE], F32, [B_we])
            dump("d_xmT", xmT[:, :, :], [128, ND, S], BF16, [B_xmT])

            for e_i in range(NE):
                for dc in range(ND):
                    b.op("gpsimd", lambda e, e_i=e_i, dc=dc: e.dma_start(warena[:, dc, 0:FH],
                         dr["ew1T"][e_i, dc * 128:(dc + 1) * 128, :]), writes=[B_wa], dma=True)
                    b.op("gpsimd", lambda e, e_i=e_i, dc=dc: e.dma_start(warena[:, dc, FH:2 * FH],
                         dr["ew2T"][e_i, dc * 128:(dc + 1) * 128, :]), writes=[B_wa], dma=True)
                b.op("gpsimd", lambda e, e_i=e_i: e.dma_start(eb1row,
                     dr["eb1"][e_i:e_i + 1, :]), writes=[B_eb1], dma=True)
                b.op("gpsimd", lambda e, e_i=e_i: e.dma_start(eb2row,
                     dr["eb2"][e_i:e_i + 1, :]), writes=[B_eb2], dma=True)
                b.op("gpsimd", lambda e, e_i=e_i: e.dma_start(eb3row,
                     dr["eb3"][e_i:e_i + 1, :]), writes=[B_eb3], dma=True)
                for hc in range(4):
                    b.op("gpsimd", lambda e, e_i=e_i, hc=hc: e.dma_start(
                         ptA[:, 2 * hc:2 * hc + 2, :].rearrange("p a q -> p (a q)"),
                         dr["ew3T"][e_i, hc * 128:(hc + 1) * 128, 0:512]), writes=[B_ptA], dma=True)
                    b.op("gpsimd", lambda e, e_i=e_i, hc=hc: e.dma_start(
                         ptA[:, 8 + 2 * hc:8 + 2 * hc + 2, :].rearrange("p a q -> p (a q)"),
                         dr["ew3T"][e_i, hc * 128:(hc + 1) * 128, 512:1024]), writes=[B_ptA], dma=True)
                for mt in range(NM):
                    def h12(e, mt=mt):
                        for dc in range(ND):
                            e.matmul(ps_a[:, 0:512], xmT[:, dc, mt * 128:(mt + 1) * 128],
                                     warena[:, dc, 0:FH], start=(dc == 0), stop=False)
                            e.matmul(ps_a[:, 512:1024], xmT[:, dc, mt * 128:(mt + 1) * 128],
                                     warena[:, dc, FH:2 * FH], start=(dc == 0), stop=False)
                        e.matmul(ps_a[:, 0:512], ones2k[0:1, 0:128], eb1row,
                                 start=False, stop=True)
                        return e.matmul(ps_a[:, 512:1024], ones2k[32:33, 0:128], eb2row,
                                        start=False, stop=True)
                    b.op("tensor", h12, reads=[B_xmT, B_wa, B_ones, B_eb1, B_eb2],
                         writes=[PA[0], PA[1]])
                    b.op("scalar", lambda e: e.activation(m1, ps_a[:, 0:512], AF.Silu),
                         reads=[PA[0]], writes=[B_m1])
                    b.op("vector", lambda e: e.tensor_tensor(g0_sb, m1,
                         ps_a[:, 512:1024], OP.mult), reads=[B_m1, PA[1]], writes=[B_g0])
                    ln512(g0_sb, B_g0, zz2, B_zz)
                    b.op("vector", lambda e: e.scalar_tensor_tensor(zz2, zz2, 1.0,
                         selgbc[:, :], OP.mult, OP.mult), reads=[B_zz, B_selg], writes=[B_zz])
                    b.op("vector", lambda e: e.tensor_tensor(zz2, zz2, selbbc[:, :],
                         OP.add), reads=[B_zz, B_selb], writes=[B_zz])
                    def trz(e):
                        for hc in range(4):
                            ii = e.transpose(ps_a_bf[:, 2048 + hc * 128:2048 + (hc + 1) * 128],
                                             p_B[:, 1536 + hc * 128:1536 + (hc + 1) * 128], ident[:, :])
                        return ii
                    b.op("tensor", trz, reads=[B_zz, B_id], writes=[PA[2]])
                    b.op("vector", lambda e: e.tensor_copy(shTt, ps_a_bf[:, 2048:2560]),
                         reads=[PA[2]], writes=[B_sh])
                    def omm(e):
                        for hc in range(4):
                            e.matmul(ps_b[:, 0:512], p_B[:, 512 + hc * 128:512 + (hc + 1) * 128],
                                     ptA[:, 2 * hc:2 * hc + 2, :].rearrange("p a q -> p (a q)"),
                                     start=(hc == 0), stop=False)
                            e.matmul(ps_b[:, 512:1024], p_B[:, 512 + hc * 128:512 + (hc + 1) * 128],
                                     ptA[:, 8 + 2 * hc:8 + 2 * hc + 2, :].rearrange("p a q -> p (a q)"),
                                     start=(hc == 0), stop=False)
                        e.matmul(ps_b[:, 0:512], ones2k[64:65, 0:128], rows_b[64:65, 0:512],
                                 start=False, stop=True)
                        return e.matmul(ps_b[:, 512:1024], ones2k[64:65, 0:128], rows_b[64:65, 512:1024],
                                        start=False, stop=True)
                    b.op("tensor", omm, reads=[B_sh, B_ptA, B_ones, B_eb3], writes=[PB[0], PB[1]])
                    a_ap = acc[:, mt // 2, (mt % 2) * 1024:(mt % 2 + 1) * 1024]
                    if e_i == 0:
                        b.op("vector", lambda e, a_ap=a_ap, mt=mt: e.tensor_scalar(a_ap,
                             ps_b[:, 0:1024], wexpall[:, mt, 0:1], None, OP.mult),
                             reads=[PB[0], PB[1], B_we], writes=[B_acc])
                    else:
                        b.op("vector", lambda e, a_ap=a_ap, mt=mt, e_i=e_i:
                             e.scalar_tensor_tensor(a_ap, ps_b[:, 0:1024],
                             wexpall[:, mt, e_i:e_i + 1], a_ap, OP.mult, OP.add),
                             reads=[PB[0], PB[1], B_we, B_acc], writes=[B_acc])
                    if e_i == NE - 1:
                        b.op("sync", lambda e, mt=mt, a_ap=a_ap: e.dma_start(
                             fix_scr[mt * 128:(mt + 1) * 128, :], a_ap),
                             reads=[B_acc], writes=[B_fixs], dma=True)
                        if mt == 0:
                            dump("d_acc", fix_scr[0:128, :], [128, D], BF16, [B_fixs])

        if upto in ("M", "ALL"):
            phase_M()
        if upto != "M":
            for seq in range(NSEQ):
                phase_A(seq)
                if upto in ("AB", "ABC", "ALL"):
                    phase_B(seq)
                if upto in ("ABC", "ALL"):
                    phase_C(seq)

        b.emit(out_toks)

    return nc, dbg


def prepare_inputs(inp, n_seq_batches, union, valid, n_moe):
    x = np.ascontiguousarray(inp["L0_attn_x"][n_seq_batches])
    xm = np.zeros((n_moe, D), np.float32)
    for u, (sl, t) in enumerate(union):
        xm[u] = x[sl, t]
    m = {
        "x": x, "xmoe": xm,
        "ident_bf": np.eye(128, dtype=bf16),
        "ident_f32": np.eye(128, dtype=np.float32),
        "bdiag": np.concatenate([np.repeat([[1, 0]], 64, 0), np.repeat([[0, 1]], 64, 0)]).astype(bf16),
        "ln_w": inp["L0_attn_ln_w"], "ln_b": inp["L0_attn_ln_b"],
        "Wq": inp["L0_attn_Wq"], "Wkv": inp["L0_attn_Wkv"], "Wo": inp["L0_attn_Wo"],
        "q_gamma": np.ascontiguousarray(inp["L0_attn_q_gamma"]).reshape(-1),
        "k_gamma": np.ascontiguousarray(inp["L0_attn_k_gamma"]).reshape(-1),
        "sw1T": np.ascontiguousarray(inp["L0_ffn_sw1"].T),
        "sw2T": np.ascontiguousarray(inp["L0_ffn_sw2"].T),
        "sw3T": np.ascontiguousarray(inp["L0_ffn_sw3"].T),
        "sb1": inp["L0_ffn_sb1"], "sb2": inp["L0_ffn_sb2"], "sb3": inp["L0_ffn_sb3"],
        "s_ln_g": inp["L0_ffn_s_ln_g"], "s_ln_b": inp["L0_ffn_s_ln_b"],
        "rw": np.ascontiguousarray(
            np.concatenate([inp["L0_ffn_sg_w"], inp["L0_ffn_eg_w"]], 0).T),
        "rb": np.concatenate([inp["L0_ffn_sg_b"], inp["L0_ffn_eg_b"]]),
        "exp_bias": inp["L0_ffn_exp_bias"],
        "ew1T": np.ascontiguousarray(inp["L0_ffn_ew1"].transpose(0, 2, 1)),
        "ew2T": np.ascontiguousarray(inp["L0_ffn_ew2"].transpose(0, 2, 1)),
        "ew3T": np.ascontiguousarray(inp["L0_ffn_ew3"].transpose(0, 2, 1)),
        "eb1": inp["L0_ffn_eb1"], "eb2": inp["L0_ffn_eb2"], "eb3": inp["L0_ffn_eb3"],
        "sel_ln_g": inp["L0_ffn_sel_ln_g"], "sel_ln_b": inp["L0_ffn_sel_ln_b"],
        "mvalid": valid,
    }
    return {k: np.ascontiguousarray(v) for k, v in m.items()}


def plan_cores(inp):
    ib = np.asarray(inp["L0_ffn_index_b"])
    it = np.asarray(inp["L0_ffn_index_t"])
    M = len(ib)
    key = ib.astype(np.int64) * S + it
    last = {}
    for i in range(M):
        last[int(key[i])] = i
    keep = sorted(last.values())
    ib2, it2 = ib[keep], it[keep]
    counts = np.bincount(ib2, minlength=32)
    order = np.argsort(-counts, kind="stable")
    cores = [[] for _ in range(8)]
    load = [0] * 8
    for b_ in order:
        cands = [c for c in range(8) if len(cores[c]) < 4]
        ci = min(cands, key=lambda c: load[c])
        cores[ci].append(int(b_))
        load[ci] += int(counts[b_])
    targets = [set() for _ in range(8)]
    owner = {}
    for ci in range(8):
        bmap = {b_: i for i, b_ in enumerate(cores[ci])}
        for j in range(len(ib2)):
            bb = int(ib2[j])
            if bb in bmap:
                targets[ci].add((bmap[bb], int(it2[j])))
    union = sorted(set().union(*targets)) if any(targets) else []
    n_moe = max(128, ((len(union) + 127) // 128) * 128)
    fix_runs = {}
    alpha_runs = {}
    i = 0
    while i < len(union):
        sl, t = union[i]
        tt = t // 128
        p = t % 128
        j = i + 1
        step = None
        while j < len(union):
            sl2, t2 = union[j]
            if sl2 != sl or t2 // 128 != tt:
                break
            st = (t2 % 128) - (union[j - 1][1] % 128)
            if step is None:
                if st <= 0:
                    break
                step = st
            elif st != step:
                break
            j += 1
        cnt = j - i
        if cnt == 1:
            step = 1
        fix_runs.setdefault((sl, tt), []).append((p, step, cnt, i))
        alpha_runs.setdefault(sl, []).append((p, step, cnt, tt, i))
        i = j
    core_plans = []
    for ci in range(8):
        batches = cores[ci]
        valid = np.zeros(n_moe, np.float32)
        rows = []
        for u, (sl, t) in enumerate(union):
            if (sl, t) in targets[ci]:
                valid[u] = 1.0
                rows.append((u, sl, t))
        core_plans.append(dict(batches=batches, union=union, valid=valid))
    return core_plans, union, n_moe, fix_runs, alpha_runs


def kernel_host(inp, upto="ALL", debug=(), core_ids=None):
    core_plans, union, n_moe, fix_runs, alpha_runs = plan_cores(inp)
    nc, dbg = build_program(n_moe, fix_runs, alpha_runs, upto=upto, debug=debug)
    in_maps = []
    ids = core_ids if core_ids is not None else list(range(8))
    for ci in ids:
        pl = core_plans[ci]
        in_maps.append(prepare_inputs(inp, pl["batches"], union, pl["valid"], n_moe))
    res = run_bass_kernel_spmd(nc, in_maps, core_ids=ids)
    return res, core_plans, dbg


def kernel(**inputs):
    res, core_plans, _ = kernel_host(inputs, upto="ALL", debug=(), core_ids=list(range(8)))
    out = np.empty((32, S, D), np.float32)
    for ci in range(8):
        got = np.asarray(res.results[ci]["out"], np.float32)
        for sl, b_ in enumerate(core_plans[ci]["batches"]):
            out[b_] = got[sl]
    return out


# version 10
# speedup vs baseline: 1.2409x; 1.0732x over previous
import numpy as np
import ml_dtypes
from contextlib import ExitStack

import concourse.bass as bass
import concourse.mybir as mybir
from concourse.alu_op_type import AluOpType
from concourse.bass_utils import run_bass_kernel_spmd

F32 = mybir.dt.float32
BF16 = mybir.dt.bfloat16
AF = mybir.ActivationFunctionType
OP = AluOpType
bf16 = ml_dtypes.bfloat16

S, D, H, DH = 2048, 1024, 16, 64
QG = 2
PVW = 128 * QG
NG = 16 // QG
NSEQ = 4
NT = S // 128
ND = D // 128
FH = 512
NE = 8
EPS = 1e-5
ENGINES = ("tensor", "vector", "scalar", "gpsimd", "sync")


class Tok:
    __slots__ = ("eng", "op")
    def __init__(self, eng, op):
        self.eng, self.op = eng, op


class Buf:
    __slots__ = ("name", "writer", "readers")
    def __init__(self, name):
        self.name = name
        self.writer = None
        self.readers = []


class Op:
    __slots__ = ("eng", "fn", "waits", "k", "consumed", "val", "slot")
    def __init__(self, eng, fn, waits, k):
        self.eng, self.fn, self.waits, self.k = eng, fn, waits, k
        self.consumed = False
        self.val = None
        self.slot = None


class Bld:
    def __init__(self, nc):
        self.nc = nc
        self.ops = {e: [] for e in ENGINES}

    def op(self, eng, fn, reads=(), writes=(), after=(), dma=False):
        waits = []
        for b_ in reads:
            if b_.writer is not None:
                waits.append(b_.writer)
        for b_ in writes:
            if b_.writer is not None:
                waits.append(b_.writer)
            waits.extend(b_.readers)
        waits.extend(t for t in after if t is not None)
        o = Op(eng, fn, waits, 16 if dma else 1)
        self.ops[eng].append(o)
        t = Tok(eng, o)
        for b_ in reads:
            b_.readers.append(t)
        for b_ in writes:
            b_.writer = t
            b_.readers = []
        return t

    NDMA = 8

    def emit(self, final_toks):
        self.op("gpsimd", lambda e: e.nop(), after=list(final_toks))
        for e in ENGINES:
            for o in self.ops[e]:
                for t in o.waits:
                    t.op.consumed = True
        for t in final_toks:
            t.op.consumed = True
        R = self.NDMA
        for e in ENGINES:
            run = 0
            dma_i = 0
            for o in self.ops[e]:
                if o.k == 16:
                    o.consumed = True
                    o.slot = dma_i % R
                    o.val = 16 * (dma_i // R + 1)
                    dma_i += 1
                elif o.consumed:
                    run += 1
                    o.val = run
                    o.slot = None
        with ExitStack() as es:
            sems = {e: es.enter_context(self.nc.semaphore("sem_" + e)) for e in ENGINES}
            dsems = {e: [es.enter_context(self.nc.semaphore(f"dma_{e}_{i}")) for i in range(R)]
                     for e in ("sync", "gpsimd", "scalar", "vector", "tensor")}
            blk = es.enter_context(self.nc.Block())

            def mkbody(engname):
                def _(eng):
                    maxw = {}
                    for o in self.ops[engname]:
                        best = {}
                        for t in o.waits:
                            v = t.op.val
                            assert v is not None, "wait on unconsumed op"
                            key = (t.eng, t.op.slot)
                            if v > best.get(key, 0):
                                best[key] = v
                        if o.k == 16 and o.val > 16:
                            key = (engname, o.slot)
                            if best.get(key, 0) < o.val - 16:
                                best[key] = o.val - 16
                        for key in sorted(best, key=lambda k: (k[0], -1 if k[1] is None else k[1])):
                            v = best[key]
                            if v > maxw.get(key, 0):
                                semh = sems[key[0]] if key[1] is None else dsems[key[0]][key[1]]
                                eng.wait_ge(semh, v)
                                maxw[key] = v
                        instr = o.fn(eng)
                        if o.consumed:
                            if instr is None:
                                instr = eng.nop()
                            if o.k == 16:
                                instr.then_inc(dsems[engname][o.slot], 16)
                            else:
                                instr.then_inc(sems[engname], 1)
                return _

            for e in ENGINES:
                getattr(blk, e)(mkbody(e))


def build_program(n_moe, fix_runs, alpha_runs, upto="ALL", debug=()):
    nc = bass.Bass("TRN2", target_bir_lowering=False)
    NM = max(n_moe // 128, 1)
    AP = bass.AP

    dr = {}
    def din(name, shape, dt=F32):
        dr[name] = nc.dram_tensor(name, shape, dt, kind="ExternalInput")
    din("x", [NSEQ, S, D]); din("xmoe", [NM * 128, D])
    din("ident_bf", [128, 128], BF16); din("ident_f32", [128, 128], F32)
    din("bdiag", [128, 2], BF16)
    din("ln_w", [D]); din("ln_b", [D])
    din("Wq", [D, D]); din("Wkv", [D, 2 * D]); din("Wo", [D, D])
    din("q_gamma", [H * DH]); din("k_gamma", [H * DH])
    din("sw1T", [D, FH]); din("sw2T", [D, FH]); din("sw3T", [FH, D])
    din("sb1", [FH]); din("sb2", [FH]); din("sb3", [D])
    din("s_ln_g", [FH]); din("s_ln_b", [FH])
    din("rw", [D, 9]); din("rb", [9]); din("exp_bias", [NE])
    din("ew1T", [NE, D, FH]); din("ew2T", [NE, D, FH]); din("ew3T", [NE, FH, D])
    din("eb1", [NE, FH]); din("eb2", [NE, FH]); din("eb3", [NE, D])
    din("sel_ln_g", [FH]); din("sel_ln_b", [FH])
    din("mvalid", [NM * 128])

    out_t = nc.dram_tensor("out", [NSEQ, S, D], F32, kind="ExternalOutput")
    v_scr = nc.dram_tensor("v_scr", [NT, 128, D], BF16)
    s_scr = nc.dram_tensor("s_scr", [NT, 128, D], BF16)
    fix_scr = nc.dram_tensor("fix_scr", [NM * 128, D], BF16)
    alpha_scr = nc.dram_tensor("alpha_scr", [NM, 128], F32)
    bounce = nc.dram_tensor("bounce", [2600, S], BF16)
    bouncef = nc.dram_tensor("bouncef", [600, 512], BF16)
    bctr = [0]
    zctr = [0]

    dbg = {}
    out_toks = []

    es = ExitStack()
    with es:
        def sbuf(name, shape, dt):
            t = es.enter_context(nc.sbuf_tensor("sb_" + name, shape, dt))
            return t, Buf(name)

        b = Bld(nc)

        ident, B_id = sbuf("ident", [128, 128], BF16)
        identf, B_idf = sbuf("identf", [128, 128], F32)
        bdiag, B_bd = sbuf("bdiag_sb", [128, 2], BF16)
        ones2k, B_ones = sbuf("ones65", [65, 512], BF16)
        gq8, B_gq = sbuf("gq8", [128, ND], F32)
        gk8, B_gk = sbuf("gk8", [128, ND], F32)
        lnw8, B_lnw = sbuf("lnw8", [128, ND], F32)
        lnwi, B_lnwi = sbuf("lnwi", [128, ND], F32)
        lnb8, B_lnb = sbuf("lnb8", [128, ND], BF16)
        sw1T, B_sw1 = sbuf("sw1T", [128, ND, FH], BF16)
        sw2T, B_sw2 = sbuf("sw2T", [128, ND, FH], BF16)
        sw3gT, B_sw3 = sbuf("sw3gT", [128, 4, D], BF16)
        rw_sb, B_rw = sbuf("rw_sb", [128, ND, 9], BF16)
        rows_a, _Ba = sbuf("rows_a", [65, D], BF16)
        rows_b, _Bb = sbuf("rows_b", [65, D], BF16)
        rows_c, _Bc = sbuf("rows_c", [65, FH], BF16)
        qbrow = rows_a[0:1, :]; B_qb = Buf("qbrow")
        kbrow = rows_a[32:33, :]; B_kb = Buf("kbrow")
        vbrow = rows_a[64:65, :]; B_vb = Buf("vbrow")
        cbrow = rows_b[0:1, :]; B_cb = Buf("cbrow")
        sb3row = rows_b[32:33, :]; B_sb3r = Buf("sb3row")
        eb3row = rows_b[64:65, :]; B_eb3 = Buf("eb3row")
        eb1row = rows_c[0:1, :]; B_eb1 = Buf("eb1row")
        eb2row = rows_c[32:33, :]; B_eb2 = Buf("eb2row")
        rbrow = rows_c[64:65, 0:9]; B_rb = Buf("rbrow")
        sb1bc, B_sb1 = sbuf("sb1bc", [128, FH], BF16)
        sb2bc, B_sb2 = sbuf("sb2bc", [128, FH], BF16)
        selgbc, B_selg = sbuf("selgbc", [128, FH], BF16)
        selbbc, B_selb = sbuf("selbbc", [128, FH], BF16)
        expbbc, B_expb = sbuf("expbbc", [128, NE], F32)
        slg4, B_slg = sbuf("slg4", [128, 4], F32)
        slb4, B_slb = sbuf("slb4", [128, 4], BF16)
        pconst, B_pc = sbuf("pconst", [128, 4], F32)
        stats, B_stats = sbuf("stats", [128, NT, 4], F32)
        warena, B_wa = sbuf("warena", [128, ND, D], BF16)
        junk_v, B_jv = sbuf("junk_v", [128, 2048], BF16)
        junk_a, B_ja = junk_v, B_jv
        tstat, B_ts = sbuf("tstat", [128, 8], F32)

        ps_a = es.enter_context(nc.psum_tensor("ps_a", [128, 2048], F32))
        ps_b = es.enter_context(nc.psum_tensor("ps_b", [128, 2048], F32))
        ps_a_bf = ps_a.bitcast(BF16)
        ps_b_bf = ps_b.bitcast(BF16)
        PA = [Buf(f"pa{i}") for i in range(4)]
        PB = [Buf(f"pb{i}") for i in range(4)]
        PA_ALL = PA
        PB_ALL = PB

        xhT, B_xhT = sbuf("xhT", [128, ND, S], BF16)
        ktl, B_kt = sbuf("ktl", [128, ND, S], BF16)
        aoT, B_ao = sbuf("aoT", [128, ND, S], BF16)
        x_t, B_x = zip(*[sbuf(f"x_t{i}", [128, D], F32) for i in range(2)])
        xc_t, B_xc = zip(*[sbuf(f"xc_t{i}", [128, D], BF16) for i in range(2)])
        v_t, B_v = zip(*[sbuf(f"v_t{i}", [128, D], BF16) for i in range(2)])
        ss_sb, B_ss = junk_v[0:2, 0:2048], B_jv
        rr_sb, B_rr = junk_v[0:2, 0:2048], B_jv
        rrbc, B_rrbc = sbuf("rrbc", [128, 2048], BF16)
        qt_sb, B_qt = sbuf("qt_sb", [128, 2048], BF16)
        p_A, B_pA = sbuf("p_A", [128, 2048], BF16)
        p_B, B_pB = sbuf("p_B", [128, 2048], BF16)
        ptA, B_ptA = sbuf("ptA", [128, 16, PVW], BF16)
        ptB, B_ptB = sbuf("ptB", [128, 16, PVW], BF16)
        negmA, B_nA = sbuf("negmA", [128, 1], F32)
        negmB, B_nB = sbuf("negmB", [128, 1], F32)
        zZ, B_z = sbuf("zZ", [128, 32], F32)
        ztl, B_ztl = sbuf("ztl", [128, 8], F32)
        zinv2, B_zi = sbuf("zinv2", [128, 8], F32)
        zt_sb, B_zt = sbuf("zt_sb", [2, 256], BF16)
        zbc, B_zbc = sbuf("zbc", [128, PVW], BF16)
        vsl, B_vsl = zip(*[sbuf(f"vsl{i}", [128, 16, 128], BF16) for i in range(1)])
        stile, B_st = v_t, B_v
        x1f, B_x1 = qt_sb[:, 0:512], B_qt
        x2f, B_x2 = qt_sb[:, 512:1024], B_qt
        x1s, B_x1s = qt_sb[:, 1024:1536], B_qt
        g0_sb, B_g0 = qt_sb[:, 1536:2048], B_qt
        gh_sb, B_gh = p_B[:, 0:512], B_pB
        shTt, B_sh = p_B[:, 512:1024], B_pB
        ffn_f, B_ffn = p_A[:, 0:1024], B_pA
        g_t, B_g = p_A[:, 1024:2048], B_pA
        junk_f32 = junk_v.bitcast(F32)
        o_t, B_o = junk_f32[:, 0:1024], B_jv
        junk_f32_holder = (junk_f32, B_jv)
        fix_t, B_fx = zip(*[sbuf(f"fix_t{i}", [128, D], BF16) for i in range(1)])
        alpha_sb, B_al = sbuf("alpha_sb", [128, NT], F32)
        wexpall, B_we = sbuf("wexpall", [128, 16, NE], F32)
        valid_sb, B_valid = sbuf("valid_sb", [128, 16], F32)

        B_vscr = Buf("v_scr"); B_sscr = Buf("s_scr"); B_fixs = Buf("fix_scr")
        B_als = Buf("alpha_scr")

        def full(t, n):
            return t[tuple(slice(None) for _ in range(n))]

        def dump(name, src_ap, shape, dt, reads):
            if name not in debug:
                return
            t = nc.dram_tensor(name, shape, dt, kind="ExternalOutput")
            dbg[name] = t
            out_toks.append(b.op("sync", lambda e: e.dma_start(full(t, len(shape)), src_ap),
                                 reads=reads, dma=True))

        def dump_ps(name, ps_ap, width, reads):
            if name not in debug:
                return
            t = nc.dram_tensor(name, [128, width], F32, kind="ExternalOutput")
            dbg[name] = t
            tmp = junk_f32_holder[0][:, 0:width]
            Bt = junk_f32_holder[1]
            b.op("vector", lambda e: e.tensor_copy(tmp, ps_ap), reads=reads, writes=[Bt])
            out_toks.append(b.op("sync", lambda e: e.dma_start(t[:, :], tmp),
                                 reads=[Bt], dma=True))

        def bounce_bcast(src_row_ap, dst_ap, width, reads, writes):
            slot = bctr[0]; bctr[0] += 1
            Bs = Buf(f"b{slot}")
            b.op("sync", lambda e: e.dma_start(bounce[slot, 0:width], src_row_ap),
                 reads=reads, writes=[Bs], dma=True)
            np_ = dst_ap.shape[0]
            b.op("sync", lambda e: e.dma_start(dst_ap, AP(bounce, slot * S, [[0, np_], [1, width]])),
                 reads=[Bs], writes=writes, dma=True)

        def ld(dst_ap, src_ap, writes, reads=(), slow=False):
            return b.op("gpsimd", lambda e: e.dma_start(dst_ap, src_ap,
                        allow_slow_non_contiguous=slow), reads=reads,
                        writes=writes, dma=True)

        ld(ident[:, :], dr["ident_bf"][:, :], [B_id])
        ld(identf[:, :], dr["ident_f32"][:, :], [B_idf])
        ld(bdiag[:, :], dr["bdiag"][:, :], [B_bd])
        ld(gq8[:, :], AP(dr["q_gamma"], 0, [[1, 128], [128, ND]]), [B_gq], slow=True)
        ld(gk8[:, :], AP(dr["k_gamma"], 0, [[1, 128], [128, ND]]), [B_gk], slow=True)
        ld(lnw8[:, :], AP(dr["ln_w"], 0, [[1, 128], [128, ND]]), [B_lnw], slow=True)
        ld(lnb8[:, :], AP(dr["ln_b"], 0, [[1, 128], [128, ND]]), [B_lnb], slow=True)
        ld(sb1bc[:, :], AP(dr["sb1"], 0, [[0, 128], [1, FH]]), [B_sb1])
        ld(sb2bc[:, :], AP(dr["sb2"], 0, [[0, 128], [1, FH]]), [B_sb2])
        ld(selgbc[:, :], AP(dr["sel_ln_g"], 0, [[0, 128], [1, FH]]), [B_selg])
        ld(selbbc[:, :], AP(dr["sel_ln_b"], 0, [[0, 128], [1, FH]]), [B_selb])
        ld(expbbc[:, :], AP(dr["exp_bias"], 0, [[0, 128], [1, NE]]), [B_expb])
        ld(slg4[:, :], AP(dr["s_ln_g"], 0, [[1, 128], [128, 4]]), [B_slg], slow=True)
        ld(slb4[:, :], AP(dr["s_ln_b"], 0, [[1, 128], [128, 4]]), [B_slb], slow=True)
        ld(sb3row, AP(dr["sb3"], 0, [[D, 1], [1, D]]), [B_sb3r])
        ld(rbrow, AP(dr["rb"], 0, [[9, 1], [1, 9]]), [B_rb])
        ld(valid_sb[:, 0:NM], AP(dr["mvalid"], 0, [[1, 128], [128, NM]]), [B_valid], slow=True)
        for dc in range(ND):
            ld(sw1T[:, dc, :], dr["sw1T"][dc * 128:(dc + 1) * 128, :], [B_sw1])
            ld(sw2T[:, dc, :], dr["sw2T"][dc * 128:(dc + 1) * 128, :], [B_sw2])
            ld(rw_sb[:, dc, :], dr["rw"][dc * 128:(dc + 1) * 128, :], [B_rw])
        for hc in range(4):
            ld(sw3gT[:, hc, :], dr["sw3T"][hc * 128:(hc + 1) * 128, :], [B_sw3])
        b.op("gpsimd", lambda e: e.memset(ones2k[:, :], 1.0), writes=[B_ones])
        def msc(e):
            e.memset(pconst[:, 0:1], -0.5)
            e.memset(pconst[:, 1:2], -1.0)
            e.memset(pconst[:, 2:3], 0.5)
            return e.memset(pconst[:, 3:4], 1e-8)
        b.op("gpsimd", msc, writes=[B_pc])
        b.op("gpsimd", lambda e: e.tensor_scalar(gq8[:, :], gq8[:, :], 8.0, None, OP.mult),
             reads=[B_gq], writes=[B_gq])
        b.op("gpsimd", lambda e: e.tensor_scalar(gk8[:, :], gk8[:, :], 8.0, None, OP.mult),
             reads=[B_gk], writes=[B_gk])
        def recip_all(e):
            for dc in range(ND):
                ii = e.reciprocal(lnwi[:, dc:dc + 1], lnw8[:, dc:dc + 1])
            return ii
        b.op("vector", recip_all, reads=[B_lnw], writes=[B_lnwi])
        def fold12(e):
            for dc in range(ND):
                e.tensor_scalar(sw1T[:, dc, :], sw1T[:, dc, :], lnwi[:, dc:dc + 1], None, OP.mult)
                ii = e.tensor_scalar(sw2T[:, dc, :], sw2T[:, dc, :], lnwi[:, dc:dc + 1], None, OP.mult)
            return ii
        b.op("vector", fold12, reads=[B_sw1, B_sw2, B_lnwi], writes=[B_sw1, B_sw2])
        def fold3(e):
            for hc in range(4):
                ii = e.tensor_scalar(sw3gT[:, hc, :], sw3gT[:, hc, :], slg4[:, hc:hc + 1], None, OP.mult)
            return ii
        b.op("vector", fold3, reads=[B_sw3, B_slg], writes=[B_sw3])

        def stream_w(src, coloff, rowlen):
            for dc in range(ND):
                b.op("gpsimd",
                     lambda e, dc=dc: e.dma_start(warena[:, dc, :],
                         AP(src, dc * 128 * rowlen + coloff, [[rowlen, 128], [1, D]])),
                     writes=[B_wa], dma=True)

        for srcname, coloff, dstrow, dstbuf in [("Wq", 0, qbrow, B_qb),
                                                ("Wkv", 0, kbrow, B_kb),
                                                ("Wkv", D, vbrow, B_vb)]:
            rowlen = 2 * D if srcname == "Wkv" else D
            stream_w(dr[srcname], coloff, rowlen)
            def bmm(e):
                for half in range(2):
                    for dc in range(ND):
                        ii = e.matmul(ps_a[0:1, half * 512:(half + 1) * 512], lnb8[:, dc:dc + 1],
                                      warena[:, dc, half * 512:(half + 1) * 512],
                                      start=(dc == 0), stop=(dc == ND - 1))
                return ii
            b.op("tensor", bmm, reads=[B_lnb, B_wa], writes=[PA[0], PA[1]])
            b.op("vector", lambda e, dstrow=dstrow: e.tensor_copy(dstrow[0:1, :], ps_a[0:1, 0:1024]),
                 reads=[PA[0], PA[1]], writes=[dstbuf])
        def cbmm(e):
            for half in range(2):
                for hc in range(4):
                    e.matmul(ps_a[0:1, half * 512:(half + 1) * 512], slb4[:, hc:hc + 1],
                             sw3gT[:, hc, half * 512:(half + 1) * 512], start=(hc == 0), stop=False)
                ii = e.matmul(ps_a[0:1, half * 512:(half + 1) * 512], ones2k[32:33, 0:1],
                              rows_b[32:33, half * 512:(half + 1) * 512], start=False, stop=True)
            return ii
        b.op("tensor", cbmm, reads=[B_slb, B_sw3, B_ones, B_sb3r], writes=[PA[0], PA[1]])
        b.op("vector", lambda e: e.tensor_copy(cbrow, ps_a[0:1, 0:1024]),
             reads=[PA[0], PA[1]], writes=[B_cb])

        if "d_rows" in debug:
            tr = nc.dram_tensor("d_rows", [4, D], BF16, kind="ExternalOutput")
            dbg["d_rows"] = tr
            for ri, (rap, rbuf) in enumerate([(qbrow, B_qb), (kbrow, B_kb),
                                              (vbrow, B_vb), (cbrow, B_cb)]):
                out_toks.append(b.op("sync", lambda e, ri=ri, rap=rap: e.dma_start(
                     tr[ri:ri + 1, :], rap), reads=[rbuf], dma=True))

        def tile_stats(x_ap, Bx, tt, mode):
            b.op("vector", lambda e: e.tensor_scalar(junk_v[:, 0:1024], x_ap, 0.0, None,
                 OP.add, op1=OP.add, accum_out=tstat[:, 0:1]), reads=[Bx], writes=[B_jv, B_ts])
            b.op("scalar", lambda e: e.activation(junk_a[:, 0:1024], x_ap, AF.Square,
                 accum_out=tstat[:, 1:2]), reads=[Bx], writes=[B_ja, B_ts])
            def post(e):
                n = 1024.0
                e.tensor_scalar(tstat[:, 2:3], tstat[:, 0:1], 1.0 / n, None, OP.mult)
                e.tensor_scalar(tstat[:, 3:4], tstat[:, 1:2], 1.0 / n, None, OP.mult)
                e.tensor_tensor(tstat[:, 4:5], tstat[:, 2:3], tstat[:, 2:3], OP.mult)
                e.tensor_tensor(tstat[:, 4:5], tstat[:, 3:4], tstat[:, 4:5], OP.subtract)
                e.tensor_scalar(tstat[:, 5:6], tstat[:, 4:5], EPS, None, OP.add)
                e.tensor_tensor(tstat[:, 6:7], tstat[:, 5:6], pconst[:, 0:1], OP.pow)
                if mode == "moe":
                    e.tensor_copy(stats[:, tt, 0:1], tstat[:, 6:7])
                    e.tensor_tensor(tstat[:, 7:8], tstat[:, 2:3], tstat[:, 6:7], OP.mult)
                    return e.tensor_scalar(stats[:, tt, 1:2], tstat[:, 7:8], -1.0, None, OP.mult)
                e.tensor_tensor(tstat[:, 7:8], tstat[:, 5:6], pconst[:, 1:2], OP.pow)
                e.tensor_tensor(tstat[:, 7:8], tstat[:, 4:5], tstat[:, 7:8], OP.mult)
                e.tensor_scalar(tstat[:, 7:8], tstat[:, 7:8], EPS, None, OP.add)
                e.tensor_tensor(stats[:, tt, 2:3], tstat[:, 7:8], pconst[:, 2:3], OP.pow)
                e.tensor_tensor(tstat[:, 7:8], tstat[:, 7:8], pconst[:, 0:1], OP.pow)
                e.tensor_tensor(tstat[:, 6:7], tstat[:, 6:7], tstat[:, 7:8], OP.mult)
                e.tensor_copy(stats[:, tt, 0:1], tstat[:, 6:7])
                e.tensor_tensor(tstat[:, 7:8], tstat[:, 2:3], tstat[:, 6:7], OP.mult)
                return e.tensor_scalar(stats[:, tt, 1:2], tstat[:, 7:8], -1.0, None, OP.mult)
            b.op("gpsimd", post, reads=[B_ts, B_pc], writes=[B_stats, B_ts])

        def rms_apply(dst_ap, dst_buf, g_ap, Bg, dbgpre=None):
            if dbgpre:
                dump_ps(dbgpre + "_raw", ps_a[:, 0:1024], 1024, PA_ALL)
            b.op("scalar", lambda e: e.activation(junk_a[:, 0:2048], ps_a[:, :], AF.Square),
                 reads=PA_ALL, writes=[B_ja])
            def ssmm(e):
                for tsl in range(4):
                    ii = e.matmul(ps_b[0:2, tsl * 512:(tsl + 1) * 512], bdiag[:, :],
                                  junk_a[:, tsl * 512:(tsl + 1) * 512], start=True, stop=True)
                return ii
            b.op("tensor", ssmm, reads=[B_bd, B_ja], writes=PB_ALL)
            b.op("vector", lambda e: e.tensor_copy(ss_sb, ps_b[0:2, 0:2048]),
                 reads=PB_ALL, writes=[B_ss])
            if dbgpre:
                dump(dbgpre + "_ss", ss_sb, [2, 2048], BF16, [B_ss])
            b.op("scalar", lambda e: e.activation(rr_sb, ss_sb, AF.Ln,
                 bias=pconst[0:2, 3:4], scale=1.0 / 64), reads=[B_ss, B_pc], writes=[B_rr])
            b.op("scalar", lambda e: e.activation(rr_sb, rr_sb, AF.Exp, scale=-0.5),
                 reads=[B_rr], writes=[B_rr])
            if dbgpre:
                dump(dbgpre + "_rr", junk_v[0:2, 0:2048], [2, 2048], BF16, [B_rr])
            bounce_bcast(junk_v[0:1, 0:2048], rrbc[0:64, :], S, [B_rr], [B_rrbc])
            bounce_bcast(junk_v[1:2, 0:2048], rrbc[64:128, :], S, [B_rr], [B_rrbc])
            if dbgpre:
                dump(dbgpre + "_rrbc", rrbc[:, :], [128, 2048], BF16, [B_rrbc])
            b.op("vector", lambda e: e.scalar_tensor_tensor(dst_ap, ps_a[:, :], g_ap, rrbc[:, :],
                 OP.mult, OP.mult), reads=PA_ALL + [Bg, B_rrbc], writes=[dst_buf])

        def proj_chunk(ic, brow_ap, Bbrow, obase):
            def mm(e, ic=ic):
                for dc in range(ND):
                    for tsl in range(4):
                        e.matmul(ps_a[:, tsl * 512:(tsl + 1) * 512],
                                 warena[:, dc, ic * 128:(ic + 1) * 128],
                                 xhT[:, dc, tsl * 512:(tsl + 1) * 512],
                                 start=(dc == 0), stop=False)
                for tsl in range(4):
                    ii = e.matmul(ps_a[:, tsl * 512:(tsl + 1) * 512], brow_ap,
                                  ones2k[obase:obase + 1, 0:512], start=False, stop=True)
                return ii
            b.op("tensor", mm, reads=[B_wa, B_xhT, Bbrow, B_ones], writes=PA_ALL)

        def phase_A(seq):
            stream_w(dr["Wkv"], D, 2 * D)
            for tt in range(NT):
                pr = tt % 2
                b.op("sync", lambda e, tt=tt, pr=pr: e.dma_start(x_t[pr][:, :],
                     dr["x"][seq, tt * 128:(tt + 1) * 128, :]), writes=[B_x[pr]], dma=True)
                tile_stats(x_t[pr][:, :], B_x[pr], tt, "seq")
                b.op("scalar", lambda e, tt=tt, pr=pr: e.activation(xc_t[pr][:, :], x_t[pr][:, :],
                     AF.Identity, bias=stats[:, tt, 1:2], scale=stats[:, tt, 0:1]),
                     reads=[B_x[pr], B_stats], writes=[B_xc[pr]])
                def tr(e, pr=pr):
                    for dc in range(ND):
                        ii = e.transpose(ps_b_bf[:, 2048 + dc * 128:2048 + (dc + 1) * 128],
                                         xc_t[pr][:, dc * 128:(dc + 1) * 128], ident[:, :])
                    return ii
                b.op("tensor", tr, reads=[B_xc[pr], B_id], writes=[PB[2]])
                def evac_xhT(e, tt=tt):
                    for dc in range(ND):
                        ii = e.tensor_scalar(xhT[:, dc, tt * 128:(tt + 1) * 128],
                                             ps_b_bf[:, 2048 + dc * 128:2048 + (dc + 1) * 128],
                                             lnw8[:, dc:dc + 1], None, OP.mult)
                    return ii
                b.op("vector", evac_xhT, reads=[PB[2], B_lnw], writes=[B_xhT])
                def vmm(e, tt=tt):
                    for dc in range(ND):
                        e.matmul(ps_b[:, 0:512], xhT[:, dc, tt * 128:(tt + 1) * 128],
                                 warena[:, dc, 0:512], start=(dc == 0), stop=False)
                        e.matmul(ps_b[:, 512:1024], xhT[:, dc, tt * 128:(tt + 1) * 128],
                                 warena[:, dc, 512:1024], start=(dc == 0), stop=False)
                    e.matmul(ps_b[:, 0:512], ones2k[64:65, 0:128], rows_a[64:65, 0:512],
                             start=False, stop=True)
                    return e.matmul(ps_b[:, 512:1024], ones2k[64:65, 0:128], rows_a[64:65, 512:1024],
                                    start=False, stop=True)
                b.op("tensor", vmm, reads=[B_xhT, B_wa, B_ones, B_vb], writes=[PB[0], PB[1]])
                b.op("vector", lambda e, pr=pr: e.tensor_copy(v_t[pr][:, :], ps_b[:, 0:1024]),
                     reads=[PB[0], PB[1]], writes=[B_v[pr]])
                b.op("sync", lambda e, tt=tt, pr=pr: e.dma_start(v_scr[tt, :, :], v_t[pr][:, :]),
                     reads=[B_v[pr]], writes=[B_vscr], dma=True)
            if seq == 0:
                dump("d_stats", stats[:, :, :], [128, NT, 4], F32, [B_stats])
                dump("d_xhT", xhT[:, :, :], [128, ND, S], BF16, [B_xhT])
                dump("d_v", v_scr[0, :, :], [128, D], BF16, [B_vscr])
            stream_w(dr["Wkv"], 0, 2 * D)
            for ic in range(ND):
                proj_chunk(ic, rows_a[32:33, ic * 128:(ic + 1) * 128], B_kb, 32)
                rms_apply(ktl[:, ic, :], B_kt, gk8[:, ic:ic + 1], B_gk,
                          dbgpre=("d_k0" if (seq == 0 and ic == 0) else None))
            if seq == 0:
                dump("d_kt", ktl[:, :, :], [128, ND, S], BF16, [B_kt])

        def phase_B(seq):
            stream_w(dr["Wq"], 0, D)
            for hp in range(ND):
                pr = 0
                b.op("sync", lambda e, hp=hp, pr=pr: e.dma_start(vsl[pr][:, :, :],
                     AP(v_scr, hp * 128, [[D, 128], [128 * D, 16], [1, 128]])),
                     reads=[B_vscr], writes=[B_vsl[pr]], dma=True)
                proj_chunk(hp, rows_a[0:1, hp * 128:(hp + 1) * 128], B_qb, 0)
                rms_apply(qt_sb[:, :], B_qt, gq8[:, hp:hp + 1], B_gq)
                if seq == 0 and hp == 0:
                    dump("d_qt", qt_sb[:, :], [128, 2048], BF16, [B_qt])
                for qt in range(NT):
                    qq, g = qt % QG, qt // QG
                    def scA(e, hp=hp, qt=qt):
                        for tsl in range(4):
                            ii = e.matmul(ps_a[:, tsl * 512:(tsl + 1) * 512],
                                          qt_sb[0:64, qt * 128:(qt + 1) * 128],
                                          ktl[0:64, hp, tsl * 512:(tsl + 1) * 512],
                                          start=True, stop=True, tile_position=(0, 0))
                        return ii
                    b.op("tensor", scA, reads=[B_qt, B_kt], writes=PA_ALL)
                    def scB(e, hp=hp, qt=qt):
                        for tsl in range(4):
                            ii = e.matmul(ps_b[:, tsl * 512:(tsl + 1) * 512],
                                          qt_sb[64:128, qt * 128:(qt + 1) * 128],
                                          ktl[64:128, hp, tsl * 512:(tsl + 1) * 512],
                                          start=True, stop=True, tile_position=(64, 0))
                        return ii
                    b.op("tensor", scB, reads=[B_qt, B_kt], writes=PB_ALL)
                    if seq == 0 and hp == 0 and qt == 0:
                        dump_ps("d_scB", ps_b[:, 0:1024], 1024, PB_ALL)
                    b.op("vector", lambda e: e.tensor_scalar(junk_v[:, :], ps_a[:, :], -1.0, None,
                         OP.mult, op1=OP.min, accum_out=negmA[:, :]),
                         reads=PA_ALL, writes=[B_jv, B_nA])
                    b.op("vector", lambda e: e.tensor_scalar(junk_v[:, :], ps_b[:, :], -1.0, None,
                         OP.mult, op1=OP.min, accum_out=negmB[:, :]),
                         reads=PB_ALL, writes=[B_jv, B_nB])
                    b.op("scalar", lambda e, qt=qt: e.activation(p_A[:, :], ps_a[:, :], AF.Exp,
                         bias=negmA[:, :], accum_out=zZ[:, qt:qt + 1]),
                         reads=PA_ALL + [B_nA], writes=[B_pA, B_z])
                    b.op("scalar", lambda e, qt=qt: e.activation(p_B[:, :], ps_b[:, :], AF.Exp,
                         bias=negmB[:, :], accum_out=zZ[:, 16 + qt:17 + qt]),
                         reads=PB_ALL + [B_nB], writes=[B_pB, B_z])
                    if seq == 0 and hp == 0 and qt == 0:
                        dump("d_p", p_A[:, :], [128, 2048], BF16, [B_pA])
                        dump("d_pB", p_B[:, :], [128, 2048], BF16, [B_pB])
                        dump("d_negm", negmA[:, :], [128, 1], F32, [B_nA])
                        dump("d_negmB", negmB[:, :], [128, 1], F32, [B_nB])
                    def trA(e):
                        for kt in range(16):
                            ii = e.transpose(ps_a_bf[:, kt * 128:(kt + 1) * 128],
                                             p_A[:, kt * 128:(kt + 1) * 128], ident[:, :])
                        return ii
                    b.op("tensor", trA, reads=[B_pA, B_id], writes=[PA[0], PA[1]])
                    def trB(e):
                        for kt in range(16):
                            ii = e.transpose(ps_a_bf[:, 2048 + kt * 128:2048 + (kt + 1) * 128],
                                             p_B[:, kt * 128:(kt + 1) * 128], ident[:, :])
                        return ii
                    b.op("tensor", trB, reads=[B_pB, B_id], writes=[PA[2], PA[3]])
                    b.op("vector", lambda e, qq=qq: e.tensor_copy(
                         ptA[:, :, qq * 128:(qq + 1) * 128],
                         ps_a_bf[:, 0:2048].rearrange("p (k q) -> p k q", q=128)),
                         reads=[PA[0], PA[1]], writes=[B_ptA])
                    b.op("vector", lambda e, qq=qq: e.tensor_copy(
                         ptB[:, :, qq * 128:(qq + 1) * 128],
                         ps_a_bf[:, 2048:4096].rearrange("p (k q) -> p k q", q=128)),
                         reads=[PA[2], PA[3]], writes=[B_ptB])
                    if qq == QG - 1:
                        b.op("scalar", lambda e, g=g: e.activation(ztl[:, 0:QG],
                             zZ[:, g * QG:(g + 1) * QG], AF.Ln), reads=[B_z], writes=[B_ztl])
                        b.op("scalar", lambda e, g=g: e.activation(ztl[:, 4:4 + QG],
                             zZ[:, 16 + g * QG:16 + (g + 1) * QG], AF.Ln), reads=[B_z], writes=[B_ztl])
                        b.op("scalar", lambda e: e.activation(zinv2[:, :], ztl[:, :], AF.Exp,
                             scale=-1.0), reads=[B_ztl], writes=[B_zi])
                        def ztr(e):
                            e.transpose(ps_b[0:QG, 1536:1664], zinv2[:, 0:QG], identf[:, :])
                            return e.transpose(ps_b[0:QG, 1664:1792], zinv2[:, 4:4 + QG], identf[:, :])
                        b.op("tensor", ztr, reads=[B_zi, B_idf], writes=[PB[3]])
                        def zev(e):
                            e.tensor_copy(zt_sb[0:QG, 0:128], ps_b[0:QG, 1536:1664])
                            return e.tensor_copy(zt_sb[0:QG, 128:256], ps_b[0:QG, 1664:1792])
                        b.op("vector", zev, reads=[PB[3]], writes=[B_zt])
                        slot = zctr[0]; zctr[0] += 2
                        BsA, BsB = Buf(f"z{slot}"), Buf(f"z{slot+1}")
                        b.op("sync", lambda e, slot=slot: e.dma_start(
                             AP(bouncef, slot * 512, [[128, QG], [1, 128]]), zt_sb[0:QG, 0:128]),
                             reads=[B_zt], writes=[BsA], dma=True)
                        b.op("sync", lambda e, slot=slot: e.dma_start(
                             AP(bouncef, (slot + 1) * 512, [[128, QG], [1, 128]]), zt_sb[0:QG, 128:256]),
                             reads=[B_zt], writes=[BsB], dma=True)
                        b.op("sync", lambda e, slot=slot: e.dma_start(zbc[0:64, :],
                             AP(bouncef, slot * 512, [[0, 64], [1, PVW]])), reads=[BsA],
                             writes=[B_zbc], dma=True)
                        b.op("sync", lambda e, slot=slot: e.dma_start(zbc[64:128, :],
                             AP(bouncef, (slot + 1) * 512, [[0, 64], [1, PVW]])), reads=[BsB],
                             writes=[B_zbc], dma=True)
                        def pv(e, pr=pr):
                            for kt in range(16):
                                e.matmul(ps_b[0:64, 0:PVW], vsl[pr][:, kt, 0:64],
                                         ptA[:, kt, :], start=(kt == 0), stop=(kt == 15),
                                         tile_position=(0, 0))
                                ii = e.matmul(ps_b[64:128, 0:PVW], vsl[pr][:, kt, 64:128],
                                              ptB[:, kt, :], start=(kt == 0), stop=(kt == 15),
                                              tile_position=(0, 64))
                            return ii
                        b.op("tensor", pv, reads=[B_vsl[pr], B_ptA, B_ptB], writes=[PB[0]])
                        if seq == 0 and hp == 0 and g == 0:
                            dump("d_pt", ptA[:, :, :], [128, 16, PVW], BF16, [B_ptA])
                            dump_ps("d_pv", ps_b[:, 0:PVW], PVW, [PB[0]])
                        b.op("vector", lambda e, hp=hp, g=g: e.scalar_tensor_tensor(
                             aoT[:, hp, g * PVW:(g + 1) * PVW], ps_b[:, 0:PVW], 1.0, zbc[:, :],
                             OP.mult, OP.mult), reads=[PB[0], B_zbc], writes=[B_ao])
                if seq == 0 and hp == 0:
                    dump("d_z", zZ[:, :], [128, 32], F32, [B_z])
            if seq == 0:
                dump("d_aoT", aoT[:, :, :], [128, ND, S], BF16, [B_ao])
            stream_w(dr["Wo"], 0, D)
            for tt in range(NT):
                pr = tt % 2
                def womm(e, tt=tt):
                    for ic in range(ND):
                        for half in range(2):
                            ii = e.matmul(ps_a[:, half * 512:(half + 1) * 512],
                                          aoT[:, ic, tt * 128:(tt + 1) * 128],
                                          warena[:, ic, half * 512:(half + 1) * 512],
                                          start=(ic == 0), stop=(ic == ND - 1))
                    return ii
                b.op("tensor", womm, reads=[B_ao, B_wa], writes=[PA[0], PA[1]])
                b.op("vector", lambda e, pr=pr: e.tensor_copy(stile[pr][:, :], ps_a[:, 0:1024]),
                     reads=[PA[0], PA[1]], writes=[B_st[pr]])
                b.op("sync", lambda e, tt=tt, pr=pr: e.dma_start(s_scr[tt, :, :], stile[pr][:, :]),
                     reads=[B_st[pr]], writes=[B_sscr], dma=True)
            if seq == 0:
                dump("d_s", stile[(NT - 1) % 2][:, :], [128, D], BF16, [B_st[(NT - 1) % 2]])

        def phase_C(seq):
            b.op("vector", lambda e: e.memset(alpha_sb[:, :], 1.0), writes=[B_al])
            for (p0, pstep, cnt, col, srow) in alpha_runs.get(seq, []):
                b.op("sync", lambda e, p0=p0, pstep=pstep, cnt=cnt, col=col, srow=srow:
                     e.dma_start(AP(alpha_sb, p0 * NT + col, [[pstep * NT, cnt], [1, 1]]),
                                 AP(alpha_scr, srow, [[1, cnt], [1, 1]])),
                     reads=[B_als, B_al], writes=[B_al], dma=True)
            for tt in range(NT):
                pr = tt % 2
                b.op("sync", lambda e, tt=tt, pr=pr: e.dma_start(x_t[pr][:, :],
                     dr["x"][seq, tt * 128:(tt + 1) * 128, :]), writes=[B_x[pr]], dma=True)
                b.op("sync", lambda e, tt=tt, pr=pr: e.dma_start(stile[pr][:, :], s_scr[tt, :, :]),
                     reads=[B_sscr], writes=[B_st[pr]], dma=True)
                b.op("vector", lambda e: e.memset(fix_t[0][:, :], 0.0), writes=[B_fx[0]])
                for (p0, pstep, cnt, srow) in fix_runs.get((seq, tt), []):
                    b.op("sync", lambda e, p0=p0, pstep=pstep, cnt=cnt, srow=srow:
                         e.dma_start(AP(fix_t[0], p0 * D, [[pstep * D, cnt], [1, D]]),
                                     fix_scr[srow:srow + cnt, :]),
                         reads=[B_fixs, B_fx[0]], writes=[B_fx[0]], dma=True)
                def x12(e, tt=tt):
                    for dc in range(ND):
                        e.matmul(ps_a[:, 0:512], xhT[:, dc, tt * 128:(tt + 1) * 128],
                                 sw1T[:, dc, :], start=(dc == 0), stop=(dc == ND - 1))
                        ii = e.matmul(ps_a[:, 512:1024], xhT[:, dc, tt * 128:(tt + 1) * 128],
                                      sw2T[:, dc, :], start=(dc == 0), stop=(dc == ND - 1))
                    return ii
                b.op("tensor", x12, reads=[B_xhT, B_sw1, B_sw2], writes=[PA[0], PA[1]])
                b.op("scalar", lambda e, tt=tt: e.activation(x1f, ps_a[:, 0:512],
                     AF.Identity, scale=stats[:, tt, 2:3]), reads=[PA[0], B_stats], writes=[B_x1])
                b.op("scalar", lambda e, tt=tt: e.activation(x2f, ps_a[:, 512:1024],
                     AF.Identity, scale=stats[:, tt, 2:3]), reads=[PA[1], B_stats], writes=[B_x2])
                b.op("vector", lambda e: e.tensor_tensor(x1f, x1f, sb1bc[:, :], OP.add),
                     reads=[B_x1, B_sb1], writes=[B_x1])
                b.op("vector", lambda e: e.tensor_tensor(x2f, x2f, sb2bc[:, :], OP.add),
                     reads=[B_x2, B_sb2], writes=[B_x2])
                b.op("scalar", lambda e: e.activation(x1s, x1f, AF.Silu),
                     reads=[B_x1], writes=[B_x1s])
                b.op("vector", lambda e: e.tensor_tensor(g0_sb, x1s, x2f, OP.mult),
                     reads=[B_x1s, B_x2], writes=[B_g0])
                if seq == 0 and tt == 0:
                    dump("d_x1", x1f, [128, FH], BF16, [B_x1])
                    dump("d_g0", g0_sb, [128, FH], BF16, [B_g0])
                ln512(g0_sb, B_g0, gh_sb, B_gh)
                def trg(e):
                    for hc in range(4):
                        ii = e.transpose(ps_a_bf[:, 2048 + hc * 128:2048 + (hc + 1) * 128],
                                         p_B[:, hc * 128:(hc + 1) * 128], ident[:, :])
                    return ii
                b.op("tensor", trg, reads=[B_gh, B_id], writes=[PA[2]])
                b.op("vector", lambda e: e.tensor_copy(shTt, ps_a_bf[:, 2048:2560]),
                     reads=[PA[2]], writes=[B_sh])
                def shmm(e):
                    for hc in range(4):
                        e.matmul(ps_b[:, 0:512], p_B[:, 512 + hc * 128:512 + (hc + 1) * 128],
                                 sw3gT[:, hc, 0:512], start=(hc == 0), stop=False)
                        e.matmul(ps_b[:, 512:1024], p_B[:, 512 + hc * 128:512 + (hc + 1) * 128],
                                 sw3gT[:, hc, 512:1024], start=(hc == 0), stop=False)
                    e.matmul(ps_b[:, 0:512], ones2k[0:1, 0:128], rows_b[0:1, 0:512],
                             start=False, stop=True)
                    return e.matmul(ps_b[:, 512:1024], ones2k[0:1, 0:128], rows_b[0:1, 512:1024],
                                    start=False, stop=True)
                b.op("tensor", shmm, reads=[B_sh, B_sw3, B_ones, B_cb], writes=[PB[0], PB[1]])
                if seq == 0 and tt == 0:
                    dump_ps("d_shared", ps_b[:, 0:1024], 1024, [PB[0], PB[1]])
                b.op("vector", lambda e, tt=tt: e.scalar_tensor_tensor(ffn_f,
                     ps_b[:, 0:1024], alpha_sb[:, tt:tt + 1], fix_t[0][:, :], OP.mult, OP.add),
                     reads=[PB[0], PB[1], B_al, B_fx[0]], writes=[B_ffn])
                b.op("scalar", lambda e: e.activation(g_t, ffn_f, AF.Silu),
                     reads=[B_ffn], writes=[B_g])
                b.op("gpsimd", lambda e, pr=pr: e.tensor_tensor(o_t, stile[pr][:, :],
                     g_t, OP.mult), reads=[B_st[pr], B_g], writes=[B_o])
                b.op("gpsimd", lambda e, pr=pr: e.tensor_tensor(o_t, o_t,
                     x_t[pr][:, :], OP.add), reads=[B_o, B_x[pr]], writes=[B_o])
                out_toks.append(b.op("sync", lambda e, tt=tt: e.dma_start(
                     out_t[seq, tt * 128:(tt + 1) * 128, :], o_t), reads=[B_o], dma=True))

        def ln512(src, Bsrc, dst, Bdst):
            b.op("vector", lambda e: e.tensor_scalar(junk_v[:, 0:512], src[:, :], 0.0, None,
                 OP.add, op1=OP.add, accum_out=tstat[:, 0:1]), reads=[Bsrc], writes=[B_jv, B_ts])
            b.op("scalar", lambda e: e.activation(junk_a[:, 0:512], src[:, :], AF.Square,
                 accum_out=tstat[:, 1:2]), reads=[Bsrc], writes=[B_ja, B_ts])
            def post(e):
                n = 512.0
                e.tensor_scalar(tstat[:, 2:3], tstat[:, 0:1], 1.0 / n, None, OP.mult)
                e.tensor_scalar(tstat[:, 3:4], tstat[:, 1:2], 1.0 / n, None, OP.mult)
                e.tensor_tensor(tstat[:, 4:5], tstat[:, 2:3], tstat[:, 2:3], OP.mult)
                e.tensor_tensor(tstat[:, 4:5], tstat[:, 3:4], tstat[:, 4:5], OP.subtract)
                e.tensor_scalar(tstat[:, 5:6], tstat[:, 4:5], EPS, None, OP.add)
                e.tensor_tensor(tstat[:, 6:7], tstat[:, 5:6], pconst[:, 0:1], OP.pow)
                e.tensor_tensor(tstat[:, 7:8], tstat[:, 2:3], tstat[:, 6:7], OP.mult)
                return e.tensor_scalar(tstat[:, 7:8], tstat[:, 7:8], -1.0, None, OP.mult)
            b.op("gpsimd", post, reads=[B_ts, B_pc], writes=[B_ts])
            b.op("scalar", lambda e: e.activation(dst[:, :], src[:, :], AF.Identity,
                 bias=tstat[:, 7:8], scale=tstat[:, 6:7]), reads=[Bsrc, B_ts], writes=[Bdst])

        def phase_M():
            ewb1 = nc.dram_tensor("ewb1", [NE, 1024, FH], BF16)
            ewb2 = nc.dram_tensor("ewb2", [NE, 1024, FH], BF16)
            ewb3 = nc.dram_tensor("ewb3", [NE, FH, 1024], BF16)
            B_ewb = Buf("ewb")
            for e_i in range(NE):
                b.op("gpsimd", lambda e, e_i=e_i: e.dma_start(ewb1[e_i, :, :],
                     dr["ew1T"][e_i, :, :]), writes=[B_ewb], dma=True)
                b.op("gpsimd", lambda e, e_i=e_i: e.dma_start(ewb2[e_i, :, :],
                     dr["ew2T"][e_i, :, :]), writes=[B_ewb], dma=True)
                b.op("gpsimd", lambda e, e_i=e_i: e.dma_start(ewb3[e_i, :, :],
                     dr["ew3T"][e_i, :, :]), writes=[B_ewb], dma=True)
            xmT, B_xmT = xhT, B_xhT
            acc, B_acc = aoT, B_ao
            sig9, B_s9 = sbuf("sig9", [128, 9], F32)
            t8, B_t8 = sbuf("t8", [128, NE], F32)
            tmax, B_tm = sbuf("tmax", [128, 1], F32)
            oh8, B_oh = sbuf("oh8", [128, NE], F32)
            pref, B_pf = sbuf("pref", [128, 1], F32)
            nots, B_no = sbuf("nots", [128, 1], F32)
            ts1, B_t1 = sbuf("ts1", [128, 1], F32)
            swv, B_swv = sbuf("swv", [128, 1], F32)
            m1, B_m1 = p_B[:, 1024:1536], B_pB
            zz2, B_zz = p_B[:, 1536:2048], B_pB

            for mt in range(NM):
                pr = mt % 2
                b.op("sync", lambda e, mt=mt, pr=pr: e.dma_start(x_t[pr][:, :],
                     dr["xmoe"][mt * 128:(mt + 1) * 128, :]), writes=[B_x[pr]], dma=True)
                tile_stats(x_t[pr][:, :], B_x[pr], mt, "moe")
                b.op("scalar", lambda e, mt=mt, pr=pr: e.activation(xc_t[pr][:, :], x_t[pr][:, :],
                     AF.Identity, bias=stats[:, mt, 1:2], scale=stats[:, mt, 0:1]),
                     reads=[B_x[pr], B_stats], writes=[B_xc[pr]])
                def trm(e, pr=pr):
                    for dc in range(ND):
                        ii = e.transpose(ps_b_bf[:, 2048 + dc * 128:2048 + (dc + 1) * 128],
                                         xc_t[pr][:, dc * 128:(dc + 1) * 128], ident[:, :])
                    return ii
                b.op("tensor", trm, reads=[B_xc[pr], B_id], writes=[PB[2]])
                def evm(e, mt=mt):
                    for dc in range(ND):
                        ii = e.tensor_copy(xmT[:, dc, mt * 128:(mt + 1) * 128],
                                           ps_b_bf[:, 2048 + dc * 128:2048 + (dc + 1) * 128])
                    return ii
                b.op("vector", evm, reads=[PB[2]], writes=[B_xmT])
                def rmm(e, mt=mt):
                    for dc in range(ND):
                        e.matmul(ps_a[:, 0:9], xmT[:, dc, mt * 128:(mt + 1) * 128],
                                 rw_sb[:, dc, :], start=(dc == 0), stop=False)
                    return e.matmul(ps_a[:, 0:9], ones2k[64:65, 0:128], rbrow,
                                    start=False, stop=True)
                b.op("tensor", rmm, reads=[B_xmT, B_rw, B_ones, B_rb], writes=[PA[0]])
                b.op("scalar", lambda e: e.activation(sig9[:, :], ps_a[:, 0:9], AF.Sigmoid),
                     reads=[PA[0]], writes=[B_s9])
                b.op("vector", lambda e: e.tensor_tensor(t8[:, :], sig9[:, 1:9], expbbc[:, :],
                     OP.add), reads=[B_s9, B_expb], writes=[B_t8])
                if mt == 0:
                    dump("d_t8pre", t8[:, :], [128, NE], F32, [B_t8])
                    dump("d_expb", expbbc[:, :], [128, NE], F32, [B_expb])
                b.op("vector", lambda e: e.tensor_tensor(ztl[:, 0:4], t8[:, 0:4],
                     t8[:, 4:8], OP.max), reads=[B_t8], writes=[B_ztl])
                b.op("vector", lambda e: e.tensor_tensor(ztl[:, 0:2], ztl[:, 0:2],
                     ztl[:, 2:4], OP.max), reads=[B_ztl], writes=[B_ztl])
                b.op("vector", lambda e: e.tensor_tensor(tmax[:, :], ztl[:, 0:1],
                     ztl[:, 1:2], OP.max), reads=[B_ztl], writes=[B_tm])
                b.op("vector", lambda e: e.tensor_scalar(oh8[:, :], t8[:, :], tmax[:, :], None,
                     OP.is_ge), reads=[B_t8, B_tm], writes=[B_oh])
                b.op("vector", lambda e: e.tensor_copy(pref[:, :], oh8[:, 0:1]),
                     reads=[B_oh], writes=[B_pf])
                for ee in range(1, NE):
                    b.op("vector", lambda e: e.tensor_scalar(nots[:, :], pref[:, :], -1.0, 1.0,
                         OP.mult, op1=OP.add), reads=[B_pf], writes=[B_no])
                    b.op("vector", lambda e, ee=ee: e.tensor_tensor(oh8[:, ee:ee + 1],
                         oh8[:, ee:ee + 1], nots[:, :], OP.mult), reads=[B_oh, B_no], writes=[B_oh])
                    b.op("vector", lambda e, ee=ee: e.tensor_tensor(pref[:, :], pref[:, :],
                         oh8[:, ee:ee + 1], OP.max), reads=[B_pf, B_oh], writes=[B_pf])
                b.op("vector", lambda e: e.tensor_tensor(t8[:, :], oh8[:, :], sig9[:, 1:9],
                     OP.mult), reads=[B_oh, B_s9], writes=[B_t8])
                b.op("vector", lambda e: e.tensor_tensor(ztl[:, 0:4], t8[:, 0:4],
                     t8[:, 4:8], OP.add), reads=[B_t8], writes=[B_ztl])
                b.op("vector", lambda e: e.tensor_tensor(ztl[:, 0:2], ztl[:, 0:2],
                     ztl[:, 2:4], OP.add), reads=[B_ztl], writes=[B_ztl])
                b.op("vector", lambda e: e.tensor_tensor(ts1[:, :], ztl[:, 0:1],
                     ztl[:, 1:2], OP.add), reads=[B_ztl], writes=[B_t1])
                b.op("vector", lambda e: e.tensor_tensor(nots[:, :], sig9[:, 0:1], ts1[:, :],
                     OP.subtract), reads=[B_s9, B_t1], writes=[B_no])
                b.op("scalar", lambda e: e.activation(swv[:, :], nots[:, :], AF.Sigmoid),
                     reads=[B_no], writes=[B_swv])
                b.op("vector", lambda e: e.tensor_scalar(nots[:, :], nots[:, :], -1.0, None,
                     OP.mult), reads=[B_no], writes=[B_no])
                b.op("scalar", lambda e: e.activation(ts1[:, :], nots[:, :], AF.Sigmoid),
                     reads=[B_no], writes=[B_t1])
                b.op("vector", lambda e, mt=mt: e.tensor_scalar(wexpall[:, mt, :], oh8[:, :],
                     ts1[:, :], None, OP.mult), reads=[B_oh, B_t1], writes=[B_we])
                b.op("vector", lambda e, mt=mt: e.tensor_scalar(wexpall[:, mt, :],
                     wexpall[:, mt, :], valid_sb[:, mt:mt + 1], None, OP.mult),
                     reads=[B_we, B_valid], writes=[B_we])
                b.op("vector", lambda e: e.tensor_scalar(swv[:, :], swv[:, :], -1.0, None, OP.add),
                     reads=[B_swv], writes=[B_swv])
                b.op("vector", lambda e, mt=mt: e.tensor_scalar(swv[:, :], swv[:, :],
                     valid_sb[:, mt:mt + 1], 1.0, OP.mult, op1=OP.add),
                     reads=[B_swv, B_valid], writes=[B_swv])
                b.op("sync", lambda e, mt=mt: e.dma_start(alpha_scr[mt, :], swv[:, 0:1]),
                     reads=[B_swv], writes=[B_als], dma=True)
                if mt == 0:
                    dump("d_sig9", sig9[:, :], [128, 9], F32, [B_s9])
                    dump("d_swv", swv[:, :], [128, 1], F32, [B_swv])
                    dump("d_t8", t8[:, :], [128, NE], F32, [B_t8])
                    dump("d_tmax", tmax[:, :], [128, 1], F32, [B_tm])
                    dump("d_oh8", oh8[:, :], [128, NE], F32, [B_oh])
                    dump("d_ts1", ts1[:, :], [128, 1], F32, [B_t1])
            dump("d_wexp", wexpall[:, 0, :], [128, NE], F32, [B_we])
            dump("d_xmT", xmT[:, :, :], [128, ND, S], BF16, [B_xmT])

            for e_i in range(NE):
                for dc in range(ND):
                    b.op("gpsimd" if dc % 2 == 0 else "sync",
                         lambda e, e_i=e_i, dc=dc: e.dma_start(warena[:, dc, 0:FH],
                         ewb1[e_i, dc * 128:(dc + 1) * 128, :]),
                         reads=[B_ewb], writes=[B_wa], dma=True)
                    b.op("sync" if dc % 2 == 0 else "gpsimd",
                         lambda e, e_i=e_i, dc=dc: e.dma_start(warena[:, dc, FH:2 * FH],
                         ewb2[e_i, dc * 128:(dc + 1) * 128, :]),
                         reads=[B_ewb], writes=[B_wa], dma=True)
                b.op("gpsimd", lambda e, e_i=e_i: e.dma_start(eb1row,
                     dr["eb1"][e_i:e_i + 1, :]), writes=[B_eb1], dma=True)
                b.op("gpsimd", lambda e, e_i=e_i: e.dma_start(eb2row,
                     dr["eb2"][e_i:e_i + 1, :]), writes=[B_eb2], dma=True)
                b.op("gpsimd", lambda e, e_i=e_i: e.dma_start(eb3row,
                     dr["eb3"][e_i:e_i + 1, :]), writes=[B_eb3], dma=True)
                for hc in range(4):
                    b.op("gpsimd" if hc % 2 == 0 else "sync",
                         lambda e, e_i=e_i, hc=hc: e.dma_start(
                         ptA[:, 2 * hc:2 * hc + 2, :].rearrange("p a q -> p (a q)"),
                         ewb3[e_i, hc * 128:(hc + 1) * 128, 0:512]),
                         reads=[B_ewb], writes=[B_ptA], dma=True)
                    b.op("sync" if hc % 2 == 0 else "gpsimd",
                         lambda e, e_i=e_i, hc=hc: e.dma_start(
                         ptA[:, 8 + 2 * hc:8 + 2 * hc + 2, :].rearrange("p a q -> p (a q)"),
                         ewb3[e_i, hc * 128:(hc + 1) * 128, 512:1024]),
                         reads=[B_ewb], writes=[B_ptA], dma=True)
                for mt in range(NM):
                    def h12(e, mt=mt):
                        for dc in range(ND):
                            e.matmul(ps_a[:, 0:512], xmT[:, dc, mt * 128:(mt + 1) * 128],
                                     warena[:, dc, 0:FH], start=(dc == 0), stop=False)
                            e.matmul(ps_a[:, 512:1024], xmT[:, dc, mt * 128:(mt + 1) * 128],
                                     warena[:, dc, FH:2 * FH], start=(dc == 0), stop=False)
                        e.matmul(ps_a[:, 0:512], ones2k[0:1, 0:128], eb1row,
                                 start=False, stop=True)
                        return e.matmul(ps_a[:, 512:1024], ones2k[32:33, 0:128], eb2row,
                                        start=False, stop=True)
                    b.op("tensor", h12, reads=[B_xmT, B_wa, B_ones, B_eb1, B_eb2],
                         writes=[PA[0], PA[1]])
                    b.op("scalar", lambda e: e.activation(m1, ps_a[:, 0:512], AF.Silu),
                         reads=[PA[0]], writes=[B_m1])
                    b.op("vector", lambda e: e.tensor_tensor(g0_sb, m1,
                         ps_a[:, 512:1024], OP.mult), reads=[B_m1, PA[1]], writes=[B_g0])
                    ln512(g0_sb, B_g0, zz2, B_zz)
                    b.op("vector", lambda e: e.scalar_tensor_tensor(zz2, zz2, 1.0,
                         selgbc[:, :], OP.mult, OP.mult), reads=[B_zz, B_selg], writes=[B_zz])
                    b.op("vector", lambda e: e.tensor_tensor(zz2, zz2, selbbc[:, :],
                         OP.add), reads=[B_zz, B_selb], writes=[B_zz])
                    def trz(e):
                        for hc in range(4):
                            ii = e.transpose(ps_a_bf[:, 2048 + hc * 128:2048 + (hc + 1) * 128],
                                             p_B[:, 1536 + hc * 128:1536 + (hc + 1) * 128], ident[:, :])
                        return ii
                    b.op("tensor", trz, reads=[B_zz, B_id], writes=[PA[2]])
                    b.op("vector", lambda e: e.tensor_copy(shTt, ps_a_bf[:, 2048:2560]),
                         reads=[PA[2]], writes=[B_sh])
                    def omm(e):
                        for hc in range(4):
                            e.matmul(ps_b[:, 0:512], p_B[:, 512 + hc * 128:512 + (hc + 1) * 128],
                                     ptA[:, 2 * hc:2 * hc + 2, :].rearrange("p a q -> p (a q)"),
                                     start=(hc == 0), stop=False)
                            e.matmul(ps_b[:, 512:1024], p_B[:, 512 + hc * 128:512 + (hc + 1) * 128],
                                     ptA[:, 8 + 2 * hc:8 + 2 * hc + 2, :].rearrange("p a q -> p (a q)"),
                                     start=(hc == 0), stop=False)
                        e.matmul(ps_b[:, 0:512], ones2k[64:65, 0:128], rows_b[64:65, 0:512],
                                 start=False, stop=True)
                        return e.matmul(ps_b[:, 512:1024], ones2k[64:65, 0:128], rows_b[64:65, 512:1024],
                                        start=False, stop=True)
                    b.op("tensor", omm, reads=[B_sh, B_ptA, B_ones, B_eb3], writes=[PB[0], PB[1]])
                    a_ap = acc[:, mt // 2, (mt % 2) * 1024:(mt % 2 + 1) * 1024]
                    if e_i == 0:
                        b.op("vector", lambda e, a_ap=a_ap, mt=mt: e.tensor_scalar(a_ap,
                             ps_b[:, 0:1024], wexpall[:, mt, 0:1], None, OP.mult),
                             reads=[PB[0], PB[1], B_we], writes=[B_acc])
                    else:
                        b.op("vector", lambda e, a_ap=a_ap, mt=mt, e_i=e_i:
                             e.scalar_tensor_tensor(a_ap, ps_b[:, 0:1024],
                             wexpall[:, mt, e_i:e_i + 1], a_ap, OP.mult, OP.add),
                             reads=[PB[0], PB[1], B_we, B_acc], writes=[B_acc])
                    if e_i == NE - 1:
                        b.op("sync", lambda e, mt=mt, a_ap=a_ap: e.dma_start(
                             fix_scr[mt * 128:(mt + 1) * 128, :], a_ap),
                             reads=[B_acc], writes=[B_fixs], dma=True)
                        if mt == 0:
                            dump("d_acc", fix_scr[0:128, :], [128, D], BF16, [B_fixs])

        if upto in ("M", "ALL"):
            phase_M()
        if upto != "M":
            for seq in range(NSEQ):
                phase_A(seq)
                if upto in ("AB", "ABC", "ALL"):
                    phase_B(seq)
                if upto in ("ABC", "ALL"):
                    phase_C(seq)

        b.emit(out_toks)

    return nc, dbg


def prepare_inputs(inp, n_seq_batches, union, valid, n_moe):
    x = np.ascontiguousarray(inp["L0_attn_x"][n_seq_batches])
    xm = np.zeros((n_moe, D), np.float32)
    for u, (sl, t) in enumerate(union):
        xm[u] = x[sl, t]
    m = {
        "x": x, "xmoe": xm,
        "ident_bf": np.eye(128, dtype=bf16),
        "ident_f32": np.eye(128, dtype=np.float32),
        "bdiag": np.concatenate([np.repeat([[1, 0]], 64, 0), np.repeat([[0, 1]], 64, 0)]).astype(bf16),
        "ln_w": inp["L0_attn_ln_w"], "ln_b": inp["L0_attn_ln_b"],
        "Wq": inp["L0_attn_Wq"], "Wkv": inp["L0_attn_Wkv"], "Wo": inp["L0_attn_Wo"],
        "q_gamma": np.ascontiguousarray(inp["L0_attn_q_gamma"]).reshape(-1),
        "k_gamma": np.ascontiguousarray(inp["L0_attn_k_gamma"]).reshape(-1),
        "sw1T": np.ascontiguousarray(inp["L0_ffn_sw1"].T),
        "sw2T": np.ascontiguousarray(inp["L0_ffn_sw2"].T),
        "sw3T": np.ascontiguousarray(inp["L0_ffn_sw3"].T),
        "sb1": inp["L0_ffn_sb1"], "sb2": inp["L0_ffn_sb2"], "sb3": inp["L0_ffn_sb3"],
        "s_ln_g": inp["L0_ffn_s_ln_g"], "s_ln_b": inp["L0_ffn_s_ln_b"],
        "rw": np.ascontiguousarray(
            np.concatenate([inp["L0_ffn_sg_w"], inp["L0_ffn_eg_w"]], 0).T),
        "rb": np.concatenate([inp["L0_ffn_sg_b"], inp["L0_ffn_eg_b"]]),
        "exp_bias": inp["L0_ffn_exp_bias"],
        "ew1T": np.ascontiguousarray(inp["L0_ffn_ew1"].transpose(0, 2, 1)),
        "ew2T": np.ascontiguousarray(inp["L0_ffn_ew2"].transpose(0, 2, 1)),
        "ew3T": np.ascontiguousarray(inp["L0_ffn_ew3"].transpose(0, 2, 1)),
        "eb1": inp["L0_ffn_eb1"], "eb2": inp["L0_ffn_eb2"], "eb3": inp["L0_ffn_eb3"],
        "sel_ln_g": inp["L0_ffn_sel_ln_g"], "sel_ln_b": inp["L0_ffn_sel_ln_b"],
        "mvalid": valid,
    }
    return {k: np.ascontiguousarray(v) for k, v in m.items()}


def plan_cores(inp):
    ib = np.asarray(inp["L0_ffn_index_b"])
    it = np.asarray(inp["L0_ffn_index_t"])
    M = len(ib)
    key = ib.astype(np.int64) * S + it
    last = {}
    for i in range(M):
        last[int(key[i])] = i
    keep = sorted(last.values())
    ib2, it2 = ib[keep], it[keep]
    counts = np.bincount(ib2, minlength=32)
    order = np.argsort(-counts, kind="stable")
    cores = [[] for _ in range(8)]
    load = [0] * 8
    for b_ in order:
        cands = [c for c in range(8) if len(cores[c]) < 4]
        ci = min(cands, key=lambda c: load[c])
        cores[ci].append(int(b_))
        load[ci] += int(counts[b_])
    targets = [set() for _ in range(8)]
    owner = {}
    for ci in range(8):
        bmap = {b_: i for i, b_ in enumerate(cores[ci])}
        for j in range(len(ib2)):
            bb = int(ib2[j])
            if bb in bmap:
                targets[ci].add((bmap[bb], int(it2[j])))
    union = sorted(set().union(*targets)) if any(targets) else []
    n_moe = max(128, ((len(union) + 127) // 128) * 128)
    fix_runs = {}
    alpha_runs = {}
    i = 0
    while i < len(union):
        sl, t = union[i]
        tt = t // 128
        p = t % 128
        j = i + 1
        step = None
        while j < len(union):
            sl2, t2 = union[j]
            if sl2 != sl or t2 // 128 != tt:
                break
            st = (t2 % 128) - (union[j - 1][1] % 128)
            if step is None:
                if st <= 0:
                    break
                step = st
            elif st != step:
                break
            j += 1
        cnt = j - i
        if cnt == 1:
            step = 1
        fix_runs.setdefault((sl, tt), []).append((p, step, cnt, i))
        alpha_runs.setdefault(sl, []).append((p, step, cnt, tt, i))
        i = j
    core_plans = []
    for ci in range(8):
        batches = cores[ci]
        valid = np.zeros(n_moe, np.float32)
        rows = []
        for u, (sl, t) in enumerate(union):
            if (sl, t) in targets[ci]:
                valid[u] = 1.0
                rows.append((u, sl, t))
        core_plans.append(dict(batches=batches, union=union, valid=valid))
    return core_plans, union, n_moe, fix_runs, alpha_runs


def kernel_host(inp, upto="ALL", debug=(), core_ids=None):
    core_plans, union, n_moe, fix_runs, alpha_runs = plan_cores(inp)
    nc, dbg = build_program(n_moe, fix_runs, alpha_runs, upto=upto, debug=debug)
    in_maps = []
    ids = core_ids if core_ids is not None else list(range(8))
    for ci in ids:
        pl = core_plans[ci]
        in_maps.append(prepare_inputs(inp, pl["batches"], union, pl["valid"], n_moe))
    res = run_bass_kernel_spmd(nc, in_maps, core_ids=ids)
    return res, core_plans, dbg


def kernel(**inputs):
    res, core_plans, _ = kernel_host(inputs, upto="ALL", debug=(), core_ids=list(range(8)))
    out = np.empty((32, S, D), np.float32)
    for ci in range(8):
        got = np.asarray(res.results[ci]["out"], np.float32)
        for sl, b_ in enumerate(core_plans[ci]["batches"]):
            out[b_] = got[sl]
    return out
